# version 35
# baseline (speedup 1.0000x reference)
import contextlib
import numpy as np
import concourse.bass as bass
import concourse.mybir as mybir
from concourse.bass_utils import run_bass_kernel_spmd
from concourse.bass_core_rust import IndirectOffsetOnAxis

F32 = mybir.dt.float32
BF16 = mybir.dt.bfloat16
I32 = mybir.dt.int32
AF = mybir.ActivationFunctionType
ALU = mybir.AluOpType
AX = mybir.AxisListType

NCORE = 8
TOWN = 1024
NCH = 8
L = 128
D = 64
HD = 2048
NHB = HD // 128
EXP = 16
CAP = 1024
IE = 1408
NIB = IE // 128
ISH = 2816
NSB = ISH // 128
RMS_EPS = 1e-6
LN_EPS = 1e-5
SURSC = 1.0 / (1.0 + 1e-6)
BIGF = 1.0e6


def build_nc(debug=False, poison=False):
    nc = bass.Bass("TRN2", target_bir_lowering=False)
    es = contextlib.ExitStack()

    dr = lambda n, s, d=F32, k="ExternalInput": nc.dram_tensor(n, s, d, kind=k).ap()
    x_own = dr("x_own", [TOWN, HD])
    wcatT = dr("wcatT", [HD, 256])
    woT = dr("woT", [D, HD])
    gwT = dr("gwT", [HD, EXP])
    wg_my = dr("wg_my", [2, HD, IE])
    wu_my = dr("wu_my", [2, HD, IE])
    wd_my = dr("wd_my", [2, IE, HD])
    swg = dr("swg", [HD, ISH])
    swu = dr("swu", [HD, ISH])
    swd = dr("swd", [ISH, HD])
    rowp = dr("rowp", [1, 4 * D])
    ident_f = dr("ident_f", [128, 128])
    ident_b = dr("ident_b", [128, 128], BF16)
    m_triuinc = dr("m_triuinc", [128, 128])
    m_striu = dr("m_striu", [128, 128])
    m_striun = dr("m_striun", [128, 128])
    m_striln = dr("m_striln", [128, 128])
    onescol = dr("onescol", [128, 1])
    e127c = dr("e127c", [128, 1])
    iota128 = dr("iota128", [128, 1])
    iota16g = dr("iota16g", [128, 16])
    iota64c = dr("iota64c", [64, 1])
    premask = dr("premask", [8, 1])
    esel = dr("esel", [16, 2])
    cvec = dr("cvec", [1, 4])
    fconst_d = dr("fconst_d", [128, 3])
    out_own = dr("out_own", [TOWN, HD], F32, "ExternalOutput")
    dbg = {}
    if debug:
        dbg["prep"] = dr("dbg_prep", [TOWN, 256], F32, "ExternalOutput")
        dbg["errs"] = dr("dbg_errs", [TOWN, 1], F32, "ExternalOutput")
        dbg["y"] = dr("dbg_y", [TOWN, D], F32, "ExternalOutput")
        dbg["h"] = dr("dbg_h", [TOWN, HD], F32, "ExternalOutput")
        dbg["rt"] = dr("dbg_rt", [TOWN, 12], F32, "ExternalOutput")
        dbg["gb"] = dr("dbg_gb", [EXP * 1025, 2], F32, "ExternalOutput")

    di = lambda n, s, d=F32: nc.dram_tensor(n, s, d, kind="Internal").ap()
    ds = lambda n, s, d=F32: nc.dram_tensor(n, s, d, kind="Internal",
                                            addr_space="Shared").ap()
    ab_my = di("ab_my", [NCH * D, 130])
    ab_allp = [ds(f"ab_all{p}", [NCORE * NCH * D, 130]) for p in range(2)]
    h0_dram = di("h0_dram", [64 * D, D])
    htm_bf_my = di("htm_bf_my", [TOWN, HD], BF16)
    htm_f32 = di("htm_f32", [TOWN, HD])
    ysh_dram = di("ysh_dram", [TOWN, HD], BF16)
    h_full = di("h_full", [NCORE * TOWN, HD], BF16)
    cnt_my = di("cnt_my", [1, EXP])
    cnt_all = ds("cnt_all", [NCORE, EXP])
    gbuf_my = di("gbuf_my", [EXP * 1025, 2])
    gbuf_all = ds("gbuf_all", [EXP * 1025, 2])
    eout_myh = [di(f"eout_myh{h}", [2 * CAP, 512], BF16) for h in range(4)]
    eout_allh = [di(f"eout_allh{h}", [EXP * CAP, 512], BF16) for h in range(4)]

    with es:
        sbA = lambda n, s, d=F32: es.enter_context(nc.sbuf_tensor(n, s, d))
        psA = lambda n, s, d=F32: es.enter_context(nc.psum_tensor(n, s, d))

        cf = sbA("cf", [128, 128 * 5])
        cb = sbA("cb", [128, 128], BF16)
        onesc = sbA("onesc", [128, 1])
        e127 = sbA("e127", [128, 1])
        iotac = sbA("iotac", [128, 1])
        iota16 = sbA("iota16", [128, 16])
        iota64 = sbA("iota64", [64, 1])
        prem = sbA("prem", [8, 1])
        eselt = sbA("eselt", [16, 2])
        cvt = sbA("cvt", [1, 4])
        rowpar = sbA("rowpar", [1, 4 * D])
        lnsig = sbA("lnsig", [1, D])
        bc64 = sbA("bc64", [128, 2 * D])
        fcon = sbA("fcon", [128, 3])
        gw_sb = sbA("gw_sb", [128, NHB * EXP])
        lgfm = sbA("lgfm", [16, TOWN])
        desti32 = sbA("desti32", [128, 32], I32)
        gidx32 = sbA("gidx32", [128, 32], I32)
        hidx32 = sbA("hidx32", [128, 16], I32)
        fmb = sbA("fmb", [128, NHB * TOWN], BF16)
        stS = sbA("stS", [128, NCH])
        errs = sbA("errs", [128, NCH])
        glc = sbA("glc", [64, NCH])
        idxg = sbA("idxg", [64, NCH], I32)
        scr = sbA("scr", [128, 1024])
        scr2 = sbA("scr2", [128, 2048])

        ARENA_COLS = 39300
        arena = sbA("arena", [128, ARENA_COLS])
        arena_bf = arena.bitcast(BF16)
        _aoff = [0]

        def aa(cols):
            s = arena[:, _aoff[0]:_aoff[0] + cols]
            _aoff[0] += cols
            assert _aoff[0] <= ARENA_COLS, f"arena overflow {_aoff[0]}"
            return s

        def aab(cols_bf):
            assert cols_bf % 2 == 0
            s = arena_bf[:, 2 * _aoff[0]:2 * _aoff[0] + cols_bf]
            _aoff[0] += cols_bf // 2
            return s

        def arena_reset():
            _aoff[0] = 0

        x_fm = aa(NHB * TOWN)
        x_fmb = fmb
        h_fmb = fmb
        MARK0 = _aoff[0]
        wcat_sb = aab(NHB * 256)
        xstg = aa(HD)
        proj = aa(NCH * 256)
        _aoff[0] = MARK0 + 6600
        stKV = aa(NCH * 128)
        stQt = aa(NCH * D)
        stQh = aa(NCH * D)
        stU0 = aa(NCH * D)
        stUc = aa(NCH * D)
        stG = aa(NCH * D)
        stLA = aa(NCH * D)
        fmQt = aa(NCH * L)[0:64, :]
        fmKt = aa(NCH * L)[0:64, :]
        fmQh = aa(NCH * L)[0:64, :]
        fmU = aa(NCH * L)[0:64, :]
        fmZ = aa(NCH * L)[0:64, :]
        solU = aa(NCH * L)
        solW = aa(NCH * L)
        solR = aa(NCH * 128)
        gP = aa(NCH * L)
        stM = aa(NCH * D)
        abATs = aa(NCH * D)[0:64, :]
        abBgs = aa(NCH * D)[0:64, :]
        h0my = aa(NCH * D)[0:64, :]
        SCAN_ARENA_END = _aoff[0]
        _aoff[0] = MARK0
        h0ping = aa(2 * D)[0:64, :]
        chABt = aa(2 * 130)[0:64, :]
        yattn = aa(NCH * D)
        ynfm = aab(TOWN)[0:64, :]
        wo_fm = aab(HD)[0:64, :]
        nwcol = aa(1)[0:64, :]
        rowa = aa(2048)[0:1, :]
        hstg_bf = aab(2 * HD)
        assert _aoff[0] <= MARK0 + 6600, f"overlay overflow {_aoff[0] - MARK0}"
        _aoff[0] = SCAN_ARENA_END

        p4 = [psA(f"p4_{i}", [128, 512]) for i in range(4)]
        pw = [psA(f"pw_{i}", [128, 1024]) for i in range(2)]

        def pslot(c):
            if c < 4:
                return p4[c], 0
            return pw[(c - 4) // 2], ((c - 4) % 2) * 512

        rq = es.enter_context(nc.semaphore("rq"))
        RN = [0]
        qG = es.enter_context(nc.semaphore("qG"))
        QG = [0]
        sXin = es.enter_context(nc.semaphore("sXin"))
        sTx = es.enter_context(nc.semaphore("sTx"))
        sVx = es.enter_context(nc.semaphore("sVx"))
        sFF = es.enter_context(nc.semaphore("sFF"))
        sFT = es.enter_context(nc.semaphore("sFT"))
        sFV = es.enter_context(nc.semaphore("sFV"))
        sFS = es.enter_context(nc.semaphore("sFS"))
        sFQ = es.enter_context(nc.semaphore("sFQ"))
        sEO = es.enter_context(nc.semaphore("sEO"))
        sCC = es.enter_context(nc.semaphore("sCC"))

        TAPS = {}

        nc.const_aps.register(F32, RMS_EPS, fcon[:, 0:1])
        nc.const_aps.register(F32, LN_EPS, fcon[:, 1:2])
        nc.const_aps.register(F32, 1e-8, fcon[:, 2:3])
        block = es.enter_context(nc.Block())
        IDE, TRI, SUU, SUN, SLN = 0, 128, 256, 384, 512

        def tap(name, ap, parts=128, dt=F32):
            if not debug:
                return
            cols = ap.shape[-1]
            t_d = nc.dram_tensor("tap_" + name, [parts, cols], dt, kind="ExternalOutput").ap()
            TAPS[name] = t_d

            def fdma(sync, ap=ap, t_d=t_d):
                sync.wait_ge(rq, RN[0])
                sync.dma_start(out=t_d[:], in_=ap, allow_slow_non_contiguous=True).then_inc(rq, 16)
                RN[0] += 16
            block.sync(fdma)

        def round_(engname, fn, inc=1):
            def body(eng):
                eng.wait_ge(rq, RN[0])
                last = fn(eng)
                last.then_inc(rq, inc)
                if engname == "vector":
                    for _ in range(6):
                        eng.engine_nop()
                elif engname == "scalar":
                    for _ in range(6):
                        eng.nop()
            getattr(block, engname)(body)
            RN[0] += inc

        if poison:
            PV = 1.0e38
            q4 = ARENA_COLS // 4
            for i in range(4):
                a0, a1 = i * q4, (ARENA_COLS if i == 3 else (i + 1) * q4)
                round_("vector", lambda v, a0=a0, a1=a1: v.memset(arena[:, a0:a1], PV))
            for tns in (scr, scr2, gw_sb, lgfm, stS, errs, glc, fcon, bc64, lnsig,
                        rowpar, cvt, eselt, prem, iota64, iota16, iotac, e127,
                        onesc, cf):
                round_("vector", lambda v, t_=tns: v.memset(t_[:], PV))
            round_("vector", lambda v: v.memset(cb[:], PV))
            round_("vector", lambda v: v.memset(fmb[:], PV))
            for it in (desti32, gidx32, hidx32, idxg):
                round_("vector", lambda v, t_=it: v.memset(t_[:], 1 << 30))
            for pt in p4 + pw:
                round_("vector", lambda v, t_=pt: v.memset(t_[:], PV))

            def poison_dram(sync):
                sync.wait_ge(rq, RN[0])
                n = 0
                s2b = scr2.bitcast(BF16)
                for t_, isbf in ((ab_my, 0), (ab_allp[0], 0), (ab_allp[1], 0),
                                 (h0_dram, 0), (htm_f32, 0),
                                 (cnt_my, 0), (cnt_all, 0), (gbuf_my, 0), (gbuf_all, 0),
                                 (htm_bf_my, 1), (ysh_dram, 1), (h_full, 1),
                                 *[(t, 1) for t in eout_myh],
                                 *[(t, 1) for t in eout_allh]):
                    rows, cols = int(t_.shape[0]), int(t_.shape[1])
                    src = s2b if isbf else scr2
                    for r0 in range(0, rows, 128):
                        r1 = min(rows, r0 + 128)
                        sync.dma_start(out=t_[r0:r1, :], in_=src[0:r1 - r0, 0:cols],
                                       allow_slow_non_contiguous=True).then_inc(rq, 16)
                        n += 16
                RN[0] += n
            block.sync(poison_dram)

        _scopes = []

        def scope(name):
            if _scopes:
                pn, sid = _scopes.pop()
                nc.leave_named_scope(pn, sid, False)
            if name is not None:
                sid, _ = nc.enter_named_scope(name, False)
                _scopes.append((name, sid))

        scope("P01_proj")

        def p0_consts(sync):
            sync.wait_ge(rq, RN[0])
            n = 0
            for dst, src in [(cf[:, 0:128], ident_f), (cf[:, 128:256], m_triuinc),
                             (cf[:, 256:384], m_striu), (cf[:, 384:512], m_striun),
                             (cf[:, 512:640], m_striln), (cb[:, 0:128], ident_b),
                             (onesc[:], onescol), (e127[:], e127c), (iotac[:], iota128),
                             (iota16[:], iota16g), (iota64[:], iota64c), (prem[:], premask),
                             (eselt[:], esel), (cvt[:], cvec), (rowpar[:], rowp),
                             (fcon[:], fconst_d)]:
                sync.dma_start(out=dst, in_=src[:]).then_inc(rq, 16)
                n += 16
            RN[0] += n
        block.sync(p0_consts)

        def p0_wcat(gp):
            gp.wait_ge(rq, RN[0])
            gp.dma_start(out=wcat_sb.rearrange("p (c n) -> p c n", n=256),
                         in_=wcatT.rearrange("(c p) n -> p c n", p=128)).then_inc(qG, 16)
            gp.dma_start(out=gw_sb.rearrange("p (c n) -> p c n", n=EXP),
                         in_=gwT.rearrange("(c p) n -> p c n", p=128)).then_inc(qG, 16)
            QG[0] += 32
            gp.engine_nop().then_inc(rq, 1)
            RN[0] += 1
        block.gpsimd(p0_wcat)

        round_("scalar", lambda sc: sc.activation(scr[0:1, 0:D], rowpar[:, 0:D], AF.Exp, scale=-1.0))

        round_("vector", lambda v: v.tensor_scalar_add(scr[0:1, 2 * D:3 * D], scr[0:1, 0:D], 1.0))
        round_("vector", lambda v: v.reciprocal(scr[0:1, D:2 * D], scr[0:1, 2 * D:3 * D]))
        round_("scalar", lambda sc: sc.activation(lnsig[:], scr[0:1, D:2 * D], AF.Ln, bias=1e-8))

        def f(t):
            t.matmul(p4[0][:, 0:D], cf[0:1, TRI:TRI + 128], lnsig[:], start=True, stop=True)
            return t.matmul(p4[0][:, D:2 * D], cf[0:1, TRI:TRI + 128], rowpar[0:1, D:2 * D],
                            start=True, stop=True)
        round_("tensor", f)
        round_("vector", lambda v: v.tensor_copy(bc64[:], p4[0][:, 0:2 * D]))
        tap("bc64", bc64[:])
        tap("lnsig", lnsig[:], 1)

        def p1_sync(sync):
            sync.wait_ge(rq, RN[0])
            for tc in range(NCH):
                if tc >= 1:
                    sync.wait_ge(sTx, tc * 4)
                sync.dma_start(out=xstg[:, 0:HD],
                               in_=x_own[tc * 128:(tc + 1) * 128, :]).then_inc(sXin, 16)
        block.sync(p1_sync)

        def p1_t(t):
            t.wait_ge(rq, RN[0])
            for tc in range(NCH):
                t.wait_ge(sXin, (tc + 1) * 16)
                for hb in range(NHB):
                    g = hb // 4
                    bank = p4[g]
                    if tc > 0 and hb % 4 == 0:
                        t.wait_ge(sVx, (tc - 1) * 4 + g + 1)
                    mm = t.transpose(bank[:, (hb % 4) * 128:(hb % 4 + 1) * 128],
                                     xstg[:, hb * 128:(hb + 1) * 128],
                                     cf[:, IDE:IDE + 128])
                    if hb % 4 == 3:
                        mm.then_inc(sTx, 1)
        block.tensor(p1_t)

        def p1_v(v):
            v.wait_ge(rq, RN[0])
            for tc in range(NCH):
                for g in range(4):
                    v.wait_ge(sTx, tc * 4 + g + 1)
                    dst = x_fm.rearrange("p (hb t) -> p hb t", t=TOWN)[:, 4 * g:4 * g + 4,
                                                                      tc * 128:(tc + 1) * 128]
                    src = p4[g].rearrange("p (k t) -> p k t", t=128)
                    v.tensor_copy(dst, src).then_inc(sVx, 1)
            v.wait_ge(sVx, 32)
            for g in range(4):
                v.tensor_copy(x_fmb[:, g * 4096:(g + 1) * 4096],
                              x_fm[:, g * 4096:(g + 1) * 4096])
            v.engine_nop().then_inc(rq, 1)
        block.vector(p1_v)
        RN[0] += 1

        def p1_tp(t):
            t.wait_ge(rq, RN[0])
            t.wait_ge(qG, 32)
            for tc in range(NCH):
                pt, off = pslot(tc)
                for hb in range(NHB):
                    mm = t.matmul(pt[:, off:off + 256],
                                  x_fmb[:, hb * TOWN + tc * 128:hb * TOWN + (tc + 1) * 128],
                                  wcat_sb[:, hb * 256:(hb + 1) * 256],
                                  start=(hb == 0), stop=(hb == NHB - 1))
            mm.then_inc(rq, 1)
            RN[0] += 1
        block.tensor(p1_tp)

        def f(v):
            for tc in range(NCH):
                pt, off = pslot(tc)
                last = v.tensor_copy(proj[:, tc * 256:(tc + 1) * 256], pt[:, off:off + 256])
            return last
        round_("vector", f)

        scope("P2_prep")
        def f(sc):
            for c in range(NCH):
                b = c * 256
                sc.activation(scr2[:, 2 * c * D:(2 * c + 1) * D], proj[:, b:b + D], AF.Square,
                              accum_out=scr[:, 512 + 2 * c:513 + 2 * c])
                last = sc.activation(scr2[:, (2 * c + 1) * D:(2 * c + 2) * D],
                                     proj[:, b + 128:b + 192], AF.Square,
                                     accum_out=scr[:, 513 + 2 * c:514 + 2 * c])
            return last
        round_("scalar", f)
        round_("scalar", lambda sc: sc.activation(scr[:, 528:528 + 2 * NCH],
                                                  scr[:, 512:512 + 2 * NCH], AF.Ln))
        round_("scalar", lambda sc: sc.activation(scr[:, 544:544 + 2 * NCH],
                                                  scr[:, 528:528 + 2 * NCH], AF.Exp, scale=-0.5))

        round_("vector", lambda v: v.tensor_scalar_min(scr[:, 560:560 + 2 * NCH],
                                                        scr[:, 544:544 + 2 * NCH], 1e12))

        def f(v):
            for c in range(NCH):
                b = c * 256
                v.tensor_scalar(scr2[:, c * 128:c * 128 + D], proj[:, b:b + D],
                                scr[:, 560 + 2 * c:561 + 2 * c], 0.0, ALU.mult, ALU.add)
                v.tensor_scalar(scr2[:, c * 128 + D:c * 128 + 2 * D], proj[:, b + 128:b + 192],
                                scr[:, 561 + 2 * c:562 + 2 * c], 0.0, ALU.mult, ALU.add)
                last = v.tensor_tensor(scr2[:, 1024 + c * D:1024 + (c + 1) * D],
                                       proj[:, b + 192:b + 256], bc64[:, D:2 * D], ALU.add)
            return last
        round_("vector", f)

        round_("scalar", lambda sc: sc.activation(scr[:, 0:NCH * D], scr2[:, 1024:1024 + NCH * D],
                                                  AF.Tanh, scale=0.5))

        def f(v):
            for c in range(NCH):
                last = v.tensor_scalar(scr[:, 512 + c * D:512 + (c + 1) * D],
                                       scr[:, c * D:(c + 1) * D], 0.5, 0.5, ALU.mult, ALU.add)
            return last
        round_("vector", f)

        def f(v):
            for c in range(NCH):
                last = v.scalar_tensor_tensor(stLA[:, c * D:(c + 1) * D],
                                              scr[:, 512 + c * D:512 + (c + 1) * D], 8.0,
                                              bc64[:, 0:D], ALU.mult, ALU.mult)
            return last
        round_("vector", f)

        tap("stLA", stLA[:])

        def f(t):
            for c in range(NCH):
                pt, off = pslot(c)
                last = t.matmul(pt[:, off:off + D], cf[:, TRI:TRI + 128],
                                stLA[:, c * D:(c + 1) * D], start=True, stop=True)
            return last
        round_("tensor", f)

        def f(sc):
            for c in range(NCH):
                pt, off = pslot(c)
                sc.activation(stG[:, c * D:(c + 1) * D], pt[:, off:off + D], AF.Exp)
                last = sc.activation(scr2[:, 1024 + c * D:1024 + (c + 1) * D],
                                     pt[:, off:off + D], AF.Exp, scale=-1.0)
            return last
        round_("scalar", f)

        def f(v):
            for c in range(NCH):
                pt, off = pslot(c)
                last = v.tensor_tensor(scr[:, c * D:(c + 1) * D], pt[:, off:off + D],
                                       stLA[:, c * D:(c + 1) * D], ALU.subtract)
            return last
        round_("vector", f)

        def f(sc):
            last = sc.activation(scr[:, 0:NCH * D], scr[:, 0:NCH * D], AF.Exp)
            return last
        round_("scalar", f)
        round_("scalar", lambda sc: sc.activation(scr2[:, 1536:1536 + NCH * D],
                                                  stLA[:, 0:NCH * D], AF.Exp))
        tap("gi", scr2[:, 1024:1536])
        tap("alpha", scr2[:, 1536:2048])
        tap("gprev", scr[:, 0:512])

        def f(v):
            kcol = lambda c: scr2[:, c * 128:c * 128 + D]
            qcol = lambda c: scr2[:, c * 128 + D:c * 128 + 2 * D]
            gp_ = lambda c: scr[:, c * D:(c + 1) * D]
            gi_ = lambda c: scr2[:, 1024 + c * D:1024 + (c + 1) * D]
            al_ = lambda c: scr2[:, 1536 + c * D:1536 + (c + 1) * D]
            for c in range(NCH):
                v.tensor_tensor(stKV[:, c * 128:c * 128 + D], kcol(c), gp_(c), ALU.mult)
                v.tensor_copy(stKV[:, c * 128 + D:c * 128 + 2 * D],
                              proj[:, c * 256 + D:c * 256 + 2 * D])
                v.tensor_tensor(stQt[:, c * D:(c + 1) * D], qcol(c), gp_(c), ALU.mult)
                v.tensor_tensor(stQh[:, c * D:(c + 1) * D], qcol(c),
                                stG[:, c * D:(c + 1) * D], ALU.mult)
            for c in range(NCH):
                last = v.tensor_scalar(scr[:, 512 + c * D:512 + (c + 1) * D], al_(c),
                                       -1.0, 1.0, ALU.mult, ALU.add)
            return last
        round_("vector", f)

        def f(v):
            kcol = lambda c: scr2[:, c * 128:c * 128 + D]
            for c in range(NCH):
                last = v.tensor_tensor(stUc[:, c * D:(c + 1) * D],
                                       scr[:, 512 + c * D:512 + (c + 1) * D], kcol(c), ALU.mult)
            return last
        round_("vector", f)

        def f(v):
            gi_ = lambda c: scr2[:, 1024 + c * D:1024 + (c + 1) * D]
            for c in range(NCH):
                last = v.tensor_tensor(stU0[:, c * D:(c + 1) * D],
                                       stUc[:, c * D:(c + 1) * D], gi_(c), ALU.mult)
            return last
        round_("vector", f)

        def f(t):
            for c in range(NCH):
                last = t.matmul(p4[0][0:1, c * D:(c + 1) * D], e127[:],
                                stG[:, c * D:(c + 1) * D], start=True, stop=True)
            return last
        round_("tensor", f)
        round_("vector", lambda v: v.tensor_copy(rowa[0:1, 0:NCH * D], p4[0][0:1, 0:NCH * D]))

        def f(t):
            for c in range(NCH):
                last = t.matmul(p4[1][0:64, c:c + 1], rowa[0:1, c * D:(c + 1) * D],
                                onesc[0:1, 0:1], start=True, stop=True)
            return last
        round_("tensor", f)
        round_("vector", lambda v: v.tensor_copy(glc[:], p4[1][0:64, 0:NCH]))

        def f(t):
            for c in range(NCH):
                pt, off = pslot(c)
                t.transpose(pt[0:64, off:off + 128], stQt[:, c * D:(c + 1) * D], cf[:, IDE:IDE + 128])
                t.transpose(pt[0:64, off + 128:off + 256], stKV[:, c * 128:c * 128 + D],
                            cf[:, IDE:IDE + 128])
                last = t.transpose(pt[0:64, off + 256:off + 384], stQh[:, c * D:(c + 1) * D],
                                   cf[:, IDE:IDE + 128])
            return last
        round_("tensor", f)

        def f(v):
            for c in range(NCH):
                pt, off = pslot(c)
                v.tensor_copy(fmQt[:, c * L:(c + 1) * L], pt[0:64, off:off + 128])
                v.tensor_copy(fmKt[:, c * L:(c + 1) * L], pt[0:64, off + 128:off + 256])
                last = v.tensor_copy(fmQh[:, c * L:(c + 1) * L], pt[0:64, off + 256:off + 384])
            return last
        round_("vector", f)

        tap("proj", proj[:])
        tap("stKV", stKV[:])
        tap("stQt", stQt[:])
        tap("stQh", stQh[:])
        tap("stU0", stU0[:])
        tap("stG", stG[:])
        tap("glc", glc[:], 64)
        tap("fmQt", fmQt[:], 64)

        scope("P3_scan")
        ch_base_col = 1
        for pas in range(2):
            if pas == 0:
                round_("vector", lambda v: v.tensor_copy(stUc[:], stU0[:]))
            else:
                def f(v):
                    for c in range(NCH):
                        last = v.tensor_scalar(stUc[:, c * D:(c + 1) * D],
                                               stU0[:, c * D:(c + 1) * D],
                                               stS[:, c:c + 1], 0.0, ALU.mult, ALU.add)
                    return last
                round_("vector", f)

            def f(t):
                for c in range(NCH):
                    pt, off = pslot(c)
                    last = t.transpose(pt[0:64, off:off + 128], stUc[:, c * D:(c + 1) * D],
                                       cf[:, IDE:IDE + 128])
                return last
            round_("tensor", f)

            def f(v):
                for c in range(NCH):
                    pt, off = pslot(c)
                    last = v.tensor_copy(fmU[:, c * L:(c + 1) * L], pt[0:64, off:off + 128])
                return last
            round_("vector", f)

            def f(t):
                for c in range(NCH):
                    pt, off = pslot(c)
                    t.matmul(pt[:, off:off + 128], fmKt[:, c * L:(c + 1) * L],
                             fmU[:, c * L:(c + 1) * L], start=True, stop=True)
                    last = t.matmul(pt[:, off + 128:off + 256], fmU[:, c * L:(c + 1) * L],
                                    fmKt[:, c * L:(c + 1) * L], start=True, stop=True)
                return last
            round_("tensor", f)

            def f(v):
                for c in range(NCH):
                    pt, off = pslot(c)
                    v.tensor_tensor(solW[:, c * L:(c + 1) * L], pt[:, off:off + 128],
                                    cf[:, SLN:SLN + 128], ALU.mult)
                    last = v.tensor_tensor(solU[:, c * L:(c + 1) * L], pt[:, off + 128:off + 256],
                                           cf[:, SUN:SUN + 128], ALU.mult)
                return last
            round_("vector", f)

            def f(t):
                for c in range(NCH):
                    pt, off = pslot(c)
                    rhs = fmQt if pas == 0 else fmQh
                    last = t.matmul(pt[:, off:off + 128], fmU[:, c * L:(c + 1) * L],
                                    rhs[:, c * L:(c + 1) * L], start=True, stop=True)
                return last
            round_("tensor", f)

            maskoff = SUU if pas == 0 else TRI
            def f(v):
                for c in range(NCH):
                    pt, off = pslot(c)
                    last = v.tensor_tensor(gP[:, c * L:(c + 1) * L], pt[:, off:off + 128],
                                           cf[:, maskoff:maskoff + 128], ALU.mult)
                return last
            round_("vector", f)

            round_("vector", lambda v: v.tensor_copy(solR[:], stKV[:]))

            for j in range(7):
                def f(t, j=j):
                    for c in range(NCH):
                        pt, off = pslot(c)
                        last = t.matmul(pt[:, off:off + 128], solU[:, c * L:(c + 1) * L],
                                        solR[:, c * 128:(c + 1) * 128], start=True, stop=True)
                        if j < 6:
                            t.matmul(pt[:, off + 128:off + 256], solU[:, c * L:(c + 1) * L],
                                     solW[:, c * L:(c + 1) * L], start=True, stop=True)
                            last = t.matmul(pt[:, off + 256:off + 384], solW[:, c * L:(c + 1) * L],
                                            solU[:, c * L:(c + 1) * L], start=True, stop=True)
                    return last
                round_("tensor", f)

                def f(v, j=j):
                    for c in range(NCH):
                        pt, off = pslot(c)
                        v.tensor_tensor(solR[:, c * 128:(c + 1) * 128], pt[:, off:off + 128],
                                        solR[:, c * 128:(c + 1) * 128], ALU.add)
                        if j < 6:
                            v.tensor_copy(solW[:, c * L:(c + 1) * L], pt[:, off + 128:off + 256])
                            last = v.tensor_copy(solU[:, c * L:(c + 1) * L], pt[:, off + 256:off + 384])
                        else:
                            last = v.engine_nop()
                    return last
                round_("vector", f)

            tap(f"solW0_p{pas}", solW[:])
            tap(f"solR_p{pas}", solR[:])

            def f(t):
                for c in range(NCH):
                    pt, off = pslot(c)
                    t.transpose(pt[0:64, off:off + 128], solR[:, c * 128:c * 128 + D],
                                cf[:, IDE:IDE + 128])
                    t.matmul(pt[0:64, off + 128:off + 192], solR[:, c * 128:c * 128 + D],
                             stUc[:, c * D:(c + 1) * D], start=True, stop=True)
                    last = t.matmul(pt[0:64, off + 192:off + 256], stUc[:, c * D:(c + 1) * D],
                                    solR[:, c * 128 + D:c * 128 + 2 * D], start=True, stop=True)
                return last
            round_("tensor", f)

            def f(v):
                for c in range(NCH):
                    pt, off = pslot(c)
                    v.tensor_scalar(fmZ[:, c * L:(c + 1) * L], pt[0:64, off:off + 128],
                                    -1.0, 0.0, ALU.mult, ALU.add)
                    v.tensor_tensor(abATs[:, c * D:(c + 1) * D], cf[0:64, IDE:IDE + D],
                                    pt[0:64, off + 128:off + 192], ALU.subtract)
                    last = v.tensor_scalar(abBgs[:, c * D:(c + 1) * D],
                                           pt[0:64, off + 192:off + 256],
                                           glc[:, c:c + 1], 0.0, ALU.mult, ALU.add)
                return last
            round_("vector", f)

            def p3_abdma(sync):
                sync.wait_ge(rq, RN[0])
                abv = ab_my.rearrange("(c r) n -> r c n", r=D)
                sync.dma_start(out=abv[:, :, 0:D],
                               in_=abATs.rearrange("r (c n) -> r c n", n=D),
                               allow_slow_non_contiguous=True).then_inc(rq, 16)
                sync.dma_start(out=abv[:, :, D:2 * D],
                               in_=abBgs.rearrange("r (c n) -> r c n", n=D),
                               allow_slow_non_contiguous=True).then_inc(rq, 16)
                sync.dma_start(out=abv[:, :, 2 * D:2 * D + 1],
                               in_=glc[:, 0:NCH].rearrange("r (c x) -> r c x", x=1),
                               allow_slow_non_contiguous=True).then_inc(rq, 16)
                sync.dma_start(out=abv[:, :, 2 * D + 1:2 * D + 2],
                               in_=glc[:, 0:NCH].rearrange("r (c x) -> r c x", x=1),
                               allow_slow_non_contiguous=True).then_inc(rq, 16)
                RN[0] += 16 * 4
            block.sync(p3_abdma)

            def p3_ag(gp):
                gp.wait_ge(rq, RN[0])
                gp.collective_compute(
                    "AllGather", ALU.bypass,
                    replica_groups=[list(range(NCORE))],
                    ins=[ab_my.opt()], outs=[ab_allp[pas].opt()],
                ).then_inc(rq, 1)
                RN[0] += 1
            block.gpsimd(p3_ag)

            h0ring = scr[0:64, 0:512]
            ab_ov = arena[0:64, MARK0:MARK0 + 50 * 130]
            ab_s2 = scr2[0:64, 0:14 * 130]

            def slotAB(gc):
                if gc < 50:
                    return ab_ov[:, gc * 130:(gc + 1) * 130]
                return ab_s2[:, (gc - 50) * 130:(gc - 49) * 130]

            round_("vector", lambda v: v.memset(h0ring[:, 0:D], 0.0))

            sx0 = NCH * 16 + pas * 160

            def p3_chain_dma(sync):
                sync.wait_ge(rq, RN[0])
                sync.dma_start(out=ab_ov.rearrange("r (c n) -> r c n", n=130),
                               in_=ab_allp[pas][0:50 * D, :].rearrange("(c r) n -> r c n", r=D),
                               allow_slow_non_contiguous=True).then_inc(sXin, 16)
                sync.dma_start(out=ab_s2.rearrange("r (c n) -> r c n", n=130),
                               in_=ab_allp[pas][50 * D:64 * D, :].rearrange("(c r) n -> r c n", r=D),
                               allow_slow_non_contiguous=True).then_inc(sXin, 16)
                for k in range(8):
                    sync.wait_ge(rq, RN[0] + 2 * (8 * k + 6) + 2)
                    sync.dma_start(out=h0_dram[k * 8 * D:(k + 1) * 8 * D, :]
                                   .rearrange("(g r) c -> r g c", r=D),
                                   in_=h0ring[:, 0:512].rearrange("r (g c) -> r g c", c=D),
                                   allow_slow_non_contiguous=True).then_inc(sXin, 16)
            block.sync(p3_chain_dma)

            def p3_chain_t(t):
                t.wait_ge(rq, RN[0])
                t.wait_ge(sXin, sx0 + 32)
                for gc in range(64):
                    bank = p4[gc % 2]
                    t.matmul(bank[0:64, 0:D], slotAB(gc)[:, 0:D],
                             h0ring[:, (gc % 8) * D:(gc % 8 + 1) * D],
                             start=True, stop=True).then_inc(rq, 1)
                    t.wait_ge(rq, RN[0] + (gc + 1) * 2)
                RN[0] += 64 * 2

            def p3_chain_v(v):
                base = RN[0] - 64 * 2
                for gc in range(64):
                    v.wait_ge(rq, base + gc * 2 + 1)
                    if gc % 8 == 7:
                        v.wait_ge(sXin, sx0 + 32 + ((gc // 8) + 1) * 16)
                    bank = p4[gc % 2]
                    v.scalar_tensor_tensor(h0ring[:, ((gc + 1) % 8) * D:((gc + 1) % 8 + 1) * D],
                                           bank[0:64, 0:D],
                                           slotAB(gc)[:, 2 * D:2 * D + 1],
                                           slotAB(gc)[:, D:2 * D],
                                           ALU.mult, ALU.add).then_inc(rq, 1)
            block.tensor(p3_chain_t)
            block.vector(p3_chain_v)

            def f(t):
                return t.matmul(p4[2][0:64, 0:1], cf[0:1, TRI:TRI + 64], cvt[0:1, 1:2],
                                start=True, stop=True)
            round_("tensor", f)

            def f(v):
                for c in range(NCH):
                    last = v.scalar_tensor_tensor(scr[0:64, c:c + 1], p4[2][0:64, 0:1], float(D),
                                                  iota64[:], ALU.mult, ALU.add)
                return last
            round_("vector", f)

            def f(v):
                for c in range(1, NCH):
                    v.tensor_scalar_add(scr[0:64, 8 + c:9 + c], scr[0:64, c:c + 1], float(c * D))
                return v.tensor_copy(scr[0:64, 8:9], scr[0:64, 0:1])
            round_("vector", f)
            round_("vector", lambda v: v.tensor_copy(idxg[:], scr[0:64, 8:8 + NCH]))

            def p3_h0g(gp):
                gp.wait_ge(rq, RN[0])
                gp.wait_ge(sXin, sx0 + 160)
                breg = gp.to_reg(64 * D - 1)
                for c in range(NCH):
                    gp.indirect_dma_start(
                        h0my[:, c * D:(c + 1) * D],
                        None, h0_dram[:],
                        IndirectOffsetOnAxis(idxg[:, c:c + 1], 0),
                        bounds_check=breg, oob_is_err=False,
                    ).then_inc(qG, 16)
                QG[0] += 16 * NCH
                gp.engine_nop().then_inc(rq, 1)
                RN[0] += 1
            block.gpsimd(p3_h0g)
            QGH0 = QG[0]

            def p3_h0w(t):
                t.wait_ge(qG, QGH0)
            block.tensor(p3_h0w)

            tap(f"h0my_p{pas}", h0my[:], 64)
            tap(f"abAT_p{pas}", abATs[:], 64)
            tap(f"abBg_p{pas}", abBgs[:], 64)

            def f(t):
                for c in range(NCH):
                    pt, off = pslot(c)
                    last = t.matmul(pt[:, off:off + D], fmZ[:, c * L:(c + 1) * L],
                                    h0my[:, c * D:(c + 1) * D], start=True, stop=True)
                return last
            round_("tensor", f)

            def f(v):
                for c in range(NCH):
                    pt, off = pslot(c)
                    last = v.tensor_tensor(stM[:, c * D:(c + 1) * D], pt[:, off:off + D],
                                           solR[:, c * 128 + D:c * 128 + 2 * D], ALU.add)
                return last
            round_("vector", f)

            def f(t):
                for c in range(NCH):
                    pt, off = pslot(c)
                    lhs = fmQt if pas == 0 else fmQh
                    t.matmul(pt[:, off:off + D], lhs[:, c * L:(c + 1) * L],
                             h0my[:, c * D:(c + 1) * D], start=True, stop=False)
                    last = t.matmul(pt[:, off:off + D], gP[:, c * L:(c + 1) * L],
                                    stM[:, c * D:(c + 1) * D], start=False, stop=True)
                return last
            round_("tensor", f)

            if pas == 0:
                def f(v):
                    for c in range(NCH):
                        pt, off = pslot(c)
                        last = v.tensor_tensor(scr[:, c * D:(c + 1) * D],
                                               stKV[:, c * 128 + D:c * 128 + 2 * D],
                                               pt[:, off:off + D], ALU.subtract)
                    return last
                round_("vector", f)

                def f(sc):
                    for c in range(NCH):
                        last = sc.activation(scr2[:, c * D:(c + 1) * D], scr[:, c * D:(c + 1) * D],
                                             AF.Square, accum_out=errs[:, c:c + 1])
                    return last
                round_("scalar", f)
                round_("scalar", lambda sc: sc.activation(scr[:, 0:NCH], errs[:, 0:NCH],
                                                          AF.Tanh, scale=SURSC / 2))
                round_("vector", lambda v: v.tensor_scalar(stS[:, 0:NCH], scr[:, 0:NCH],
                                                           0.5, 0.5, ALU.mult, ALU.add))
            else:
                def f(v):
                    for c in range(NCH):
                        pt, off = pslot(c)
                        last = v.tensor_copy(yattn[:, c * D:(c + 1) * D], pt[:, off:off + D])
                    return last
                round_("vector", f)

        tap("stM_p1", stM[:])
        tap("errs", errs[:])
        tap("stS", stS[:])
        tap("yattn", yattn[:])

        scope("P4_wo_ln")
        def p4_wo(sync):
            sync.wait_ge(rq, RN[0])
            sync.dma_start(out=scr2[0:64, 0:HD], in_=woT[:]).then_inc(rq, 16)
            RN[0] += 16
        block.sync(p4_wo)

        def f(t):
            return t.matmul(p4[0][0:64, 0:1], rowpar[0:1, 2 * D:3 * D], onesc[0:1, 0:1],
                            start=True, stop=True)
        round_("tensor", f)

        round_("vector", lambda v: v.tensor_copy(nwcol[:], p4[0][0:64, 0:1]))
        round_("vector", lambda v: v.tensor_scalar(wo_fm[:], scr2[0:64, 0:HD], nwcol[:],
                                                   0.0, ALU.mult, ALU.add))

        def f(sc):
            for c in range(NCH):
                last = sc.activation(scr2[:, c * D:(c + 1) * D],
                                     yattn[:, c * D:(c + 1) * D], AF.Square,
                                     accum_out=scr[:, 512 + c:513 + c])
            return last
        round_("scalar", f)
        round_("scalar", lambda sc: sc.activation(scr[:, 520:520 + NCH], scr[:, 512:512 + NCH],
                                                  AF.Ln, scale=1.0 / D, bias=RMS_EPS))
        round_("scalar", lambda sc: sc.activation(scr[:, 528:528 + NCH], scr[:, 520:520 + NCH],
                                                  AF.Exp, scale=-0.5))

        def f(v):
            for c in range(NCH):
                last = v.tensor_scalar(scr2[:, c * D:(c + 1) * D],
                                       yattn[:, c * D:(c + 1) * D],
                                       scr[:, 528 + c:529 + c], 0.0, ALU.mult, ALU.add)
            return last
        round_("vector", f)

        tap("ynm", scr2[:, 0:NCH * D])
        tap("rstdy", scr[:, 528:528 + NCH])
        tap("wofm", wo_fm[:], 64, BF16)

        def f(t):
            for c in range(NCH):
                pt, off = pslot(c)
                last = t.transpose(pt[0:64, off:off + 128],
                                   scr2[:, c * D:(c + 1) * D], cf[:, IDE:IDE + 128])
            return last
        round_("tensor", f)

        def f(v):
            for c in range(NCH):
                pt, off = pslot(c)
                last = v.tensor_copy(ynfm[:, c * 128:(c + 1) * 128],
                                     pt[0:64, off:off + 128])
            return last
        round_("vector", f)

        for ob2 in range(NHB // 2):
            def f(t, ob2=ob2):
                for k in range(2):
                    ob = ob2 * 2 + k
                    for half in range(2):
                        last = t.matmul(pw[k][:, half * 512:(half + 1) * 512],
                                        wo_fm[:, ob * 128:(ob + 1) * 128],
                                        ynfm[:, half * 512:(half + 1) * 512],
                                        start=True, stop=True)
                return last
            round_("tensor", f)

            def f(v, ob2=ob2):
                for k in range(2):
                    ob = ob2 * 2 + k
                    last = v.tensor_tensor(x_fm[:, ob * TOWN:(ob + 1) * TOWN],
                                           pw[k][:, 0:1024],
                                           x_fm[:, ob * TOWN:(ob + 1) * TOWN], ALU.add)
                return last
            round_("vector", f)

        tap("s1fm", x_fm[:, 0:2048])

        for tile in range(2):
            tsl = slice(tile * 512, (tile + 1) * 512)
            def f(t, tile=tile):
                for hb in range(NHB):
                    last = t.matmul(p4[0][0:1, 0:512],
                                    onesc[:], x_fm[:, hb * TOWN + tile * 512:hb * TOWN + (tile + 1) * 512],
                                    start=(hb == 0), stop=(hb == NHB - 1))
                return last
            round_("tensor", f)
            round_("vector", lambda v, tile=tile: v.tensor_copy(rowa[0:1, tile * 512:(tile + 1) * 512],
                                                                p4[0][0:1, 0:512]))
            for hb in range(NHB):
                round_("scalar", lambda sc, hb=hb, tile=tile: sc.activation(
                    scr2[:, 0:512],
                    x_fm[:, hb * TOWN + tile * 512:hb * TOWN + (tile + 1) * 512], AF.Square))
                round_("tensor", lambda t, hb=hb: t.matmul(
                    p4[1][0:1, 0:512], onesc[:], scr2[:, 0:512],
                    start=(hb == 0), stop=(hb == NHB - 1)))
            round_("vector", lambda v, tile=tile: v.tensor_copy(rowa[0:1, 1024 + tile * 512:1024 + (tile + 1) * 512],
                                                                p4[1][0:1, 0:512]))

        def f(v):
            v.tensor_scalar_mul(rowa[0:1, 0:1024], rowa[0:1, 0:1024], 1.0 / HD)
            return v.tensor_scalar_mul(rowa[0:1, 1024:2048], rowa[0:1, 1024:2048], 1.0 / HD)
        round_("vector", f)
        round_("vector", lambda v: v.tensor_tensor(scr[0:1, 0:1024], rowa[0:1, 0:1024],
                                                   rowa[0:1, 0:1024], ALU.mult))
        round_("vector", lambda v: v.tensor_tensor(rowa[0:1, 1024:2048], rowa[0:1, 1024:2048],
                                                   scr[0:1, 0:1024], ALU.subtract))
        round_("scalar", lambda sc: sc.activation(rowa[0:1, 1024:2048], rowa[0:1, 1024:2048],
                                                  AF.Ln, bias=LN_EPS))
        round_("scalar", lambda sc: sc.activation(rowa[0:1, 1024:2048], rowa[0:1, 1024:2048],
                                                  AF.Exp, scale=-0.5))
        def f(v):
            return v.tensor_tensor(rowa[0:1, 0:1024], rowa[0:1, 0:1024],
                                   rowa[0:1, 1024:2048], ALU.mult)
        round_("vector", f)

        def f(t):
            for half in range(2):
                t.matmul(pw[0][:, half * 512:(half + 1) * 512], cf[0:1, TRI:TRI + 128],
                         rowa[0:1, 1024 + half * 512:1024 + (half + 1) * 512],
                         start=True, stop=True)
                last = t.matmul(pw[1][:, half * 512:(half + 1) * 512], cf[0:1, TRI:TRI + 128],
                                rowa[0:1, half * 512:(half + 1) * 512],
                                start=True, stop=True)
            return last
        round_("tensor", f)

        def f(v):
            for hb in range(NHB):
                last = v.tensor_tensor(x_fm[:, hb * TOWN:(hb + 1) * TOWN],
                                       x_fm[:, hb * TOWN:(hb + 1) * TOWN], pw[0][:, 0:1024],
                                       ALU.mult)
            return last
        round_("vector", f)

        def f(v):
            for hb in range(NHB):
                last = v.tensor_tensor(x_fm[:, hb * TOWN:(hb + 1) * TOWN],
                                       x_fm[:, hb * TOWN:(hb + 1) * TOWN], pw[1][:, 0:1024],
                                       ALU.subtract)
            return last
        round_("vector", f)

        def f(v):
            for hb in range(NHB):
                last = v.tensor_copy(h_fmb[:, hb * TOWN:(hb + 1) * TOWN],
                                     x_fm[:, hb * TOWN:(hb + 1) * TOWN])
            return last
        round_("vector", f)

        tap("rowa", rowa[:], 1)

        def emit_h_tm():
            RQ_EH = RN[0]

            def pslot_h(tc, g):
                if tc % 2 == 0:
                    return p4[g][:, 0:512]
                return pw[g // 2][:, (g % 2) * 512:(g % 2 + 1) * 512]

            for tc in range(NCH):
                def ft(t, tc=tc):
                    if tc <= 1:
                        t.wait_ge(rq, RQ_EH)
                    if tc >= 2:
                        t.wait_ge(sVx, 32 + tc - 1)
                    for hb in range(NHB):
                        g = hb // 4
                        mm = t.transpose(pslot_h(tc, g)[:, (hb % 4) * 128:(hb % 4) * 128 + 128],
                                         x_fm[:, hb * TOWN + tc * 128:hb * TOWN + (tc + 1) * 128],
                                         cf[:, IDE:IDE + 128])
                    mm.then_inc(sTx, 1)
                block.tensor(ft)

                rqd = RN[0]
                def fv(v, tc=tc, rqd=rqd):
                    v.wait_ge(sTx, 32 + tc + 1)
                    v.wait_ge(rq, rqd)
                    for g in range(4):
                        v.tensor_copy(hstg_bf[:, g * 512:(g + 1) * 512], pslot_h(tc, g))
                    for g in range(4):
                        v.tensor_copy(scr2[:, g * 512:(g + 1) * 512], pslot_h(tc, g))
                    for _ in range(6):
                        v.engine_nop()
                    v.engine_nop().then_inc(sVx, 1)
                block.vector(fv)

                def fd(sync, tc=tc):
                    sync.wait_ge(sVx, 32 + tc + 1)
                    sync.dma_start(out=htm_bf_my[tc * 128:(tc + 1) * 128, :],
                                   in_=hstg_bf[:, 0:HD]).then_inc(rq, 16)
                    sync.dma_start(out=htm_f32[tc * 128:(tc + 1) * 128, :],
                                   in_=scr2[:, 0:HD]).then_inc(rq, 16)
                    RN[0] += 32
                block.sync(fd)
        emit_h_tm()

        def p4_hag(gp):
            gp.wait_ge(rq, RN[0])
            gp.collective_compute("AllGather", ALU.bypass,
                                  replica_groups=[list(range(NCORE))],
                                  ins=[htm_bf_my.opt()], outs=[h_full.opt()]).then_inc(sCC, 1)
        block.gpsimd(p4_hag)

        scope("P5_router")
        for tile in range(2):
            def f(t, tile=tile):
                t.wait_ge(qG, 32)
                if tile == 0:
                    t.wait_ge(sVx, 40)
                for hb in range(NHB):
                    last = t.matmul(p4[tile % 2][0:16, 0:512],
                                    gw_sb[:, hb * EXP:(hb + 1) * EXP],
                                    x_fm[:, hb * TOWN + tile * 512:hb * TOWN + (tile + 1) * 512],
                                    start=(hb == 0), stop=(hb == NHB - 1))
                return last
            round_("tensor", f)
            round_("vector", lambda v, tile=tile: v.tensor_copy(
                lgfm[:, tile * 512:(tile + 1) * 512], p4[tile % 2][0:16, 0:512]))

        arena_reset()
        rscr = aa(1040)
        rscr2 = aa(1032)
        lgtm = aa(8 * 16)
        vals = aa(32)
        e4t = aa(32)
        s4t = aa(8)
        lns4 = aa(8)
        negm4 = aa(8)
        negb4 = aa(8)
        idx4 = aa(32)
        w4 = aa(32)
        excl = aa(4 * 8 * 16)
        maskg = aa(8 * 16)
        rankg = aa(8 * 16)
        slot4 = aa(32)
        slotc4 = aa(32)
        destf = aa(32)
        gidxf = aa(32)
        tokid0 = aa(1)
        tokidc = aa(8)
        pairsA = aa(64)
        cntrow = aa(8 * 16)[0:1, :]
        carrow = aa(8 * 16)[0:1, :]
        prefr = aa(16)[0:1, :]
        cnasb = aa(16)[0:8, :]
        gbsb = aa(2052)[0:16, :]
        myg = aa(2052)[0:2, :]
        idcols = aa(16)
        wcols = aa(16)
        workt = rscr[:, 384:640]
        hidxf = aa(16)
        FFN_MARK = _aoff[0]

        rB = es.enter_context(nc.semaphore("rB"))
        RB = [0]
        RQ_P5A = RN[0]
        BQ = []
        BFIRST = [True]

        def roundB(engname, fn, inc=1):
            def thunk(engname=engname, fn=fn, inc=inc):
                first = BFIRST[0]
                BFIRST[0] = False

                def body(eng):
                    if first:
                        eng.wait_ge(rq, RQ_P5A)
                    eng.wait_ge(rB, RB[0])
                    last = fn(eng)
                    last.then_inc(rB, inc)
                    if engname == "vector":
                        for _ in range(6):
                            eng.engine_nop()
                    elif engname == "scalar":
                        for _ in range(6):
                            eng.nop()
                getattr(block, engname)(body)
                RB[0] += inc
            BQ.append(thunk)

        def rawB(engname, emitfn):
            def thunk(engname=engname, emitfn=emitfn):
                getattr(block, engname)(emitfn)
            BQ.append(thunk)

        def emitB(n):
            for _ in range(min(n, len(BQ))):
                BQ.pop(0)()

        def bslot(i):
            return p4[i % 4], (i // 4) * 16

        def f(t):
            for i in range(8):
                pt, off = bslot(i)
                last = t.transpose(pt[:, off:off + 16],
                                   lgfm[:, i * 128:(i + 1) * 128], cf[0:16, IDE:IDE + 16])
            return last
        roundB("tensor", f)

        def f(v):
            for i in range(8):
                pt, off = bslot(i)
                last = v.tensor_copy(lgtm[:, i * 16:(i + 1) * 16], pt[:, off:off + 16])
            return last
        roundB("vector", f)

        roundB("vector", lambda v: v.tensor_scalar_add(rscr[:, 960:976], iota16[:], BIGF))
        roundB("vector", lambda v: v.tensor_copy(workt[:, 0:128], lgtm[:]))
        roundB("vector", lambda v: v.memset(maskg[:], 0.0))

        for k in range(4):
            wsrc = workt[:, (k % 2) * 128:(k % 2) * 128 + 128]
            wdst = workt[:, ((k + 1) % 2) * 128:((k + 1) % 2) * 128 + 128]

            def mk(fn):
                def f(v, k=k, wsrc=wsrc, wdst=wdst, fn=fn):
                    for i in range(8):
                        last = fn(v, i, k, wsrc, wdst)
                    return last
                roundB("vector", f)
            mk(lambda v, i, k, ws, wd: v.tensor_reduce(vals[:, i * 4 + k:i * 4 + k + 1],
                                                       ws[:, i * 16:(i + 1) * 16], AX.X, ALU.max))
            mk(lambda v, i, k, ws, wd: v.tensor_scalar(rscr[:, i * 16:(i + 1) * 16],
                                                       ws[:, i * 16:(i + 1) * 16],
                                                       vals[:, i * 4 + k:i * 4 + k + 1], 0.0,
                                                       ALU.is_equal, ALU.add))
            mk(lambda v, i, k, ws, wd: v.scalar_tensor_tensor(
                rscr[:, 128 + i * 16:128 + (i + 1) * 16], rscr[:, i * 16:(i + 1) * 16], -BIGF,
                rscr[:, 960:976], ALU.mult, ALU.add))
            mk(lambda v, i, k, ws, wd: v.tensor_reduce(idx4[:, i * 4 + k:i * 4 + k + 1],
                                                       rscr[:, 128 + i * 16:128 + (i + 1) * 16],
                                                       AX.X, ALU.min))
            mk(lambda v, i, k, ws, wd: v.tensor_scalar(
                excl[:, (k * 8 + i) * 16:(k * 8 + i + 1) * 16], iota16[:],
                idx4[:, i * 4 + k:i * 4 + k + 1], 0.0, ALU.is_equal, ALU.add))
            mk(lambda v, i, k, ws, wd: v.scalar_tensor_tensor(
                wd[:, i * 16:(i + 1) * 16], excl[:, (k * 8 + i) * 16:(k * 8 + i + 1) * 16],
                -BIGF, ws[:, i * 16:(i + 1) * 16], ALU.mult, ALU.add))
            mk(lambda v, i, k, ws, wd: v.tensor_tensor(
                maskg[:, i * 16:(i + 1) * 16], maskg[:, i * 16:(i + 1) * 16],
                excl[:, (k * 8 + i) * 16:(k * 8 + i + 1) * 16], ALU.add))

        def f(v):
            for i in range(8):
                last = v.tensor_reduce(rscr[:, i:i + 1], vals[:, i * 4:(i + 1) * 4], AX.X, ALU.max)
            return last
        roundB("vector", f)

        def f(v):
            for i in range(8):
                last = v.tensor_scalar(negm4[:, i:i + 1], rscr[:, i:i + 1], -1.0, 0.0,
                                       ALU.mult, ALU.add)
            return last
        roundB("vector", f)

        def f(sc):
            for i in range(8):
                last = sc.activation(e4t[:, i * 4:(i + 1) * 4], vals[:, i * 4:(i + 1) * 4],
                                     AF.Exp, bias=negm4[:, i:i + 1])
            return last
        roundB("scalar", f)

        def f(v):
            for i in range(8):
                last = v.tensor_reduce(s4t[:, i:i + 1], e4t[:, i * 4:(i + 1) * 4], AX.X, ALU.add)
            return last
        roundB("vector", f)
        roundB("scalar", lambda sc: sc.activation(lns4[:], s4t[:], AF.Ln))

        def f(v):
            for i in range(8):
                last = v.tensor_tensor(rscr[:, i:i + 1], negm4[:, i:i + 1], lns4[:, i:i + 1],
                                       ALU.subtract)
            return last
        roundB("vector", f)
        roundB("vector", lambda v: v.tensor_copy(negb4[:], rscr[:, 0:8]))

        def f(sc):
            for i in range(8):
                last = sc.activation(w4[:, i * 4:(i + 1) * 4], vals[:, i * 4:(i + 1) * 4],
                                     AF.Exp, bias=negb4[:, i:i + 1])
            return last
        roundB("scalar", f)

        def f(t):
            for i in range(8):
                last = t.matmul(p4[0][0:1, i * 16:(i + 1) * 16], onesc[:],
                                maskg[:, i * 16:(i + 1) * 16], start=True, stop=True)
            return last
        roundB("tensor", f)
        roundB("vector", lambda v: v.tensor_copy(cntrow[:], p4[0][0:1, 0:128]))

        def f(v):
            v.tensor_tensor(rscr[0:1, 0:16], cntrow[0:1, 0:16], cntrow[0:1, 16:32], ALU.add)
            v.tensor_tensor(rscr[0:1, 16:32], cntrow[0:1, 32:48], cntrow[0:1, 48:64], ALU.add)
            v.tensor_tensor(rscr[0:1, 32:48], cntrow[0:1, 64:80], cntrow[0:1, 80:96], ALU.add)
            return v.tensor_tensor(rscr[0:1, 48:64], cntrow[0:1, 96:112], cntrow[0:1, 112:128],
                                   ALU.add)
        roundB("vector", f)

        def f(v):
            v.tensor_tensor(rscr[0:1, 64:80], rscr[0:1, 0:16], rscr[0:1, 16:32], ALU.add)
            return v.tensor_tensor(rscr[0:1, 80:96], rscr[0:1, 32:48], rscr[0:1, 48:64], ALU.add)
        roundB("vector", f)
        roundB("vector", lambda v: v.tensor_tensor(rscr[0:1, 96:112], rscr[0:1, 64:80],
                                                   rscr[0:1, 80:96], ALU.add))

        def fd(sync):
            sync.wait_ge(rB, RB[0])
            sync.dma_start(out=cnt_my[:], in_=rscr[0:1, 96:112]).then_inc(rB, 16)
            RB[0] += 16
        rawB("sync", fd)

        def fg(gp):
            gp.wait_ge(rB, RB[0])
            gp.collective_compute("AllGather", ALU.bypass,
                                  replica_groups=[list(range(NCORE))],
                                  ins=[cnt_my.opt()], outs=[cnt_all.opt()]).then_inc(rB, 1)
            RB[0] += 1
        rawB("gpsimd", fg)

        def fd(sync):
            sync.wait_ge(rB, RB[0])
            sync.dma_start(out=cnasb[:], in_=cnt_all[:]).then_inc(rB, 16)
            RB[0] += 16
        rawB("sync", fd)

        def f(t):
            return t.matmul(p4[1][0:1, 0:16], prem[:], cnasb[:], start=True, stop=True)
        roundB("tensor", f)
        roundB("vector", lambda v: v.tensor_copy(prefr[:], p4[1][0:1, 0:16]))

        roundB("vector", lambda v: v.tensor_copy(carrow[0:1, 0:16], prefr[:]))
        for i in range(1, 8):
            roundB("vector", lambda v, i=i: v.tensor_tensor(
                carrow[0:1, i * 16:(i + 1) * 16], carrow[0:1, (i - 1) * 16:i * 16],
                cntrow[0:1, (i - 1) * 16:i * 16], ALU.add))

        def f(t):
            for i in range(8):
                pt, off = bslot(i)
                t.matmul(pt[:, off:off + 16], cf[:, SUU:SUU + 128],
                         maskg[:, i * 16:(i + 1) * 16], start=True, stop=False)
                last = t.matmul(pt[:, off:off + 16], cf[0:1, TRI:TRI + 128],
                                carrow[0:1, i * 16:(i + 1) * 16], start=False, stop=True)
            return last
        roundB("tensor", f)

        def f(v):
            for i in range(8):
                pt, off = bslot(i)
                last = v.tensor_copy(rankg[:, i * 16:(i + 1) * 16], pt[:, off:off + 16])
            return last
        roundB("vector", f)

        def f(v):
            for i in range(8):
                for k in range(4):
                    last = v.tensor_tensor(rscr[:, (i * 4 + k) * 16:(i * 4 + k + 1) * 16],
                                           rankg[:, i * 16:(i + 1) * 16],
                                           excl[:, (k * 8 + i) * 16:(k * 8 + i + 1) * 16],
                                           ALU.mult)
            return last
        roundB("vector", f)

        def f(v):
            for j in range(32):
                last = v.tensor_reduce(slot4[:, j:j + 1], rscr[:, j * 16:(j + 1) * 16],
                                       AX.X, ALU.add)
            return last
        roundB("vector", f)

        def f(v):
            for j in range(32):
                last = v.tensor_scalar_min(slotc4[:, j:j + 1], slot4[:, j:j + 1], float(CAP))
            return last
        roundB("vector", f)

        def f(v):
            for j in range(32):
                v.scalar_tensor_tensor(destf[:, j:j + 1], idx4[:, j:j + 1], 1025.0,
                                       slotc4[:, j:j + 1], ALU.mult, ALU.add)
            for j in range(32):
                v.tensor_scalar(rscr[:, j:j + 1], slot4[:, j:j + 1], float(CAP), BIGF,
                                ALU.is_ge, ALU.mult)
            for j in range(32):
                last = v.scalar_tensor_tensor(rscr[:, 32 + j:33 + j], idx4[:, j:j + 1], 1024.0,
                                              slot4[:, j:j + 1], ALU.mult, ALU.add)
            return last
        roundB("vector", f)

        def f(v):
            for j in range(32):
                last = v.tensor_tensor(gidxf[:, j:j + 1], rscr[:, 32 + j:33 + j],
                                       rscr[:, j:j + 1], ALU.add)
            return last
        roundB("vector", f)

        def f(t):
            return t.matmul(p4[2][:, 0:1], cf[0:1, TRI:TRI + 128], cvt[0:1, 0:1],
                            start=True, stop=True)
        roundB("tensor", f)

        roundB("vector", lambda v: v.scalar_tensor_tensor(tokid0[:], p4[2][:, 0:1], 1.0,
                                                          iotac[:], ALU.mult, ALU.add))

        def f(v):
            for i in range(8):
                last = v.tensor_scalar_add(tokidc[:, i:i + 1], tokid0[:], float(i * 128 + 1))
            return last
        roundB("vector", f)

        def f(v):
            for i in range(8):
                for k in range(4):
                    j = i * 4 + k
                    v.tensor_copy(pairsA[:, 2 * j:2 * j + 1], tokidc[:, i:i + 1])
                    last = v.tensor_copy(pairsA[:, 2 * j + 1:2 * j + 2], w4[:, j:j + 1])
            return last
        roundB("vector", f)

        def f(v):
            last = v.tensor_copy(desti32[:], destf[:])
            return last
        roundB("vector", f)
        roundB("vector", lambda v: v.tensor_copy(gidx32[:], gidxf[:]))

        roundB("vector", lambda v: v.memset(rscr2[:, 0:258], 0.0))

        def fd(sync):
            sync.wait_ge(rB, RB[0])
            sync.dma_start(out=gbuf_my[0:16384, :].rearrange("(p n) c -> p (n c)", p=128),
                           in_=rscr2[:, 0:256]).then_inc(rB, 16)
            sync.dma_start(out=gbuf_my[16384:16400, :], in_=rscr2[0:16, 256:258]).then_inc(rB, 16)
            RB[0] += 32
        rawB("sync", fd)

        def fg(gp):
            gp.wait_ge(rB, RB[0])
            qbase = QG[0]
            for j in range(32):
                if j:
                    gp.wait_ge(qG, qbase + j * 16)
                gp.indirect_dma_start(
                    gbuf_my[:], IndirectOffsetOnAxis(desti32[:, j:j + 1], 0),
                    pairsA[:, 2 * j:2 * j + 2], None,
                ).then_inc(qG, 16)
            QG[0] += 32 * 16
            gp.engine_nop().then_inc(rB, 1)
            RB[0] += 1
        rawB("gpsimd", fg)

        QG_SCAT = [0]

        def fg(gp):
            gp.wait_ge(rB, RB[0])
            gp.wait_ge(qG, QG_SCAT[0])
            gp.collective_compute("AllReduce", ALU.add,
                                  replica_groups=[list(range(NCORE))],
                                  ins=[gbuf_my.opt()], outs=[gbuf_all.opt()]).then_inc(rB, 1)
            RB[0] += 1
        def thunk_ar(fg=fg):
            QG_SCAT[0] = QG[0]
            block.gpsimd(fg)
        BQ.append(thunk_ar)

        def fd(sync):
            sync.wait_ge(rB, RB[0])
            sync.dma_start(out=gbsb[:, 0:2050],
                           in_=gbuf_all.rearrange("(e s) c -> e (s c)", e=16)).then_inc(rB, 16)
            RB[0] += 16
        rawB("sync", fd)

        def f(t):
            for j in range(4):
                last = t.matmul(p4[j][0:2, 0:512], eselt[:], gbsb[:, j * 512:(j + 1) * 512],
                                start=True, stop=True)
            return last
        roundB("tensor", f)

        def f(v):
            for j in range(4):
                last = v.tensor_copy(myg[:, j * 512:(j + 1) * 512], p4[j][0:2, 0:512])
            return last
        roundB("vector", f)

        roundB("tensor", lambda t: t.matmul(p4[0][0:2, 0:2], eselt[:], gbsb[:, 2048:2050],
                                            start=True, stop=True))
        roundB("vector", lambda v: v.tensor_copy(myg[:, 2048:2050], p4[0][0:2, 0:2]))

        def f(v):
            mygv = myg[:, 0:2050].rearrange("p (s c) -> p s c", c=2)
            v.tensor_copy(rscr2[0:2, 0:1025], mygv[:, :, 0])
            last = v.tensor_copy(gbsb[0:2, 0:1025], mygv[:, :, 1])
            return last
        roundB("vector", f)

        def f(t):
            for si in range(8):
                pt, off = p4[si % 4], (si // 4) * 4
                t.transpose(pt[:, off:off + 2], rscr2[0:2, si * 128:(si + 1) * 128],
                            cf[0:2, IDE:IDE + 2])
                last = t.transpose(pt[:, off + 2:off + 4],
                                   gbsb[0:2, si * 128:(si + 1) * 128],
                                   cf[0:2, IDE:IDE + 2])
            return last
        roundB("tensor", f)

        def f(v):
            for si in range(8):
                pt, off = p4[si % 4], (si // 4) * 4
                v.tensor_copy(idcols[:, si * 2:si * 2 + 2], pt[:, off:off + 2])
                last = v.tensor_copy(wcols[:, si * 2:si * 2 + 2], pt[:, off + 2:off + 4])
            return last
        roundB("vector", f)

        def f(v):
            for j in range(16):
                last = v.tensor_scalar(rscr[:, j:j + 1], idcols[:, j:j + 1], 0.0, BIGF,
                                       ALU.is_equal, ALU.mult)
            return last
        roundB("vector", f)

        def f(v):
            for j in range(16):
                last = v.tensor_tensor(rscr[:, 16 + j:17 + j], rscr[:, j:j + 1],
                                       idcols[:, j:j + 1], ALU.add)
            return last
        roundB("vector", f)

        def f(v):
            for j in range(16):
                last = v.tensor_scalar_add(hidxf[:, j:j + 1], rscr[:, 16 + j:17 + j], -1.0)
            return last
        roundB("vector", f)
        roundB("vector", lambda v: v.tensor_copy(hidx32[:], hidxf[:]))

        NB_TOTAL = len(BQ)

        scope("P6_shared")
        swgb = [aab(16 * 128) for _ in range(2)]
        swub = [aab(16 * 128) for _ in range(2)]
        smid = aab(NSB * 1024)
        swdb = [aab(NSB * 512) for _ in range(2)]
        P6_END = _aoff[0]

        FF_CNT = [0]
        FT_CNT = [0]

        def p6_dma(isb, par):
            sem = sFF if par == 0 else sFT
            mmw = [0]
            def fd(gp, isb=isb, par=par, sem=sem):
                gp.wait_ge(rq, RN[0])
                if mmw[0]:
                    gp.wait_ge(sMM, mmw[0])
                gp.dma_start(out=swgb[par].rearrange("p (c n) -> p c n", n=128),
                             in_=swg.rearrange("(c p) n -> p c n", p=128)[:, :, isb * 128:(isb + 1) * 128]
                             ).then_inc(sem, 16)
                gp.dma_start(out=swub[par].rearrange("p (c n) -> p c n", n=128),
                             in_=swu.rearrange("(c p) n -> p c n", p=128)[:, :, isb * 128:(isb + 1) * 128]
                             ).then_inc(sem, 16)
                gp.engine_nop().then_inc(rq, 1)
                RN[0] += 1
            mmw[0] = MMC[0]
            block.gpsimd(fd)
            (FF_CNT if par == 0 else FT_CNT)[0] += 32

        sMM = sFV
        sSI = sFS
        sMD = sFQ
        MMC, SIC, MDC = [0], [0], [0]
        MD_OF_SILU = []
        RQ_H6 = RN[0]

        def p6_mmh(isb, half):
            par = isb % 2
            sem = sFF if par == 0 else sFT
            wv = (FF_CNT if par == 0 else FT_CNT)[0]
            r = MMC[0]
            def f(t, isb=isb, half=half, par=par, sem=sem, wv=wv, r=r):
                if r == 0:
                    t.wait_ge(rq, RQ_H6)
                t.wait_ge(sem, wv)
                if r >= 2:
                    t.wait_ge(sMD, r - 1)
                for hb in range(NHB):
                    t.matmul(pw[0][:, (r % 2) * 512:(r % 2 + 1) * 512],
                             swgb[par][:, hb * 128:(hb + 1) * 128],
                             h_fmb[:, hb * TOWN + half * 512:hb * TOWN + (half + 1) * 512],
                             start=(hb == 0), stop=(hb == NHB - 1))
                    mmu = t.matmul(pw[1][:, (r % 2) * 512:(r % 2 + 1) * 512],
                                   swub[par][:, hb * 128:(hb + 1) * 128],
                                   h_fmb[:, hb * TOWN + half * 512:hb * TOWN + (half + 1) * 512],
                                   start=(hb == 0), stop=(hb == NHB - 1))
                mmu.then_inc(sMM, 1)
            block.tensor(f)
            MMC[0] += 1
            return r

        def gu_epih(rmm, dst_mid, extra_rq=None):
            rs = SIC[0]
            war = MD_OF_SILU[rs - 2] + 1 if rs >= 2 else 0
            hh = rmm % 2
            def fs(sc, rmm=rmm, rs=rs, war=war, hh=hh):
                if extra_rq is not None:
                    sc.wait_ge(rq, extra_rq)
                sc.wait_ge(sMM, rmm + 1)
                if war:
                    sc.wait_ge(sMD, war)
                sc.activation(scr2[:, hh * 512:(hh + 1) * 512],
                              pw[0][:, hh * 512:(hh + 1) * 512],
                              AF.Silu).then_inc(sSI, 1)
                for _ in range(6):
                    sc.nop()
            block.scalar(fs)
            SIC[0] += 1

            def fv(v, rs=rs, hh=hh, dst_mid=dst_mid):
                v.wait_ge(sSI, rs + 1)
                v.tensor_tensor(dst_mid, scr2[:, hh * 512:(hh + 1) * 512],
                                pw[1][:, hh * 512:(hh + 1) * 512],
                                ALU.mult).then_inc(sMD, 1)
                for _ in range(6):
                    v.engine_nop()
            block.vector(fv)
            MD_OF_SILU.append(MDC[0])
            MDC[0] += 1

        def p6_epih(isb, half):
            gu_epih(2 * isb + half,
                    smid[:, (2 * isb + half) * 512:(2 * isb + half + 1) * 512])

        B_SCHED = [7, 7, 7, 7, 8, 5, 0, 0, 3, 2, 8, 2, 11, 3, 1] + [0] * (NSB - 15)

        p6_dma(0, 0)
        p6_dma(1, 1)
        emitB(5)
        for isb in range(NSB):
            p6_mmh(isb, 0)
            p6_mmh(isb, 1)
            if isb + 2 < NSB:
                p6_dma(isb + 2, isb % 2)
            p6_epih(isb, 0)
            p6_epih(isb, 1)
            emitB(B_SCHED[isb])

        if debug:
            def fjs(v):
                v.wait_ge(sMD, MDC[0])
                return v.engine_nop()
            round_("vector", fjs)
        tap("smid", smid[:], 128, BF16)

        def p6d_dma(hb2, par, first=False):
            sem = sFF if par == 0 else sFT
            mmw = 0 if first else MMC[0]
            def fd(gp, hb2=hb2, sem=sem, mmw=mmw):
                gp.wait_ge(rq, RN[0])
                if mmw:
                    gp.wait_ge(sMM, mmw)
                for isb in range(NSB):
                    gp.dma_start(out=swdb[par][:, isb * 512:(isb + 1) * 512],
                                 in_=swd[isb * 128:(isb + 1) * 128, hb2 * 512:(hb2 + 1) * 512]
                                 ).then_inc(sem, 16)
                gp.engine_nop().then_inc(rq, 1)
                RN[0] += 1
            block.gpsimd(fd)
            (FF_CNT if par == 0 else FT_CNT)[0] += NSB * 16

        def down_rounds(src_mid, src_wd, nblk, sem, wv, smid_rdy, store_to, wcol=None):
            r = MMC[0]
            def f(t, r=r, sem=sem, wv=wv, smid_rdy=smid_rdy):
                t.wait_ge(sem, wv)
                t.wait_ge(sMD, max(smid_rdy, r - 1))
                pt = pw[r % 2]
                for ib in range(nblk):
                    mmv = t.matmul(pt[:, 0:512], src_mid(ib), src_wd(ib),
                                   start=(ib == 0), stop=(ib == nblk - 1))
                mmv.then_inc(sMM, 1)
            block.tensor(f)
            MMC[0] += 1

            r2 = MDC[0]
            rqe = RN[0]
            def fv(v, r2=r2, rqe=rqe):
                v.wait_ge(sMM, r2 + 1)
                v.wait_ge(rq, rqe)
                if wcol is None:
                    out = scr2.bitcast(BF16)[:, (r2 % 2) * 512:(r2 % 2 + 1) * 512]
                    v.tensor_copy(out, pw[r2 % 2][:, 0:512]).then_inc(sMD, 1)
                else:
                    out = eoutst[:, (r2 % 2) * 512:(r2 % 2 + 1) * 512]
                    v.tensor_scalar(out, pw[r2 % 2][:, 0:512], wcol, 0.0,
                                    ALU.mult, ALU.add).then_inc(sMD, 1)
                for _ in range(6):
                    v.engine_nop()
            block.vector(fv)
            MDC[0] += 1

            def fd2(sync, r2=r2):
                sync.wait_ge(sMD, r2 + 1)
                if wcol is None:
                    src = scr2.bitcast(BF16)[:, (r2 % 2) * 512:(r2 % 2 + 1) * 512]
                else:
                    src = eoutst[:, (r2 % 2) * 512:(r2 % 2 + 1) * 512]
                sync.dma_start(out=store_to, in_=src).then_inc(rq, 16)
                RN[0] += 16
            block.sync(fd2)

        p6d_dma(0, 0, first=True)
        p6d_dma(1, 1, first=True)
        SMID_RDY = MDC[0]
        for hb2 in range(4):
            par = hb2 % 2
            sem = sFF if par == 0 else sFT
            wv = (FF_CNT if par == 0 else FT_CNT)[0]
            for tb in range(8):
                down_rounds(
                    (lambda ib, tb=tb: smid[:, ib * 1024 + tb * 128:ib * 1024 + (tb + 1) * 128]),
                    (lambda ib, par=par: swdb[par][:, ib * 512:(ib + 1) * 512]),
                    NSB, sem, wv, SMID_RDY,
                    ysh_dram[tb * 128:(tb + 1) * 128, hb2 * 512:(hb2 + 1) * 512])
            if hb2 + 2 < 4:
                p6d_dma(hb2 + 2, par)
            if hb2 >= 2:
                emitB(6)
        emitB(len(BQ))
        RBF = RB[0]

        if debug:
            def fjoin(sync):
                sync.wait_ge(rB, RBF)
            block.sync(fjoin)
            tap("lgfm", lgfm[:], 16)
            tap("h_fmb", h_fmb[:], 128, BF16)
            tap("lgtm", lgtm[:])
            tap("vals", vals[:])
            tap("idx4", idx4[:])
            tap("w4", w4[:])
            tap("maskg", maskg[:])
            tap("cntrow", cntrow[:], 1)
            tap("carrow", carrow[:], 1)
            tap("rankg", rankg[:])
            tap("slot4", slot4[:])
            tap("destf", destf[:])
            tap("gidxf", gidxf[:])
            tap("gbsb", gbsb[:], 16)
            tap("myg", myg[:], 2)
            tap("idcols", idcols[:])
            tap("wcols", wcols[:])
            tap("hidxf", hidxf[:])
            tap("pairsA", pairsA[:])

        scope("P7_moe")
        _aoff[0] = FFN_MARK
        xg_fm = aab(NHB * 1024)
        xg_tm = aab(2 * HD)
        wgb = [aab(16 * 128) for _ in range(2)]
        wub = [aab(16 * 128) for _ in range(2)]
        wdbuf = [aab(NIB * 512) for _ in range(2)]
        midb = [aab(NIB * 1024) for _ in range(2)]
        eoutst = rscr2.bitcast(BF16)[:, 0:1024]

        for e01 in range(2):
            mm_e0 = MMC[0]
            for si in range(8):
                def f(v, si=si, e01=e01, mm_e0=mm_e0):
                    if e01 == 1 and si == 0:
                        v.wait_ge(sMM, mm_e0)
                    return v.memset(xg_tm[:, (si % 2) * HD:(si % 2 + 1) * HD], 0.0)
                round_("vector", f)

                def fg(gp, e01=e01, si=si):
                    gp.wait_ge(rq, RN[0])
                    gp.wait_ge(sCC, 1)
                    if e01 == 0 and si == 0:
                        gp.wait_ge(rB, RBF)
                    breg = gp.to_reg(NCORE * TOWN - 1)
                    gp.indirect_dma_start(
                        xg_tm[:, (si % 2) * HD:(si % 2 + 1) * HD], None, h_full[:],
                        IndirectOffsetOnAxis(hidx32[:, si * 2 + e01:si * 2 + e01 + 1], 0),
                        bounds_check=breg, oob_is_err=False,
                    ).then_inc(qG, 16)
                    QG[0] += 16
                    gp.engine_nop().then_inc(rq, 1)
                    RN[0] += 1
                block.gpsimd(fg)
                QGXG = QG[0]

                def f(t, si=si, e01=e01, QGXG=QGXG):
                    t.wait_ge(qG, QGXG)
                    if e01 == 0 and si == 0:
                        t.wait_ge(rB, RBF)
                    for hb in range(NHB):
                        g = hb // 4
                        last = t.transpose(p4[g].bitcast(BF16)[:, (hb % 4) * 256:(hb % 4) * 256 + 128],
                                           xg_tm[:, (si % 2) * HD + hb * 128:(si % 2) * HD + (hb + 1) * 128],
                                           cb[:, 0:128])
                    return last
                round_("tensor", f)

                def f(v, si=si):
                    for g in range(4):
                        dst = xg_fm.rearrange("p (hb t) -> p hb t", t=1024)[:, 4 * g:4 * g + 4,
                                                                           si * 128:(si + 1) * 128]
                        src = p4[g].bitcast(BF16).rearrange("p (k t) -> p k t", t=256)[:, :, 0:128]
                        last = v.tensor_copy(dst, src)
                    return last
                round_("vector", f)

            if e01 == 0:
                tap("xg_fm_e0", xg_fm[:], 128, BF16)

            def p7_dma(ib, par, e01=e01):
                sem = sFF if par == 0 else sFT
                mmw = MMC[0]
                def fd(gp, ib=ib, par=par, sem=sem, mmw=mmw):
                    gp.wait_ge(rq, RN[0])
                    if mmw:
                        gp.wait_ge(sMM, mmw)
                    gp.dma_start(out=wgb[par].rearrange("p (c n) -> p c n", n=128),
                                 in_=wg_my[e01].rearrange("(c p) n -> p c n", p=128)[:, :, ib * 128:(ib + 1) * 128]
                                 ).then_inc(sem, 16)
                    gp.dma_start(out=wub[par].rearrange("p (c n) -> p c n", n=128),
                                 in_=wu_my[e01].rearrange("(c p) n -> p c n", p=128)[:, :, ib * 128:(ib + 1) * 128]
                                 ).then_inc(sem, 16)
                    gp.engine_nop().then_inc(rq, 1)
                    RN[0] += 1
                block.gpsimd(fd)
                (FF_CNT if par == 0 else FT_CNT)[0] += 32

            def p7_mmh(ib, half, e01=e01):
                par = ib % 2
                sem = sFF if par == 0 else sFT
                wv = (FF_CNT if par == 0 else FT_CNT)[0]
                r = MMC[0]
                rqx = RN[0]
                mdw = MDC[0]
                def f(t, ib=ib, half=half, par=par, sem=sem, wv=wv, r=r, rqx=rqx,
                      mdw=mdw, e01=e01):
                    if ib == 0 and half == 0:
                        t.wait_ge(rq, rqx)
                        if e01 == 0:
                            t.wait_ge(sMD, mdw)
                    t.wait_ge(sem, wv)
                    if r >= 2:
                        t.wait_ge(sMD, r - 1)
                    for hb in range(NHB):
                        t.matmul(pw[0][:, (r % 2) * 512:(r % 2 + 1) * 512],
                                 wgb[par][:, hb * 128:(hb + 1) * 128],
                                 xg_fm[:, hb * 1024 + half * 512:hb * 1024 + (half + 1) * 512],
                                 start=(hb == 0), stop=(hb == NHB - 1))
                        mmu = t.matmul(pw[1][:, (r % 2) * 512:(r % 2 + 1) * 512],
                                       wub[par][:, hb * 128:(hb + 1) * 128],
                                       xg_fm[:, hb * 1024 + half * 512:hb * 1024 + (half + 1) * 512],
                                       start=(hb == 0), stop=(hb == NHB - 1))
                    mmu.then_inc(sMM, 1)
                block.tensor(f)
                MMC[0] += 1
                return r

            p7_dma(0, 0)
            p7_dma(1, 1)
            for ib in range(NIB):
                ra = p7_mmh(ib, 0)
                rb_ = p7_mmh(ib, 1)
                if ib + 2 < NIB:
                    p7_dma(ib + 2, ib % 2)
                gu_epih(ra, midb[e01][:, ib * 1024:ib * 1024 + 512],
                        extra_rq=(RN[0] if (e01 == 0 and ib == 0) else None))
                gu_epih(rb_, midb[e01][:, ib * 1024 + 512:ib * 1024 + 1024])

        def p7d_dma(s, first=False):
            hb2, e01, par = s >> 1, s & 1, s % 2
            sem = sFF if par == 0 else sFT
            mmw = 0 if first else MMC[0]
            def fd(gp, hb2=hb2, e01=e01, par=par, sem=sem, mmw=mmw):
                gp.wait_ge(rq, RN[0])
                if mmw:
                    gp.wait_ge(sMM, mmw)
                for ib in range(NIB):
                    gp.dma_start(out=wdbuf[par][:, ib * 512:(ib + 1) * 512],
                                 in_=wd_my[e01][ib * 128:(ib + 1) * 128, hb2 * 512:(hb2 + 1) * 512]
                                 ).then_inc(sem, 16)
                gp.engine_nop().then_inc(rq, 1)
                RN[0] += 1
            block.gpsimd(fd)
            (FF_CNT if par == 0 else FT_CNT)[0] += NIB * 16

        p7d_dma(0, first=True)
        p7d_dma(1, first=True)
        MID_RDY = MDC[0]
        for s in range(8):
            hb2, e01, par = s >> 1, s & 1, s % 2
            sem = sFF if par == 0 else sFT
            wv = (FF_CNT if par == 0 else FT_CNT)[0]
            for sb in range(8):
                down_rounds(
                    (lambda ib, e01=e01, sb=sb: midb[e01][:, ib * 1024 + sb * 128:ib * 1024 + (sb + 1) * 128]),
                    (lambda ib, par=par: wdbuf[par][:, ib * 512:(ib + 1) * 512]),
                    NIB, sem, wv, MID_RDY,
                    eout_myh[hb2][e01 * CAP + sb * 128:e01 * CAP + (sb + 1) * 128, :],
                    wcol=wcols[:, sb * 2 + e01:sb * 2 + e01 + 1])
            if s + 2 < 8:
                p7d_dma(s + 2)
            if e01 == 1:
                def fg(gp, hb2=hb2):
                    gp.wait_ge(rq, RN[0])
                    gp.collective_compute("AllGather", ALU.bypass,
                                          replica_groups=[list(range(NCORE))],
                                          ins=[eout_myh[hb2].opt()],
                                          outs=[eout_allh[hb2].opt()]).then_inc(sCC, 1)
                block.gpsimd(fg)

        def fg_drain(gp):
            gp.collective_compute("AllGather", ALU.bypass,
                                  replica_groups=[list(range(NCORE))],
                                  ins=[cnt_my.opt()],
                                  outs=[cnt_all.opt()]).then_inc(sCC, 1)
        block.gpsimd(fg_drain)

        scope("P8_final")
        _aoff[0] = FFN_MARK
        y4f = aa(8 * HD)
        y4s = aa(4 * HD)
        y4sb = y4s.bitcast(BF16)
        stats = aa(8 * 16)
        htile = y4s[:, 0:2 * HD]
        ysht = y4sb[:, 4 * HD:6 * HD]

        def f(v):
            for q in range(4):
                last = v.memset(y4f[:, q * HD:(q + 1) * HD], 0.0)
            return last
        round_("vector", f)

        def f(v):
            for q in range(4, 8):
                last = v.memset(y4f[:, q * HD:(q + 1) * HD], 0.0)
            return last
        round_("vector", f)

        for t0 in (0, 2, 4, 6):
            def fd(sync, t0=t0):
                sync.wait_ge(rq, RN[0])
                for j in range(2):
                    tt = t0 + j
                    sync.dma_start(out=htile[:, j * HD:(j + 1) * HD],
                                   in_=htm_f32[tt * 128:(tt + 1) * 128, :]).then_inc(rq, 16)
                    sync.dma_start(out=ysht[:, j * HD:(j + 1) * HD],
                                   in_=ysh_dram[tt * 128:(tt + 1) * 128, :]).then_inc(rq, 16)
                RN[0] += 64
            block.sync(fd)

            if t0 == 0:
                tap("ysht0", ysht[:, 0:HD], 128, BF16)
                tap("htile0", htile[:, 0:HD])

            def f(v, t0=t0):
                v.tensor_tensor(y4f[:, t0 * HD:(t0 + 1) * HD],
                                y4f[:, t0 * HD:(t0 + 1) * HD], htile[:, 0:HD], ALU.add)
                return v.tensor_tensor(y4f[:, (t0 + 1) * HD:(t0 + 2) * HD],
                                       y4f[:, (t0 + 1) * HD:(t0 + 2) * HD],
                                       htile[:, HD:2 * HD], ALU.add)
            round_("vector", f)

            def f(v, t0=t0):
                v.tensor_tensor(y4f[:, t0 * HD:(t0 + 1) * HD],
                                y4f[:, t0 * HD:(t0 + 1) * HD], ysht[:, 0:HD], ALU.add)
                return v.tensor_tensor(y4f[:, (t0 + 1) * HD:(t0 + 2) * HD],
                                       y4f[:, (t0 + 1) * HD:(t0 + 2) * HD],
                                       ysht[:, HD:2 * HD], ALU.add)
            round_("vector", f)

        round_("vector", lambda v: v.memset(y4s[:, 0:2 * HD], 0.0))
        round_("vector", lambda v: v.memset(y4s[:, 2 * HD:4 * HD], 0.0))

        for h in range(4):
            def fg(gp, h=h):
                gp.wait_ge(rq, RN[0])
                gp.wait_ge(sCC, 6)
                breg = gp.to_reg(EXP * CAP - 1)
                for tt in range(8):
                    for k in range(4):
                        s = tt * 4 + k
                        gp.indirect_dma_start(
                            y4sb[:, s * 512:(s + 1) * 512],
                            None, eout_allh[h][:],
                            IndirectOffsetOnAxis(gidx32[:, tt * 4 + k:tt * 4 + k + 1], 0),
                            bounds_check=breg, oob_is_err=False,
                        ).then_inc(qG, 16)
                QG[0] += 32 * 16
                gp.engine_nop().then_inc(rq, 1)
                RN[0] += 1
            block.gpsimd(fg)

            wv = QG[0]
            def fa(v, h=h, wv=wv):
                v.wait_ge(qG, wv)
                for k in range(4):
                    for tt in range(8):
                        s = tt * 4 + k
                        v.tensor_tensor(
                            y4f[:, tt * HD + h * 512:tt * HD + (h + 1) * 512],
                            y4f[:, tt * HD + h * 512:tt * HD + (h + 1) * 512],
                            y4sb[:, s * 512:(s + 1) * 512],
                            ALU.add)
                for q in range(4):
                    last = v.memset(y4s[:, q * HD:(q + 1) * HD], 0.0)
                return last
            round_("vector", fa)

        if debug:
            tap("y4a0", y4f[:, 0:HD])

        def f(v):
            for tt in range(8):
                last = v.tensor_reduce(stats[:, tt * 16 + 0:tt * 16 + 1],
                                       y4f[:, tt * HD:(tt + 1) * HD], AX.X, ALU.add)
            return last
        round_("vector", f)

        def f2(sc):
            for tt in range(4):
                last = sc.activation(y4s[:, tt * HD:(tt + 1) * HD],
                                     y4f[:, tt * HD:(tt + 1) * HD], AF.Square,
                                     accum_out=stats[:, tt * 16 + 1:tt * 16 + 2])
            return last
        round_("scalar", f2)

        def f2(sc):
            for tt in range(4, 8):
                last = sc.activation(y4s[:, (tt - 4) * HD:(tt - 3) * HD],
                                     y4f[:, tt * HD:(tt + 1) * HD], AF.Square,
                                     accum_out=stats[:, tt * 16 + 1:tt * 16 + 2])
            return last
        round_("scalar", f2)

        def f(v):
            for tt in range(8):
                v.tensor_scalar_mul(stats[:, tt * 16 + 2:tt * 16 + 3],
                                    stats[:, tt * 16 + 0:tt * 16 + 1], 1.0 / HD)
                last = v.tensor_scalar_mul(stats[:, tt * 16 + 3:tt * 16 + 4],
                                           stats[:, tt * 16 + 1:tt * 16 + 2], 1.0 / HD)
            return last
        round_("vector", f)

        def f(v):
            for tt in range(8):
                last = v.tensor_tensor(stats[:, tt * 16 + 4:tt * 16 + 5],
                                       stats[:, tt * 16 + 2:tt * 16 + 3],
                                       stats[:, tt * 16 + 2:tt * 16 + 3], ALU.mult)
            return last
        round_("vector", f)

        def f(v):
            for tt in range(8):
                last = v.tensor_tensor(stats[:, tt * 16 + 5:tt * 16 + 6],
                                       stats[:, tt * 16 + 3:tt * 16 + 4],
                                       stats[:, tt * 16 + 4:tt * 16 + 5], ALU.subtract)
            return last
        round_("vector", f)

        def f(sc):
            for tt in range(8):
                last = sc.activation(stats[:, tt * 16 + 6:tt * 16 + 7],
                                     stats[:, tt * 16 + 5:tt * 16 + 6], AF.Ln, bias=LN_EPS)
            return last
        round_("scalar", f)

        def f(sc):
            for tt in range(8):
                last = sc.activation(stats[:, tt * 16 + 7:tt * 16 + 8],
                                     stats[:, tt * 16 + 6:tt * 16 + 7], AF.Exp, scale=-0.5)
            return last
        round_("scalar", f)

        def f(v):
            for tt in range(8):
                last = v.tensor_scalar(y4f[:, tt * HD:(tt + 1) * HD],
                                       y4f[:, tt * HD:(tt + 1) * HD],
                                       stats[:, tt * 16 + 2:tt * 16 + 3],
                                       stats[:, tt * 16 + 7:tt * 16 + 8],
                                       ALU.subtract, ALU.mult)
            return last
        round_("vector", f)

        for _ in range(2):
            round_("vector", lambda v: v.engine_nop())

        def fd(sync):
            sync.wait_ge(rq, RN[0])
            for tt in range(8):
                sync.dma_start(out=out_own[tt * 128:(tt + 1) * 128, :],
                               in_=y4f[:, tt * HD:(tt + 1) * HD]).then_inc(rq, 16)
            RN[0] += 128
        block.sync(fd)

        tap("statsall", stats[:])

        def fin(sync):
            sync.wait_ge(rq, RN[0])
        block.sync(fin)
        scope(None)

    return nc


def _consts():
    c = {}
    c["ident_f"] = np.eye(128, dtype=np.float32)
    c["ident_b"] = np.eye(128, dtype=np.float32)
    c["m_triuinc"] = np.triu(np.ones((128, 128), np.float32))
    c["m_striu"] = np.triu(np.ones((128, 128), np.float32), 1)
    c["m_striun"] = -np.triu(np.ones((128, 128), np.float32), 1)
    c["m_striln"] = -np.tril(np.ones((128, 128), np.float32), -1)
    c["onescol"] = np.ones((128, 1), np.float32)
    e = np.zeros((128, 1), np.float32); e[127, 0] = 1.0
    c["e127c"] = e
    c["iota128"] = np.arange(128, dtype=np.float32).reshape(128, 1)
    c["iota16g"] = np.tile(np.arange(16, dtype=np.float32), (128, 1))
    c["iota64c"] = np.arange(64, dtype=np.float32).reshape(64, 1)
    c["fconst_d"] = np.tile(np.array([[1e-6, 1e-5, 1e-8]], np.float32), (128, 1))
    return c


def host_inmaps(inputs):
    import ml_dtypes
    x = np.asarray(inputs["L0_attn_x"])[0]
    Wk = np.asarray(inputs["L0_attn_Wk"]); Wv = np.asarray(inputs["L0_attn_Wv"])
    Wq = np.asarray(inputs["L0_attn_Wq"]); Wa = np.asarray(inputs["L0_attn_Wa_w"])
    wab = np.asarray(inputs["L0_attn_Wa_b"]).reshape(1, D)
    lam = np.asarray(inputs["L0_attn_lam"]).reshape(1, D)
    normw = np.asarray(inputs["L0_attn_norm_w"]).reshape(1, D)
    Wo = np.asarray(inputs["L0_attn_Wo"])
    gw = np.asarray(inputs["L0_ffn_gate_w"])
    w_gate = np.asarray(inputs["L0_ffn_w_gate"]); w_up = np.asarray(inputs["L0_ffn_w_up"])
    w_down = np.asarray(inputs["L0_ffn_w_down"])
    swg = np.ascontiguousarray(np.asarray(inputs["L0_ffn_sw_gate"]))
    swu = np.ascontiguousarray(np.asarray(inputs["L0_ffn_sw_up"]))
    swd = np.ascontiguousarray(np.asarray(inputs["L0_ffn_sw_down"]))

    wcatT = np.ascontiguousarray(np.concatenate([Wk, Wv, Wq, Wa], 0).T)
    woT = np.ascontiguousarray(Wo.T)
    gwT = np.ascontiguousarray(gw.T)
    rowp = np.ascontiguousarray(np.concatenate(
        [lam, wab, normw, np.zeros((1, D), np.float32)], 1))
    consts = _consts()

    in_maps = []
    for c in range(NCORE):
        pm = np.zeros((8, 1), np.float32); pm[:c] = 1.0
        sel = np.zeros((16, 2), np.float32)
        sel[2 * c, 0] = 1.0; sel[2 * c + 1, 1] = 1.0
        cv = np.array([[c * TOWN, c * NCH, c, 0]], np.float32)
        m = dict(
            x_own=np.ascontiguousarray(x[c * TOWN:(c + 1) * TOWN]),
            wcatT=wcatT, woT=woT, gwT=gwT,
            wg_my=np.ascontiguousarray(w_gate[2 * c:2 * c + 2]),
            wu_my=np.ascontiguousarray(w_up[2 * c:2 * c + 2]),
            wd_my=np.ascontiguousarray(w_down[2 * c:2 * c + 2]),
            swg=swg, swu=swu, swd=swd,
            rowp=rowp, premask=pm, esel=sel, cvec=cv,
            **consts,
        )
        m["ident_b"] = np.eye(128).astype(ml_dtypes.bfloat16)
        in_maps.append(m)
    return in_maps


def kernel(**inputs):
    in_maps = host_inmaps(inputs)
    nc = build_nc(debug=False)
    res = run_bass_kernel_spmd(nc, in_maps, core_ids=list(range(NCORE)))
    out = np.concatenate([r["out_own"] for r in res.results], 0)
    return out.reshape(1, NCORE * TOWN, HD).astype(np.float32)


# version 36
# speedup vs baseline: 1.0009x; 1.0009x over previous
import contextlib
import numpy as np
import concourse.bass as bass
import concourse.mybir as mybir
from concourse.bass_utils import run_bass_kernel_spmd
from concourse.bass_core_rust import IndirectOffsetOnAxis

F32 = mybir.dt.float32
BF16 = mybir.dt.bfloat16
I32 = mybir.dt.int32
AF = mybir.ActivationFunctionType
ALU = mybir.AluOpType
AX = mybir.AxisListType

NCORE = 8
TOWN = 1024
NCH = 8
L = 128
D = 64
HD = 2048
NHB = HD // 128
EXP = 16
CAP = 1024
IE = 1408
NIB = IE // 128
ISH = 2816
NSB = ISH // 128
RMS_EPS = 1e-6
LN_EPS = 1e-5
SURSC = 1.0 / (1.0 + 1e-6)
BIGF = 1.0e6


def build_nc(debug=False, poison=False):
    nc = bass.Bass("TRN2", target_bir_lowering=False)
    es = contextlib.ExitStack()

    dr = lambda n, s, d=F32, k="ExternalInput": nc.dram_tensor(n, s, d, kind=k).ap()
    x_own = dr("x_own", [TOWN, HD])
    wcatT = dr("wcatT", [HD, 256])
    woT = dr("woT", [D, HD])
    gwT = dr("gwT", [HD, EXP])
    wg_my = dr("wg_my", [2, HD, IE])
    wu_my = dr("wu_my", [2, HD, IE])
    wd_my = dr("wd_my", [2, IE, HD])
    swg = dr("swg", [HD, ISH])
    swu = dr("swu", [HD, ISH])
    swd = dr("swd", [ISH, HD])
    rowp = dr("rowp", [1, 4 * D])
    ident_f = dr("ident_f", [128, 128])
    ident_b = dr("ident_b", [128, 128], BF16)
    m_triuinc = dr("m_triuinc", [128, 128])
    m_striu = dr("m_striu", [128, 128])
    m_striun = dr("m_striun", [128, 128])
    m_striln = dr("m_striln", [128, 128])
    onescol = dr("onescol", [128, 1])
    e127c = dr("e127c", [128, 1])
    iota128 = dr("iota128", [128, 1])
    iota16g = dr("iota16g", [128, 16])
    iota64c = dr("iota64c", [64, 1])
    premask = dr("premask", [8, 1])
    esel = dr("esel", [16, 2])
    cvec = dr("cvec", [1, 4])
    fconst_d = dr("fconst_d", [128, 3])
    out_own = dr("out_own", [TOWN, HD], F32, "ExternalOutput")
    dbg = {}
    if debug:
        dbg["prep"] = dr("dbg_prep", [TOWN, 256], F32, "ExternalOutput")
        dbg["errs"] = dr("dbg_errs", [TOWN, 1], F32, "ExternalOutput")
        dbg["y"] = dr("dbg_y", [TOWN, D], F32, "ExternalOutput")
        dbg["h"] = dr("dbg_h", [TOWN, HD], F32, "ExternalOutput")
        dbg["rt"] = dr("dbg_rt", [TOWN, 12], F32, "ExternalOutput")
        dbg["gb"] = dr("dbg_gb", [EXP * 1025, 2], F32, "ExternalOutput")

    di = lambda n, s, d=F32: nc.dram_tensor(n, s, d, kind="Internal").ap()
    ds = lambda n, s, d=F32: nc.dram_tensor(n, s, d, kind="Internal",
                                            addr_space="Shared").ap()
    ab_my = di("ab_my", [NCH * D, 130])
    ab_allp = [ds(f"ab_all{p}", [NCORE * NCH * D, 130]) for p in range(2)]
    h0_dram = di("h0_dram", [64 * D, D])
    htm_bf_my = di("htm_bf_my", [TOWN, HD], BF16)
    htm_f32 = di("htm_f32", [TOWN, HD])
    ysh_dram = di("ysh_dram", [TOWN, HD], BF16)
    h_full = di("h_full", [NCORE * TOWN, HD], BF16)
    cnt_my = di("cnt_my", [1, EXP])
    cnt_all = ds("cnt_all", [NCORE, EXP])
    gbuf_my = di("gbuf_my", [EXP * 1025, 2])
    gbuf_all = ds("gbuf_all", [EXP * 1025, 2])
    eout_myh = [di(f"eout_myh{h}", [2 * CAP, 512], BF16) for h in range(4)]
    eout_allh = [di(f"eout_allh{h}", [EXP * CAP, 512], BF16) for h in range(4)]

    with es:
        sbA = lambda n, s, d=F32: es.enter_context(nc.sbuf_tensor(n, s, d))
        psA = lambda n, s, d=F32: es.enter_context(nc.psum_tensor(n, s, d))

        cf = sbA("cf", [128, 128 * 5])
        cb = sbA("cb", [128, 128], BF16)
        onesc = sbA("onesc", [128, 1])
        e127 = sbA("e127", [128, 1])
        iotac = sbA("iotac", [128, 1])
        iota16 = sbA("iota16", [128, 16])
        iota64 = sbA("iota64", [64, 1])
        prem = sbA("prem", [8, 1])
        eselt = sbA("eselt", [16, 2])
        cvt = sbA("cvt", [1, 4])
        rowpar = sbA("rowpar", [1, 4 * D])
        lnsig = sbA("lnsig", [1, D])
        bc64 = sbA("bc64", [128, 2 * D])
        fcon = sbA("fcon", [128, 3])
        gw_sb = sbA("gw_sb", [128, NHB * EXP])
        lgfm = sbA("lgfm", [16, TOWN])
        desti32 = sbA("desti32", [128, 32], I32)
        gidx32 = sbA("gidx32", [128, 32], I32)
        hidx32 = sbA("hidx32", [128, 16], I32)
        fmb = sbA("fmb", [128, NHB * TOWN], BF16)
        stS = sbA("stS", [128, NCH])
        errs = sbA("errs", [128, NCH])
        glc = sbA("glc", [64, NCH])
        idxg = sbA("idxg", [64, NCH], I32)
        scr = sbA("scr", [128, 1024])
        scr2 = sbA("scr2", [128, 2048])

        ARENA_COLS = 39300
        arena = sbA("arena", [128, ARENA_COLS])
        arena_bf = arena.bitcast(BF16)
        _aoff = [0]

        def aa(cols):
            s = arena[:, _aoff[0]:_aoff[0] + cols]
            _aoff[0] += cols
            assert _aoff[0] <= ARENA_COLS, f"arena overflow {_aoff[0]}"
            return s

        def aab(cols_bf):
            assert cols_bf % 2 == 0
            s = arena_bf[:, 2 * _aoff[0]:2 * _aoff[0] + cols_bf]
            _aoff[0] += cols_bf // 2
            return s

        def arena_reset():
            _aoff[0] = 0

        x_fm = aa(NHB * TOWN)
        x_fmb = fmb
        h_fmb = fmb
        MARK0 = _aoff[0]
        wcat_sb = aab(NHB * 256)
        xstg = aa(HD)
        proj = aa(NCH * 256)
        _aoff[0] = MARK0 + 6600
        stKV = aa(NCH * 128)
        stQt = aa(NCH * D)
        stQh = aa(NCH * D)
        stU0 = aa(NCH * D)
        stUc = aa(NCH * D)
        stG = aa(NCH * D)
        stLA = aa(NCH * D)
        fmQt = aa(NCH * L)[0:64, :]
        fmKt = aa(NCH * L)[0:64, :]
        fmQh = aa(NCH * L)[0:64, :]
        fmU = aa(NCH * L)[0:64, :]
        fmZ = aa(NCH * L)[0:64, :]
        solU = aa(NCH * L)
        solW = aa(NCH * L)
        solR = aa(NCH * 128)
        gP = aa(NCH * L)
        stM = aa(NCH * D)
        abATs = aa(NCH * D)[0:64, :]
        abBgs = aa(NCH * D)[0:64, :]
        h0my = aa(NCH * D)[0:64, :]
        SCAN_ARENA_END = _aoff[0]
        _aoff[0] = MARK0
        h0ping = aa(2 * D)[0:64, :]
        chABt = aa(2 * 130)[0:64, :]
        yattn = aa(NCH * D)
        ynfm = aab(TOWN)[0:64, :]
        wo_fm = aab(HD)[0:64, :]
        nwcol = aa(1)[0:64, :]
        rowa = aa(2048)[0:1, :]
        hstg_bf = aab(2 * HD)
        assert _aoff[0] <= MARK0 + 6600, f"overlay overflow {_aoff[0] - MARK0}"
        _aoff[0] = SCAN_ARENA_END

        p4 = [psA(f"p4_{i}", [128, 512]) for i in range(4)]
        pw = [psA(f"pw_{i}", [128, 1024]) for i in range(2)]

        def pslot(c):
            if c < 4:
                return p4[c], 0
            return pw[(c - 4) // 2], ((c - 4) % 2) * 512

        rq = es.enter_context(nc.semaphore("rq"))
        RN = [0]
        qG = es.enter_context(nc.semaphore("qG"))
        QG = [0]
        sXin = es.enter_context(nc.semaphore("sXin"))
        sTx = es.enter_context(nc.semaphore("sTx"))
        sVx = es.enter_context(nc.semaphore("sVx"))
        sFF = es.enter_context(nc.semaphore("sFF"))
        sFT = es.enter_context(nc.semaphore("sFT"))
        sFV = es.enter_context(nc.semaphore("sFV"))
        sFS = es.enter_context(nc.semaphore("sFS"))
        sFQ = es.enter_context(nc.semaphore("sFQ"))
        sEO = es.enter_context(nc.semaphore("sEO"))
        sCC = es.enter_context(nc.semaphore("sCC"))

        TAPS = {}

        nc.const_aps.register(F32, RMS_EPS, fcon[:, 0:1])
        nc.const_aps.register(F32, LN_EPS, fcon[:, 1:2])
        nc.const_aps.register(F32, 1e-8, fcon[:, 2:3])
        block = es.enter_context(nc.Block())
        IDE, TRI, SUU, SUN, SLN = 0, 128, 256, 384, 512

        def tap(name, ap, parts=128, dt=F32):
            if not debug:
                return
            cols = ap.shape[-1]
            t_d = nc.dram_tensor("tap_" + name, [parts, cols], dt, kind="ExternalOutput").ap()
            TAPS[name] = t_d

            def fdma(sync, ap=ap, t_d=t_d):
                sync.wait_ge(rq, RN[0])
                sync.dma_start(out=t_d[:], in_=ap, allow_slow_non_contiguous=True).then_inc(rq, 16)
                RN[0] += 16
            block.sync(fdma)

        def round_(engname, fn, inc=1):
            def body(eng):
                eng.wait_ge(rq, RN[0])
                last = fn(eng)
                last.then_inc(rq, inc)
                if engname == "vector":
                    for _ in range(6):
                        eng.engine_nop()
                elif engname == "scalar":
                    for _ in range(6):
                        eng.nop()
            getattr(block, engname)(body)
            RN[0] += inc

        if poison:
            PV = 1.0e38
            q4 = ARENA_COLS // 4
            for i in range(4):
                a0, a1 = i * q4, (ARENA_COLS if i == 3 else (i + 1) * q4)
                round_("vector", lambda v, a0=a0, a1=a1: v.memset(arena[:, a0:a1], PV))
            for tns in (scr, scr2, gw_sb, lgfm, stS, errs, glc, fcon, bc64, lnsig,
                        rowpar, cvt, eselt, prem, iota64, iota16, iotac, e127,
                        onesc, cf):
                round_("vector", lambda v, t_=tns: v.memset(t_[:], PV))
            round_("vector", lambda v: v.memset(cb[:], PV))
            round_("vector", lambda v: v.memset(fmb[:], PV))
            for it in (desti32, gidx32, hidx32, idxg):
                round_("vector", lambda v, t_=it: v.memset(t_[:], 1 << 30))
            for pt in p4 + pw:
                round_("vector", lambda v, t_=pt: v.memset(t_[:], PV))

            def poison_dram(sync):
                sync.wait_ge(rq, RN[0])
                n = 0
                s2b = scr2.bitcast(BF16)
                for t_, isbf in ((ab_my, 0), (ab_allp[0], 0), (ab_allp[1], 0),
                                 (h0_dram, 0), (htm_f32, 0),
                                 (cnt_my, 0), (cnt_all, 0), (gbuf_my, 0), (gbuf_all, 0),
                                 (htm_bf_my, 1), (ysh_dram, 1), (h_full, 1),
                                 *[(t, 1) for t in eout_myh],
                                 *[(t, 1) for t in eout_allh]):
                    rows, cols = int(t_.shape[0]), int(t_.shape[1])
                    src = s2b if isbf else scr2
                    for r0 in range(0, rows, 128):
                        r1 = min(rows, r0 + 128)
                        sync.dma_start(out=t_[r0:r1, :], in_=src[0:r1 - r0, 0:cols],
                                       allow_slow_non_contiguous=True).then_inc(rq, 16)
                        n += 16
                RN[0] += n
            block.sync(poison_dram)

        _scopes = []

        def scope(name):
            if _scopes:
                pn, sid = _scopes.pop()
                nc.leave_named_scope(pn, sid, False)
            if name is not None:
                sid, _ = nc.enter_named_scope(name, False)
                _scopes.append((name, sid))

        scope("P01_proj")

        def p0_consts(sync):
            sync.wait_ge(rq, RN[0])
            n = 0
            for dst, src in [(cf[:, 0:128], ident_f), (cf[:, 128:256], m_triuinc),
                             (cf[:, 256:384], m_striu), (cf[:, 384:512], m_striun),
                             (cf[:, 512:640], m_striln), (cb[:, 0:128], ident_b),
                             (onesc[:], onescol), (e127[:], e127c), (iotac[:], iota128),
                             (iota16[:], iota16g), (iota64[:], iota64c), (prem[:], premask),
                             (eselt[:], esel), (cvt[:], cvec), (rowpar[:], rowp),
                             (fcon[:], fconst_d)]:
                sync.dma_start(out=dst, in_=src[:]).then_inc(rq, 16)
                n += 16
            RN[0] += n
        block.sync(p0_consts)

        def p0_wcat(gp):
            gp.wait_ge(rq, RN[0])
            gp.dma_start(out=wcat_sb.rearrange("p (c n) -> p c n", n=256),
                         in_=wcatT.rearrange("(c p) n -> p c n", p=128)).then_inc(qG, 16)
            gp.dma_start(out=gw_sb.rearrange("p (c n) -> p c n", n=EXP),
                         in_=gwT.rearrange("(c p) n -> p c n", p=128)).then_inc(qG, 16)
            QG[0] += 32
            gp.engine_nop().then_inc(rq, 1)
            RN[0] += 1
        block.gpsimd(p0_wcat)

        round_("scalar", lambda sc: sc.activation(scr[0:1, 0:D], rowpar[:, 0:D], AF.Exp, scale=-1.0))

        round_("vector", lambda v: v.tensor_scalar_add(scr[0:1, 2 * D:3 * D], scr[0:1, 0:D], 1.0))
        round_("vector", lambda v: v.reciprocal(scr[0:1, D:2 * D], scr[0:1, 2 * D:3 * D]))
        round_("scalar", lambda sc: sc.activation(lnsig[:], scr[0:1, D:2 * D], AF.Ln, bias=1e-8))

        def f(t):
            t.matmul(p4[0][:, 0:D], cf[0:1, TRI:TRI + 128], lnsig[:], start=True, stop=True)
            return t.matmul(p4[0][:, D:2 * D], cf[0:1, TRI:TRI + 128], rowpar[0:1, D:2 * D],
                            start=True, stop=True)
        round_("tensor", f)
        round_("vector", lambda v: v.tensor_copy(bc64[:], p4[0][:, 0:2 * D]))
        tap("bc64", bc64[:])
        tap("lnsig", lnsig[:], 1)

        def p1_sync(sync):
            sync.wait_ge(rq, RN[0])
            for tc in range(NCH):
                if tc >= 1:
                    sync.wait_ge(sTx, tc * 4)
                sync.dma_start(out=xstg[:, 0:HD],
                               in_=x_own[tc * 128:(tc + 1) * 128, :]).then_inc(sXin, 16)
        block.sync(p1_sync)

        def p1_t(t):
            t.wait_ge(rq, RN[0])
            for tc in range(NCH):
                t.wait_ge(sXin, (tc + 1) * 16)
                for hb in range(NHB):
                    g = hb // 4
                    bank = p4[g]
                    if tc > 0 and hb % 4 == 0:
                        t.wait_ge(sVx, (tc - 1) * 4 + g + 1)
                    mm = t.transpose(bank[:, (hb % 4) * 128:(hb % 4 + 1) * 128],
                                     xstg[:, hb * 128:(hb + 1) * 128],
                                     cf[:, IDE:IDE + 128])
                    if hb % 4 == 3:
                        mm.then_inc(sTx, 1)
        block.tensor(p1_t)

        def p1_v(v):
            v.wait_ge(rq, RN[0])
            for tc in range(NCH):
                for g in range(4):
                    v.wait_ge(sTx, tc * 4 + g + 1)
                    dst = x_fm.rearrange("p (hb t) -> p hb t", t=TOWN)[:, 4 * g:4 * g + 4,
                                                                      tc * 128:(tc + 1) * 128]
                    src = p4[g].rearrange("p (k t) -> p k t", t=128)
                    v.tensor_copy(dst, src).then_inc(sVx, 1)
            v.wait_ge(sVx, 32)
            for g in range(4):
                v.tensor_copy(x_fmb[:, g * 4096:(g + 1) * 4096],
                              x_fm[:, g * 4096:(g + 1) * 4096])
            v.engine_nop().then_inc(rq, 1)
        block.vector(p1_v)
        RN[0] += 1

        def p1_tp(t):
            t.wait_ge(rq, RN[0])
            t.wait_ge(qG, 32)
            for tc in range(NCH):
                pt, off = pslot(tc)
                for hb in range(NHB):
                    mm = t.matmul(pt[:, off:off + 256],
                                  x_fmb[:, hb * TOWN + tc * 128:hb * TOWN + (tc + 1) * 128],
                                  wcat_sb[:, hb * 256:(hb + 1) * 256],
                                  start=(hb == 0), stop=(hb == NHB - 1))
            mm.then_inc(rq, 1)
            RN[0] += 1
        block.tensor(p1_tp)

        def f(v):
            for tc in range(NCH):
                pt, off = pslot(tc)
                last = v.tensor_copy(proj[:, tc * 256:(tc + 1) * 256], pt[:, off:off + 256])
            return last
        round_("vector", f)

        scope("P2_prep")
        def f(sc):
            for c in range(NCH):
                b = c * 256
                sc.activation(scr2[:, 2 * c * D:(2 * c + 1) * D], proj[:, b:b + D], AF.Square,
                              accum_out=scr[:, 512 + 2 * c:513 + 2 * c])
                last = sc.activation(scr2[:, (2 * c + 1) * D:(2 * c + 2) * D],
                                     proj[:, b + 128:b + 192], AF.Square,
                                     accum_out=scr[:, 513 + 2 * c:514 + 2 * c])
            return last
        round_("scalar", f)
        round_("scalar", lambda sc: sc.activation(scr[:, 528:528 + 2 * NCH],
                                                  scr[:, 512:512 + 2 * NCH], AF.Ln))
        round_("scalar", lambda sc: sc.activation(scr[:, 544:544 + 2 * NCH],
                                                  scr[:, 528:528 + 2 * NCH], AF.Exp, scale=-0.5))

        round_("vector", lambda v: v.tensor_scalar_min(scr[:, 560:560 + 2 * NCH],
                                                        scr[:, 544:544 + 2 * NCH], 1e12))

        def f(v):
            for c in range(NCH):
                b = c * 256
                v.tensor_scalar(scr2[:, c * 128:c * 128 + D], proj[:, b:b + D],
                                scr[:, 560 + 2 * c:561 + 2 * c], 0.0, ALU.mult, ALU.add)
                v.tensor_scalar(scr2[:, c * 128 + D:c * 128 + 2 * D], proj[:, b + 128:b + 192],
                                scr[:, 561 + 2 * c:562 + 2 * c], 0.0, ALU.mult, ALU.add)
                last = v.tensor_tensor(scr2[:, 1024 + c * D:1024 + (c + 1) * D],
                                       proj[:, b + 192:b + 256], bc64[:, D:2 * D], ALU.add)
            return last
        round_("vector", f)

        round_("scalar", lambda sc: sc.activation(scr[:, 0:NCH * D], scr2[:, 1024:1024 + NCH * D],
                                                  AF.Tanh, scale=0.5))

        def f(v):
            for c in range(NCH):
                last = v.tensor_scalar(scr[:, 512 + c * D:512 + (c + 1) * D],
                                       scr[:, c * D:(c + 1) * D], 0.5, 0.5, ALU.mult, ALU.add)
            return last
        round_("vector", f)

        def f(v):
            for c in range(NCH):
                last = v.scalar_tensor_tensor(stLA[:, c * D:(c + 1) * D],
                                              scr[:, 512 + c * D:512 + (c + 1) * D], 8.0,
                                              bc64[:, 0:D], ALU.mult, ALU.mult)
            return last
        round_("vector", f)

        tap("stLA", stLA[:])

        def f(t):
            for c in range(NCH):
                pt, off = pslot(c)
                last = t.matmul(pt[:, off:off + D], cf[:, TRI:TRI + 128],
                                stLA[:, c * D:(c + 1) * D], start=True, stop=True)
            return last
        round_("tensor", f)

        def f(sc):
            for c in range(NCH):
                pt, off = pslot(c)
                sc.activation(stG[:, c * D:(c + 1) * D], pt[:, off:off + D], AF.Exp)
                last = sc.activation(scr2[:, 1024 + c * D:1024 + (c + 1) * D],
                                     pt[:, off:off + D], AF.Exp, scale=-1.0)
            return last
        round_("scalar", f)

        def f(v):
            for c in range(NCH):
                pt, off = pslot(c)
                last = v.tensor_tensor(scr[:, c * D:(c + 1) * D], pt[:, off:off + D],
                                       stLA[:, c * D:(c + 1) * D], ALU.subtract)
            return last
        round_("vector", f)

        def f(sc):
            last = sc.activation(scr[:, 0:NCH * D], scr[:, 0:NCH * D], AF.Exp)
            return last
        round_("scalar", f)
        round_("scalar", lambda sc: sc.activation(scr2[:, 1536:1536 + NCH * D],
                                                  stLA[:, 0:NCH * D], AF.Exp))
        tap("gi", scr2[:, 1024:1536])
        tap("alpha", scr2[:, 1536:2048])
        tap("gprev", scr[:, 0:512])

        def f(v):
            kcol = lambda c: scr2[:, c * 128:c * 128 + D]
            qcol = lambda c: scr2[:, c * 128 + D:c * 128 + 2 * D]
            gp_ = lambda c: scr[:, c * D:(c + 1) * D]
            gi_ = lambda c: scr2[:, 1024 + c * D:1024 + (c + 1) * D]
            al_ = lambda c: scr2[:, 1536 + c * D:1536 + (c + 1) * D]
            for c in range(NCH):
                v.tensor_tensor(stKV[:, c * 128:c * 128 + D], kcol(c), gp_(c), ALU.mult)
                v.tensor_copy(stKV[:, c * 128 + D:c * 128 + 2 * D],
                              proj[:, c * 256 + D:c * 256 + 2 * D])
                v.tensor_tensor(stQt[:, c * D:(c + 1) * D], qcol(c), gp_(c), ALU.mult)
                v.tensor_tensor(stQh[:, c * D:(c + 1) * D], qcol(c),
                                stG[:, c * D:(c + 1) * D], ALU.mult)
            for c in range(NCH):
                last = v.tensor_scalar(scr[:, 512 + c * D:512 + (c + 1) * D], al_(c),
                                       -1.0, 1.0, ALU.mult, ALU.add)
            return last
        round_("vector", f)

        def f(v):
            kcol = lambda c: scr2[:, c * 128:c * 128 + D]
            for c in range(NCH):
                last = v.tensor_tensor(stUc[:, c * D:(c + 1) * D],
                                       scr[:, 512 + c * D:512 + (c + 1) * D], kcol(c), ALU.mult)
            return last
        round_("vector", f)

        def f(v):
            gi_ = lambda c: scr2[:, 1024 + c * D:1024 + (c + 1) * D]
            for c in range(NCH):
                last = v.tensor_tensor(stU0[:, c * D:(c + 1) * D],
                                       stUc[:, c * D:(c + 1) * D], gi_(c), ALU.mult)
            return last
        round_("vector", f)

        def f(t):
            for c in range(NCH):
                last = t.matmul(p4[0][0:1, c * D:(c + 1) * D], e127[:],
                                stG[:, c * D:(c + 1) * D], start=True, stop=True)
            return last
        round_("tensor", f)
        round_("vector", lambda v: v.tensor_copy(rowa[0:1, 0:NCH * D], p4[0][0:1, 0:NCH * D]))

        def f(t):
            for c in range(NCH):
                last = t.matmul(p4[1][0:64, c:c + 1], rowa[0:1, c * D:(c + 1) * D],
                                onesc[0:1, 0:1], start=True, stop=True)
            return last
        round_("tensor", f)
        round_("vector", lambda v: v.tensor_copy(glc[:], p4[1][0:64, 0:NCH]))

        def f(t):
            for c in range(NCH):
                pt, off = pslot(c)
                t.transpose(pt[0:64, off:off + 128], stQt[:, c * D:(c + 1) * D], cf[:, IDE:IDE + 128])
                t.transpose(pt[0:64, off + 128:off + 256], stKV[:, c * 128:c * 128 + D],
                            cf[:, IDE:IDE + 128])
                last = t.transpose(pt[0:64, off + 256:off + 384], stQh[:, c * D:(c + 1) * D],
                                   cf[:, IDE:IDE + 128])
            return last
        round_("tensor", f)

        def f(v):
            for c in range(NCH):
                pt, off = pslot(c)
                v.tensor_copy(fmQt[:, c * L:(c + 1) * L], pt[0:64, off:off + 128])
                v.tensor_copy(fmKt[:, c * L:(c + 1) * L], pt[0:64, off + 128:off + 256])
                last = v.tensor_copy(fmQh[:, c * L:(c + 1) * L], pt[0:64, off + 256:off + 384])
            return last
        round_("vector", f)

        tap("proj", proj[:])
        tap("stKV", stKV[:])
        tap("stQt", stQt[:])
        tap("stQh", stQh[:])
        tap("stU0", stU0[:])
        tap("stG", stG[:])
        tap("glc", glc[:], 64)
        tap("fmQt", fmQt[:], 64)

        scope("P3_scan")
        ch_base_col = 1
        for pas in range(2):
            if pas == 0:
                round_("vector", lambda v: v.tensor_copy(stUc[:], stU0[:]))
            else:
                def f(v):
                    for c in range(NCH):
                        last = v.tensor_scalar(stUc[:, c * D:(c + 1) * D],
                                               stU0[:, c * D:(c + 1) * D],
                                               stS[:, c:c + 1], 0.0, ALU.mult, ALU.add)
                    return last
                round_("vector", f)

            def f(t):
                for c in range(NCH):
                    pt, off = pslot(c)
                    last = t.transpose(pt[0:64, off:off + 128], stUc[:, c * D:(c + 1) * D],
                                       cf[:, IDE:IDE + 128])
                return last
            round_("tensor", f)

            def f(v):
                for c in range(NCH):
                    pt, off = pslot(c)
                    last = v.tensor_copy(fmU[:, c * L:(c + 1) * L], pt[0:64, off:off + 128])
                return last
            round_("vector", f)

            def f(t):
                for c in range(NCH):
                    pt, off = pslot(c)
                    t.matmul(pt[:, off:off + 128], fmKt[:, c * L:(c + 1) * L],
                             fmU[:, c * L:(c + 1) * L], start=True, stop=True)
                    last = t.matmul(pt[:, off + 128:off + 256], fmU[:, c * L:(c + 1) * L],
                                    fmKt[:, c * L:(c + 1) * L], start=True, stop=True)
                return last
            round_("tensor", f)

            def f(v):
                for c in range(NCH):
                    pt, off = pslot(c)
                    v.tensor_tensor(solW[:, c * L:(c + 1) * L], pt[:, off:off + 128],
                                    cf[:, SLN:SLN + 128], ALU.mult)
                    last = v.tensor_tensor(solU[:, c * L:(c + 1) * L], pt[:, off + 128:off + 256],
                                           cf[:, SUN:SUN + 128], ALU.mult)
                return last
            round_("vector", f)

            def f(t):
                for c in range(NCH):
                    pt, off = pslot(c)
                    rhs = fmQt if pas == 0 else fmQh
                    last = t.matmul(pt[:, off:off + 128], fmU[:, c * L:(c + 1) * L],
                                    rhs[:, c * L:(c + 1) * L], start=True, stop=True)
                return last
            round_("tensor", f)

            maskoff = SUU if pas == 0 else TRI
            def f(v):
                for c in range(NCH):
                    pt, off = pslot(c)
                    last = v.tensor_tensor(gP[:, c * L:(c + 1) * L], pt[:, off:off + 128],
                                           cf[:, maskoff:maskoff + 128], ALU.mult)
                return last
            round_("vector", f)

            round_("vector", lambda v: v.tensor_copy(solR[:], stKV[:]))

            for j in range(7):
                def f(t, j=j):
                    for c in range(NCH):
                        pt, off = pslot(c)
                        last = t.matmul(pt[:, off:off + 128], solU[:, c * L:(c + 1) * L],
                                        solR[:, c * 128:(c + 1) * 128], start=True, stop=True)
                        if j < 6:
                            t.matmul(pt[:, off + 128:off + 256], solU[:, c * L:(c + 1) * L],
                                     solW[:, c * L:(c + 1) * L], start=True, stop=True)
                            last = t.matmul(pt[:, off + 256:off + 384], solW[:, c * L:(c + 1) * L],
                                            solU[:, c * L:(c + 1) * L], start=True, stop=True)
                    return last
                round_("tensor", f)

                def f(v, j=j):
                    for c in range(NCH):
                        pt, off = pslot(c)
                        v.tensor_tensor(solR[:, c * 128:(c + 1) * 128], pt[:, off:off + 128],
                                        solR[:, c * 128:(c + 1) * 128], ALU.add)
                        if j < 6:
                            v.tensor_copy(solW[:, c * L:(c + 1) * L], pt[:, off + 128:off + 256])
                            last = v.tensor_copy(solU[:, c * L:(c + 1) * L], pt[:, off + 256:off + 384])
                        else:
                            last = v.engine_nop()
                    return last
                round_("vector", f)

            tap(f"solW0_p{pas}", solW[:])
            tap(f"solR_p{pas}", solR[:])

            def f(t):
                for c in range(NCH):
                    pt, off = pslot(c)
                    t.transpose(pt[0:64, off:off + 128], solR[:, c * 128:c * 128 + D],
                                cf[:, IDE:IDE + 128])
                    t.matmul(pt[0:64, off + 128:off + 192], solR[:, c * 128:c * 128 + D],
                             stUc[:, c * D:(c + 1) * D], start=True, stop=True)
                    last = t.matmul(pt[0:64, off + 192:off + 256], stUc[:, c * D:(c + 1) * D],
                                    solR[:, c * 128 + D:c * 128 + 2 * D], start=True, stop=True)
                return last
            round_("tensor", f)

            def f(v):
                for c in range(NCH):
                    pt, off = pslot(c)
                    v.tensor_scalar(fmZ[:, c * L:(c + 1) * L], pt[0:64, off:off + 128],
                                    -1.0, 0.0, ALU.mult, ALU.add)
                    v.tensor_tensor(abATs[:, c * D:(c + 1) * D], cf[0:64, IDE:IDE + D],
                                    pt[0:64, off + 128:off + 192], ALU.subtract)
                    last = v.tensor_scalar(abBgs[:, c * D:(c + 1) * D],
                                           pt[0:64, off + 192:off + 256],
                                           glc[:, c:c + 1], 0.0, ALU.mult, ALU.add)
                return last
            round_("vector", f)

            def p3_abdma(sync):
                sync.wait_ge(rq, RN[0])
                sync.dma_start(out=h0_dram[0:1, 0:3], in_=fcon[0:1, 0:3],
                               allow_slow_non_contiguous=True).then_inc(rq, 16)
                RN[0] += 16
                abv = ab_my.rearrange("(c r) n -> r c n", r=D)
                sync.dma_start(out=abv[:, :, 0:D],
                               in_=abATs.rearrange("r (c n) -> r c n", n=D),
                               allow_slow_non_contiguous=True).then_inc(rq, 16)
                sync.dma_start(out=abv[:, :, D:2 * D],
                               in_=abBgs.rearrange("r (c n) -> r c n", n=D),
                               allow_slow_non_contiguous=True).then_inc(rq, 16)
                sync.dma_start(out=abv[:, :, 2 * D:2 * D + 1],
                               in_=glc[:, 0:NCH].rearrange("r (c x) -> r c x", x=1),
                               allow_slow_non_contiguous=True).then_inc(rq, 16)
                sync.dma_start(out=abv[:, :, 2 * D + 1:2 * D + 2],
                               in_=glc[:, 0:NCH].rearrange("r (c x) -> r c x", x=1),
                               allow_slow_non_contiguous=True).then_inc(rq, 16)
                RN[0] += 16 * 4
            block.sync(p3_abdma)

            def p3_ag(gp):
                gp.wait_ge(rq, RN[0])
                gp.collective_compute(
                    "AllGather", ALU.bypass,
                    replica_groups=[list(range(NCORE))],
                    ins=[ab_my.opt()], outs=[ab_allp[pas].opt()],
                ).then_inc(rq, 1)
                RN[0] += 1
            block.gpsimd(p3_ag)

            h0ring = scr[0:64, 0:512]
            ab_ov = arena[0:64, MARK0:MARK0 + 50 * 130]
            ab_s2 = scr2[0:64, 0:14 * 130]

            def slotAB(gc):
                if gc < 50:
                    return ab_ov[:, gc * 130:(gc + 1) * 130]
                return ab_s2[:, (gc - 50) * 130:(gc - 49) * 130]

            round_("vector", lambda v: v.memset(h0ring[:, 0:D], 0.0))

            sx0 = NCH * 16 + pas * 160

            def p3_chain_dma(sync):
                sync.wait_ge(rq, RN[0])
                sync.dma_start(out=ab_ov.rearrange("r (c n) -> r c n", n=130),
                               in_=ab_allp[pas][0:50 * D, :].rearrange("(c r) n -> r c n", r=D),
                               allow_slow_non_contiguous=True).then_inc(sXin, 16)
                sync.dma_start(out=ab_s2.rearrange("r (c n) -> r c n", n=130),
                               in_=ab_allp[pas][50 * D:64 * D, :].rearrange("(c r) n -> r c n", r=D),
                               allow_slow_non_contiguous=True).then_inc(sXin, 16)
                for k in range(8):
                    sync.wait_ge(rq, RN[0] + 2 * (8 * k + 6) + 2)
                    sync.dma_start(out=h0_dram[k * 8 * D:(k + 1) * 8 * D, :]
                                   .rearrange("(g r) c -> r g c", r=D),
                                   in_=h0ring[:, 0:512].rearrange("r (g c) -> r g c", c=D),
                                   allow_slow_non_contiguous=True).then_inc(sXin, 16)
            block.sync(p3_chain_dma)

            def p3_chain_t(t):
                t.wait_ge(rq, RN[0])
                t.wait_ge(sXin, sx0 + 32)
                for gc in range(64):
                    bank = p4[gc % 2]
                    t.matmul(bank[0:64, 0:D], slotAB(gc)[:, 0:D],
                             h0ring[:, (gc % 8) * D:(gc % 8 + 1) * D],
                             start=True, stop=True).then_inc(rq, 1)
                    t.wait_ge(rq, RN[0] + (gc + 1) * 2)
                RN[0] += 64 * 2

            def p3_chain_v(v):
                base = RN[0] - 64 * 2
                for gc in range(64):
                    v.wait_ge(rq, base + gc * 2 + 1)
                    if gc % 8 == 7:
                        v.wait_ge(sXin, sx0 + 32 + ((gc // 8) + 1) * 16)
                    bank = p4[gc % 2]
                    v.scalar_tensor_tensor(h0ring[:, ((gc + 1) % 8) * D:((gc + 1) % 8 + 1) * D],
                                           bank[0:64, 0:D],
                                           slotAB(gc)[:, 2 * D:2 * D + 1],
                                           slotAB(gc)[:, D:2 * D],
                                           ALU.mult, ALU.add).then_inc(rq, 1)
            block.tensor(p3_chain_t)
            block.vector(p3_chain_v)

            def f(t):
                return t.matmul(p4[2][0:64, 0:1], cf[0:1, TRI:TRI + 64], cvt[0:1, 1:2],
                                start=True, stop=True)
            round_("tensor", f)

            def f(v):
                for c in range(NCH):
                    last = v.scalar_tensor_tensor(scr[0:64, c:c + 1], p4[2][0:64, 0:1], float(D),
                                                  iota64[:], ALU.mult, ALU.add)
                return last
            round_("vector", f)

            def f(v):
                for c in range(1, NCH):
                    v.tensor_scalar_add(scr[0:64, 8 + c:9 + c], scr[0:64, c:c + 1], float(c * D))
                return v.tensor_copy(scr[0:64, 8:9], scr[0:64, 0:1])
            round_("vector", f)
            round_("vector", lambda v: v.tensor_copy(idxg[:], scr[0:64, 8:8 + NCH]))

            def p3_h0g(gp):
                gp.wait_ge(rq, RN[0])
                gp.wait_ge(sXin, sx0 + 160)
                breg = gp.to_reg(64 * D - 1)
                for c in range(NCH):
                    gp.indirect_dma_start(
                        h0my[:, c * D:(c + 1) * D],
                        None, h0_dram[:],
                        IndirectOffsetOnAxis(idxg[:, c:c + 1], 0),
                        bounds_check=breg, oob_is_err=False,
                    ).then_inc(qG, 16)
                QG[0] += 16 * NCH
                gp.engine_nop().then_inc(rq, 1)
                RN[0] += 1
            block.gpsimd(p3_h0g)
            QGH0 = QG[0]

            def p3_h0w(t):
                t.wait_ge(qG, QGH0)
            block.tensor(p3_h0w)

            tap(f"h0my_p{pas}", h0my[:], 64)
            tap(f"abAT_p{pas}", abATs[:], 64)
            tap(f"abBg_p{pas}", abBgs[:], 64)

            def f(t):
                for c in range(NCH):
                    pt, off = pslot(c)
                    last = t.matmul(pt[:, off:off + D], fmZ[:, c * L:(c + 1) * L],
                                    h0my[:, c * D:(c + 1) * D], start=True, stop=True)
                return last
            round_("tensor", f)

            def f(v):
                for c in range(NCH):
                    pt, off = pslot(c)
                    last = v.tensor_tensor(stM[:, c * D:(c + 1) * D], pt[:, off:off + D],
                                           solR[:, c * 128 + D:c * 128 + 2 * D], ALU.add)
                return last
            round_("vector", f)

            def f(t):
                for c in range(NCH):
                    pt, off = pslot(c)
                    lhs = fmQt if pas == 0 else fmQh
                    t.matmul(pt[:, off:off + D], lhs[:, c * L:(c + 1) * L],
                             h0my[:, c * D:(c + 1) * D], start=True, stop=False)
                    last = t.matmul(pt[:, off:off + D], gP[:, c * L:(c + 1) * L],
                                    stM[:, c * D:(c + 1) * D], start=False, stop=True)
                return last
            round_("tensor", f)

            if pas == 0:
                def f(v):
                    for c in range(NCH):
                        pt, off = pslot(c)
                        last = v.tensor_tensor(scr[:, c * D:(c + 1) * D],
                                               stKV[:, c * 128 + D:c * 128 + 2 * D],
                                               pt[:, off:off + D], ALU.subtract)
                    return last
                round_("vector", f)

                def f(sc):
                    for c in range(NCH):
                        last = sc.activation(scr2[:, c * D:(c + 1) * D], scr[:, c * D:(c + 1) * D],
                                             AF.Square, accum_out=errs[:, c:c + 1])
                    return last
                round_("scalar", f)
                round_("scalar", lambda sc: sc.activation(scr[:, 0:NCH], errs[:, 0:NCH],
                                                          AF.Tanh, scale=SURSC / 2))
                round_("vector", lambda v: v.tensor_scalar(stS[:, 0:NCH], scr[:, 0:NCH],
                                                           0.5, 0.5, ALU.mult, ALU.add))
            else:
                def f(v):
                    for c in range(NCH):
                        pt, off = pslot(c)
                        last = v.tensor_copy(yattn[:, c * D:(c + 1) * D], pt[:, off:off + D])
                    return last
                round_("vector", f)

        tap("stM_p1", stM[:])
        tap("errs", errs[:])
        tap("stS", stS[:])
        tap("yattn", yattn[:])

        scope("P4_wo_ln")
        def p4_wo(sync):
            sync.wait_ge(rq, RN[0])
            sync.dma_start(out=scr2[0:64, 0:HD], in_=woT[:]).then_inc(rq, 16)
            RN[0] += 16
        block.sync(p4_wo)

        def f(t):
            return t.matmul(p4[0][0:64, 0:1], rowpar[0:1, 2 * D:3 * D], onesc[0:1, 0:1],
                            start=True, stop=True)
        round_("tensor", f)

        round_("vector", lambda v: v.tensor_copy(nwcol[:], p4[0][0:64, 0:1]))
        round_("vector", lambda v: v.tensor_scalar(wo_fm[:], scr2[0:64, 0:HD], nwcol[:],
                                                   0.0, ALU.mult, ALU.add))

        def f(sc):
            for c in range(NCH):
                last = sc.activation(scr2[:, c * D:(c + 1) * D],
                                     yattn[:, c * D:(c + 1) * D], AF.Square,
                                     accum_out=scr[:, 512 + c:513 + c])
            return last
        round_("scalar", f)
        round_("scalar", lambda sc: sc.activation(scr[:, 520:520 + NCH], scr[:, 512:512 + NCH],
                                                  AF.Ln, scale=1.0 / D, bias=RMS_EPS))
        round_("scalar", lambda sc: sc.activation(scr[:, 528:528 + NCH], scr[:, 520:520 + NCH],
                                                  AF.Exp, scale=-0.5))

        def f(v):
            for c in range(NCH):
                last = v.tensor_scalar(scr2[:, c * D:(c + 1) * D],
                                       yattn[:, c * D:(c + 1) * D],
                                       scr[:, 528 + c:529 + c], 0.0, ALU.mult, ALU.add)
            return last
        round_("vector", f)

        tap("ynm", scr2[:, 0:NCH * D])
        tap("rstdy", scr[:, 528:528 + NCH])
        tap("wofm", wo_fm[:], 64, BF16)

        def f(t):
            for c in range(NCH):
                pt, off = pslot(c)
                last = t.transpose(pt[0:64, off:off + 128],
                                   scr2[:, c * D:(c + 1) * D], cf[:, IDE:IDE + 128])
            return last
        round_("tensor", f)

        def f(v):
            for c in range(NCH):
                pt, off = pslot(c)
                last = v.tensor_copy(ynfm[:, c * 128:(c + 1) * 128],
                                     pt[0:64, off:off + 128])
            return last
        round_("vector", f)

        for ob2 in range(NHB // 2):
            def f(t, ob2=ob2):
                for k in range(2):
                    ob = ob2 * 2 + k
                    for half in range(2):
                        last = t.matmul(pw[k][:, half * 512:(half + 1) * 512],
                                        wo_fm[:, ob * 128:(ob + 1) * 128],
                                        ynfm[:, half * 512:(half + 1) * 512],
                                        start=True, stop=True)
                return last
            round_("tensor", f)

            def f(v, ob2=ob2):
                for k in range(2):
                    ob = ob2 * 2 + k
                    last = v.tensor_tensor(x_fm[:, ob * TOWN:(ob + 1) * TOWN],
                                           pw[k][:, 0:1024],
                                           x_fm[:, ob * TOWN:(ob + 1) * TOWN], ALU.add)
                return last
            round_("vector", f)

        tap("s1fm", x_fm[:, 0:2048])

        for tile in range(2):
            tsl = slice(tile * 512, (tile + 1) * 512)
            def f(t, tile=tile):
                for hb in range(NHB):
                    last = t.matmul(p4[0][0:1, 0:512],
                                    onesc[:], x_fm[:, hb * TOWN + tile * 512:hb * TOWN + (tile + 1) * 512],
                                    start=(hb == 0), stop=(hb == NHB - 1))
                return last
            round_("tensor", f)
            round_("vector", lambda v, tile=tile: v.tensor_copy(rowa[0:1, tile * 512:(tile + 1) * 512],
                                                                p4[0][0:1, 0:512]))
            for hb in range(NHB):
                round_("scalar", lambda sc, hb=hb, tile=tile: sc.activation(
                    scr2[:, 0:512],
                    x_fm[:, hb * TOWN + tile * 512:hb * TOWN + (tile + 1) * 512], AF.Square))
                round_("tensor", lambda t, hb=hb: t.matmul(
                    p4[1][0:1, 0:512], onesc[:], scr2[:, 0:512],
                    start=(hb == 0), stop=(hb == NHB - 1)))
            round_("vector", lambda v, tile=tile: v.tensor_copy(rowa[0:1, 1024 + tile * 512:1024 + (tile + 1) * 512],
                                                                p4[1][0:1, 0:512]))

        def f(v):
            v.tensor_scalar_mul(rowa[0:1, 0:1024], rowa[0:1, 0:1024], 1.0 / HD)
            return v.tensor_scalar_mul(rowa[0:1, 1024:2048], rowa[0:1, 1024:2048], 1.0 / HD)
        round_("vector", f)
        round_("vector", lambda v: v.tensor_tensor(scr[0:1, 0:1024], rowa[0:1, 0:1024],
                                                   rowa[0:1, 0:1024], ALU.mult))
        round_("vector", lambda v: v.tensor_tensor(rowa[0:1, 1024:2048], rowa[0:1, 1024:2048],
                                                   scr[0:1, 0:1024], ALU.subtract))
        round_("scalar", lambda sc: sc.activation(rowa[0:1, 1024:2048], rowa[0:1, 1024:2048],
                                                  AF.Ln, bias=LN_EPS))
        round_("scalar", lambda sc: sc.activation(rowa[0:1, 1024:2048], rowa[0:1, 1024:2048],
                                                  AF.Exp, scale=-0.5))
        def f(v):
            return v.tensor_tensor(rowa[0:1, 0:1024], rowa[0:1, 0:1024],
                                   rowa[0:1, 1024:2048], ALU.mult)
        round_("vector", f)

        def f(t):
            for half in range(2):
                t.matmul(pw[0][:, half * 512:(half + 1) * 512], cf[0:1, TRI:TRI + 128],
                         rowa[0:1, 1024 + half * 512:1024 + (half + 1) * 512],
                         start=True, stop=True)
                last = t.matmul(pw[1][:, half * 512:(half + 1) * 512], cf[0:1, TRI:TRI + 128],
                                rowa[0:1, half * 512:(half + 1) * 512],
                                start=True, stop=True)
            return last
        round_("tensor", f)

        def f(v):
            for hb in range(NHB):
                last = v.tensor_tensor(x_fm[:, hb * TOWN:(hb + 1) * TOWN],
                                       x_fm[:, hb * TOWN:(hb + 1) * TOWN], pw[0][:, 0:1024],
                                       ALU.mult)
            return last
        round_("vector", f)

        def f(v):
            for hb in range(NHB):
                last = v.tensor_tensor(x_fm[:, hb * TOWN:(hb + 1) * TOWN],
                                       x_fm[:, hb * TOWN:(hb + 1) * TOWN], pw[1][:, 0:1024],
                                       ALU.subtract)
            return last
        round_("vector", f)

        def f(v):
            for hb in range(NHB):
                last = v.tensor_copy(h_fmb[:, hb * TOWN:(hb + 1) * TOWN],
                                     x_fm[:, hb * TOWN:(hb + 1) * TOWN])
            return last
        round_("vector", f)

        tap("rowa", rowa[:], 1)

        def emit_h_tm():
            RQ_EH = RN[0]

            def pslot_h(tc, g):
                if tc % 2 == 0:
                    return p4[g][:, 0:512]
                return pw[g // 2][:, (g % 2) * 512:(g % 2 + 1) * 512]

            for tc in range(NCH):
                def ft(t, tc=tc):
                    if tc <= 1:
                        t.wait_ge(rq, RQ_EH)
                    if tc >= 2:
                        t.wait_ge(sVx, 32 + tc - 1)
                    for hb in range(NHB):
                        g = hb // 4
                        mm = t.transpose(pslot_h(tc, g)[:, (hb % 4) * 128:(hb % 4) * 128 + 128],
                                         x_fm[:, hb * TOWN + tc * 128:hb * TOWN + (tc + 1) * 128],
                                         cf[:, IDE:IDE + 128])
                    mm.then_inc(sTx, 1)
                block.tensor(ft)

                rqd = RN[0]
                def fv(v, tc=tc, rqd=rqd):
                    v.wait_ge(sTx, 32 + tc + 1)
                    v.wait_ge(rq, rqd)
                    for g in range(4):
                        v.tensor_copy(hstg_bf[:, g * 512:(g + 1) * 512], pslot_h(tc, g))
                    for g in range(4):
                        v.tensor_copy(scr2[:, g * 512:(g + 1) * 512], pslot_h(tc, g))
                    for _ in range(6):
                        v.engine_nop()
                    v.engine_nop().then_inc(sVx, 1)
                block.vector(fv)

                def fd(sync, tc=tc):
                    sync.wait_ge(sVx, 32 + tc + 1)
                    sync.dma_start(out=htm_bf_my[tc * 128:(tc + 1) * 128, :],
                                   in_=hstg_bf[:, 0:HD]).then_inc(rq, 16)
                    sync.dma_start(out=htm_f32[tc * 128:(tc + 1) * 128, :],
                                   in_=scr2[:, 0:HD]).then_inc(rq, 16)
                    RN[0] += 32
                block.sync(fd)
        emit_h_tm()

        def p4_hag(gp):
            gp.wait_ge(rq, RN[0])
            gp.collective_compute("AllGather", ALU.bypass,
                                  replica_groups=[list(range(NCORE))],
                                  ins=[htm_bf_my.opt()], outs=[h_full.opt()]).then_inc(sCC, 1)
        block.gpsimd(p4_hag)

        scope("P5_router")
        for tile in range(2):
            def f(t, tile=tile):
                t.wait_ge(qG, 32)
                if tile == 0:
                    t.wait_ge(sVx, 40)
                for hb in range(NHB):
                    last = t.matmul(p4[tile % 2][0:16, 0:512],
                                    gw_sb[:, hb * EXP:(hb + 1) * EXP],
                                    x_fm[:, hb * TOWN + tile * 512:hb * TOWN + (tile + 1) * 512],
                                    start=(hb == 0), stop=(hb == NHB - 1))
                return last
            round_("tensor", f)
            round_("vector", lambda v, tile=tile: v.tensor_copy(
                lgfm[:, tile * 512:(tile + 1) * 512], p4[tile % 2][0:16, 0:512]))

        arena_reset()
        rscr = aa(1040)
        rscr2 = aa(1032)
        lgtm = aa(8 * 16)
        vals = aa(32)
        e4t = aa(32)
        s4t = aa(8)
        lns4 = aa(8)
        negm4 = aa(8)
        negb4 = aa(8)
        idx4 = aa(32)
        w4 = aa(32)
        excl = aa(4 * 8 * 16)
        maskg = aa(8 * 16)
        rankg = aa(8 * 16)
        slot4 = aa(32)
        slotc4 = aa(32)
        destf = aa(32)
        gidxf = aa(32)
        tokid0 = aa(1)
        tokidc = aa(8)
        pairsA = aa(64)
        cntrow = aa(8 * 16)[0:1, :]
        carrow = aa(8 * 16)[0:1, :]
        prefr = aa(16)[0:1, :]
        cnasb = aa(16)[0:8, :]
        gbsb = aa(2052)[0:16, :]
        myg = aa(2052)[0:2, :]
        idcols = aa(16)
        wcols = aa(16)
        workt = rscr[:, 384:640]
        hidxf = aa(16)
        FFN_MARK = _aoff[0]

        rB = es.enter_context(nc.semaphore("rB"))
        RB = [0]
        RQ_P5A = RN[0]
        BQ = []
        BFIRST = [True]

        def roundB(engname, fn, inc=1):
            def thunk(engname=engname, fn=fn, inc=inc):
                first = BFIRST[0]
                BFIRST[0] = False

                def body(eng):
                    if first:
                        eng.wait_ge(rq, RQ_P5A)
                    eng.wait_ge(rB, RB[0])
                    last = fn(eng)
                    last.then_inc(rB, inc)
                    if engname == "vector":
                        for _ in range(6):
                            eng.engine_nop()
                    elif engname == "scalar":
                        for _ in range(6):
                            eng.nop()
                getattr(block, engname)(body)
                RB[0] += inc
            BQ.append(thunk)

        def rawB(engname, emitfn):
            def thunk(engname=engname, emitfn=emitfn):
                getattr(block, engname)(emitfn)
            BQ.append(thunk)

        def emitB(n):
            for _ in range(min(n, len(BQ))):
                BQ.pop(0)()

        def bslot(i):
            return p4[i % 4], (i // 4) * 16

        def f(t):
            for i in range(8):
                pt, off = bslot(i)
                last = t.transpose(pt[:, off:off + 16],
                                   lgfm[:, i * 128:(i + 1) * 128], cf[0:16, IDE:IDE + 16])
            return last
        roundB("tensor", f)

        def f(v):
            for i in range(8):
                pt, off = bslot(i)
                last = v.tensor_copy(lgtm[:, i * 16:(i + 1) * 16], pt[:, off:off + 16])
            return last
        roundB("vector", f)

        roundB("vector", lambda v: v.tensor_scalar_add(rscr[:, 960:976], iota16[:], BIGF))
        roundB("vector", lambda v: v.tensor_copy(workt[:, 0:128], lgtm[:]))
        roundB("vector", lambda v: v.memset(maskg[:], 0.0))

        for k in range(4):
            wsrc = workt[:, (k % 2) * 128:(k % 2) * 128 + 128]
            wdst = workt[:, ((k + 1) % 2) * 128:((k + 1) % 2) * 128 + 128]

            def mk(fn):
                def f(v, k=k, wsrc=wsrc, wdst=wdst, fn=fn):
                    for i in range(8):
                        last = fn(v, i, k, wsrc, wdst)
                    return last
                roundB("vector", f)
            mk(lambda v, i, k, ws, wd: v.tensor_reduce(vals[:, i * 4 + k:i * 4 + k + 1],
                                                       ws[:, i * 16:(i + 1) * 16], AX.X, ALU.max))
            mk(lambda v, i, k, ws, wd: v.tensor_scalar(rscr[:, i * 16:(i + 1) * 16],
                                                       ws[:, i * 16:(i + 1) * 16],
                                                       vals[:, i * 4 + k:i * 4 + k + 1], 0.0,
                                                       ALU.is_equal, ALU.add))
            mk(lambda v, i, k, ws, wd: v.scalar_tensor_tensor(
                rscr[:, 128 + i * 16:128 + (i + 1) * 16], rscr[:, i * 16:(i + 1) * 16], -BIGF,
                rscr[:, 960:976], ALU.mult, ALU.add))
            mk(lambda v, i, k, ws, wd: v.tensor_reduce(idx4[:, i * 4 + k:i * 4 + k + 1],
                                                       rscr[:, 128 + i * 16:128 + (i + 1) * 16],
                                                       AX.X, ALU.min))
            mk(lambda v, i, k, ws, wd: v.tensor_scalar(
                excl[:, (k * 8 + i) * 16:(k * 8 + i + 1) * 16], iota16[:],
                idx4[:, i * 4 + k:i * 4 + k + 1], 0.0, ALU.is_equal, ALU.add))
            mk(lambda v, i, k, ws, wd: v.scalar_tensor_tensor(
                wd[:, i * 16:(i + 1) * 16], excl[:, (k * 8 + i) * 16:(k * 8 + i + 1) * 16],
                -BIGF, ws[:, i * 16:(i + 1) * 16], ALU.mult, ALU.add))
            mk(lambda v, i, k, ws, wd: v.tensor_tensor(
                maskg[:, i * 16:(i + 1) * 16], maskg[:, i * 16:(i + 1) * 16],
                excl[:, (k * 8 + i) * 16:(k * 8 + i + 1) * 16], ALU.add))

        def f(v):
            for i in range(8):
                last = v.tensor_reduce(rscr[:, i:i + 1], vals[:, i * 4:(i + 1) * 4], AX.X, ALU.max)
            return last
        roundB("vector", f)

        def f(v):
            for i in range(8):
                last = v.tensor_scalar(negm4[:, i:i + 1], rscr[:, i:i + 1], -1.0, 0.0,
                                       ALU.mult, ALU.add)
            return last
        roundB("vector", f)

        def f(sc):
            for i in range(8):
                last = sc.activation(e4t[:, i * 4:(i + 1) * 4], vals[:, i * 4:(i + 1) * 4],
                                     AF.Exp, bias=negm4[:, i:i + 1])
            return last
        roundB("scalar", f)

        def f(v):
            for i in range(8):
                last = v.tensor_reduce(s4t[:, i:i + 1], e4t[:, i * 4:(i + 1) * 4], AX.X, ALU.add)
            return last
        roundB("vector", f)
        roundB("scalar", lambda sc: sc.activation(lns4[:], s4t[:], AF.Ln))

        def f(v):
            for i in range(8):
                last = v.tensor_tensor(rscr[:, i:i + 1], negm4[:, i:i + 1], lns4[:, i:i + 1],
                                       ALU.subtract)
            return last
        roundB("vector", f)
        roundB("vector", lambda v: v.tensor_copy(negb4[:], rscr[:, 0:8]))

        def f(sc):
            for i in range(8):
                last = sc.activation(w4[:, i * 4:(i + 1) * 4], vals[:, i * 4:(i + 1) * 4],
                                     AF.Exp, bias=negb4[:, i:i + 1])
            return last
        roundB("scalar", f)

        def f(t):
            for i in range(8):
                last = t.matmul(p4[0][0:1, i * 16:(i + 1) * 16], onesc[:],
                                maskg[:, i * 16:(i + 1) * 16], start=True, stop=True)
            return last
        roundB("tensor", f)
        roundB("vector", lambda v: v.tensor_copy(cntrow[:], p4[0][0:1, 0:128]))

        def f(v):
            v.tensor_tensor(rscr[0:1, 0:16], cntrow[0:1, 0:16], cntrow[0:1, 16:32], ALU.add)
            v.tensor_tensor(rscr[0:1, 16:32], cntrow[0:1, 32:48], cntrow[0:1, 48:64], ALU.add)
            v.tensor_tensor(rscr[0:1, 32:48], cntrow[0:1, 64:80], cntrow[0:1, 80:96], ALU.add)
            return v.tensor_tensor(rscr[0:1, 48:64], cntrow[0:1, 96:112], cntrow[0:1, 112:128],
                                   ALU.add)
        roundB("vector", f)

        def f(v):
            v.tensor_tensor(rscr[0:1, 64:80], rscr[0:1, 0:16], rscr[0:1, 16:32], ALU.add)
            return v.tensor_tensor(rscr[0:1, 80:96], rscr[0:1, 32:48], rscr[0:1, 48:64], ALU.add)
        roundB("vector", f)
        roundB("vector", lambda v: v.tensor_tensor(rscr[0:1, 96:112], rscr[0:1, 64:80],
                                                   rscr[0:1, 80:96], ALU.add))

        def fd(sync):
            sync.wait_ge(rB, RB[0])
            sync.dma_start(out=h0_dram[1:2, 0:3], in_=fcon[0:1, 0:3],
                           allow_slow_non_contiguous=True).then_inc(rB, 16)
            sync.dma_start(out=cnt_my[:], in_=rscr[0:1, 96:112]).then_inc(rB, 16)
            RB[0] += 32
        rawB("sync", fd)

        def fg(gp):
            gp.wait_ge(rB, RB[0])
            gp.collective_compute("AllGather", ALU.bypass,
                                  replica_groups=[list(range(NCORE))],
                                  ins=[cnt_my.opt()], outs=[cnt_all.opt()]).then_inc(rB, 1)
            RB[0] += 1
        rawB("gpsimd", fg)

        def fd(sync):
            sync.wait_ge(rB, RB[0])
            sync.dma_start(out=cnasb[:], in_=cnt_all[:]).then_inc(rB, 16)
            RB[0] += 16
        rawB("sync", fd)

        def f(t):
            return t.matmul(p4[1][0:1, 0:16], prem[:], cnasb[:], start=True, stop=True)
        roundB("tensor", f)
        roundB("vector", lambda v: v.tensor_copy(prefr[:], p4[1][0:1, 0:16]))

        roundB("vector", lambda v: v.tensor_copy(carrow[0:1, 0:16], prefr[:]))
        for i in range(1, 8):
            roundB("vector", lambda v, i=i: v.tensor_tensor(
                carrow[0:1, i * 16:(i + 1) * 16], carrow[0:1, (i - 1) * 16:i * 16],
                cntrow[0:1, (i - 1) * 16:i * 16], ALU.add))

        def f(t):
            for i in range(8):
                pt, off = bslot(i)
                t.matmul(pt[:, off:off + 16], cf[:, SUU:SUU + 128],
                         maskg[:, i * 16:(i + 1) * 16], start=True, stop=False)
                last = t.matmul(pt[:, off:off + 16], cf[0:1, TRI:TRI + 128],
                                carrow[0:1, i * 16:(i + 1) * 16], start=False, stop=True)
            return last
        roundB("tensor", f)

        def f(v):
            for i in range(8):
                pt, off = bslot(i)
                last = v.tensor_copy(rankg[:, i * 16:(i + 1) * 16], pt[:, off:off + 16])
            return last
        roundB("vector", f)

        def f(v):
            for i in range(8):
                for k in range(4):
                    last = v.tensor_tensor(rscr[:, (i * 4 + k) * 16:(i * 4 + k + 1) * 16],
                                           rankg[:, i * 16:(i + 1) * 16],
                                           excl[:, (k * 8 + i) * 16:(k * 8 + i + 1) * 16],
                                           ALU.mult)
            return last
        roundB("vector", f)

        def f(v):
            for j in range(32):
                last = v.tensor_reduce(slot4[:, j:j + 1], rscr[:, j * 16:(j + 1) * 16],
                                       AX.X, ALU.add)
            return last
        roundB("vector", f)

        def f(v):
            for j in range(32):
                last = v.tensor_scalar_min(slotc4[:, j:j + 1], slot4[:, j:j + 1], float(CAP))
            return last
        roundB("vector", f)

        def f(v):
            for j in range(32):
                v.scalar_tensor_tensor(destf[:, j:j + 1], idx4[:, j:j + 1], 1025.0,
                                       slotc4[:, j:j + 1], ALU.mult, ALU.add)
            for j in range(32):
                v.tensor_scalar(rscr[:, j:j + 1], slot4[:, j:j + 1], float(CAP), BIGF,
                                ALU.is_ge, ALU.mult)
            for j in range(32):
                last = v.scalar_tensor_tensor(rscr[:, 32 + j:33 + j], idx4[:, j:j + 1], 1024.0,
                                              slot4[:, j:j + 1], ALU.mult, ALU.add)
            return last
        roundB("vector", f)

        def f(v):
            for j in range(32):
                last = v.tensor_tensor(gidxf[:, j:j + 1], rscr[:, 32 + j:33 + j],
                                       rscr[:, j:j + 1], ALU.add)
            return last
        roundB("vector", f)

        def f(t):
            return t.matmul(p4[2][:, 0:1], cf[0:1, TRI:TRI + 128], cvt[0:1, 0:1],
                            start=True, stop=True)
        roundB("tensor", f)

        roundB("vector", lambda v: v.scalar_tensor_tensor(tokid0[:], p4[2][:, 0:1], 1.0,
                                                          iotac[:], ALU.mult, ALU.add))

        def f(v):
            for i in range(8):
                last = v.tensor_scalar_add(tokidc[:, i:i + 1], tokid0[:], float(i * 128 + 1))
            return last
        roundB("vector", f)

        def f(v):
            for i in range(8):
                for k in range(4):
                    j = i * 4 + k
                    v.tensor_copy(pairsA[:, 2 * j:2 * j + 1], tokidc[:, i:i + 1])
                    last = v.tensor_copy(pairsA[:, 2 * j + 1:2 * j + 2], w4[:, j:j + 1])
            return last
        roundB("vector", f)

        def f(v):
            last = v.tensor_copy(desti32[:], destf[:])
            return last
        roundB("vector", f)
        roundB("vector", lambda v: v.tensor_copy(gidx32[:], gidxf[:]))

        roundB("vector", lambda v: v.memset(rscr2[:, 0:258], 0.0))

        def fd(sync):
            sync.wait_ge(rB, RB[0])
            sync.dma_start(out=h0_dram[1:2, 0:3], in_=fcon[0:1, 0:3],
                           allow_slow_non_contiguous=True).then_inc(rB, 16)
            sync.dma_start(out=gbuf_my[0:16384, :].rearrange("(p n) c -> p (n c)", p=128),
                           in_=rscr2[:, 0:256]).then_inc(rB, 16)
            sync.dma_start(out=gbuf_my[16384:16400, :], in_=rscr2[0:16, 256:258]).then_inc(rB, 16)
            RB[0] += 48
        rawB("sync", fd)

        def fg(gp):
            gp.wait_ge(rB, RB[0])
            qbase = QG[0]
            for j in range(32):
                if j:
                    gp.wait_ge(qG, qbase + j * 16)
                gp.indirect_dma_start(
                    gbuf_my[:], IndirectOffsetOnAxis(desti32[:, j:j + 1], 0),
                    pairsA[:, 2 * j:2 * j + 2], None,
                ).then_inc(qG, 16)
            QG[0] += 32 * 16
            gp.engine_nop().then_inc(rB, 1)
            RB[0] += 1
        rawB("gpsimd", fg)

        QG_SCAT = [0]

        def fg(gp):
            gp.wait_ge(rB, RB[0])
            gp.wait_ge(qG, QG_SCAT[0])
            gp.collective_compute("AllReduce", ALU.add,
                                  replica_groups=[list(range(NCORE))],
                                  ins=[gbuf_my.opt()], outs=[gbuf_all.opt()]).then_inc(rB, 1)
            RB[0] += 1
        def thunk_ar(fg=fg):
            QG_SCAT[0] = QG[0]
            block.gpsimd(fg)
        BQ.append(thunk_ar)

        def fd(sync):
            sync.wait_ge(rB, RB[0])
            sync.dma_start(out=gbsb[:, 0:2050],
                           in_=gbuf_all.rearrange("(e s) c -> e (s c)", e=16)).then_inc(rB, 16)
            RB[0] += 16
        rawB("sync", fd)

        def f(t):
            for j in range(4):
                last = t.matmul(p4[j][0:2, 0:512], eselt[:], gbsb[:, j * 512:(j + 1) * 512],
                                start=True, stop=True)
            return last
        roundB("tensor", f)

        def f(v):
            for j in range(4):
                last = v.tensor_copy(myg[:, j * 512:(j + 1) * 512], p4[j][0:2, 0:512])
            return last
        roundB("vector", f)

        roundB("tensor", lambda t: t.matmul(p4[0][0:2, 0:2], eselt[:], gbsb[:, 2048:2050],
                                            start=True, stop=True))
        roundB("vector", lambda v: v.tensor_copy(myg[:, 2048:2050], p4[0][0:2, 0:2]))

        def f(v):
            mygv = myg[:, 0:2050].rearrange("p (s c) -> p s c", c=2)
            v.tensor_copy(rscr2[0:2, 0:1025], mygv[:, :, 0])
            last = v.tensor_copy(gbsb[0:2, 0:1025], mygv[:, :, 1])
            return last
        roundB("vector", f)

        def f(t):
            for si in range(8):
                pt, off = p4[si % 4], (si // 4) * 4
                t.transpose(pt[:, off:off + 2], rscr2[0:2, si * 128:(si + 1) * 128],
                            cf[0:2, IDE:IDE + 2])
                last = t.transpose(pt[:, off + 2:off + 4],
                                   gbsb[0:2, si * 128:(si + 1) * 128],
                                   cf[0:2, IDE:IDE + 2])
            return last
        roundB("tensor", f)

        def f(v):
            for si in range(8):
                pt, off = p4[si % 4], (si // 4) * 4
                v.tensor_copy(idcols[:, si * 2:si * 2 + 2], pt[:, off:off + 2])
                last = v.tensor_copy(wcols[:, si * 2:si * 2 + 2], pt[:, off + 2:off + 4])
            return last
        roundB("vector", f)

        def f(v):
            for j in range(16):
                last = v.tensor_scalar(rscr[:, j:j + 1], idcols[:, j:j + 1], 0.0, BIGF,
                                       ALU.is_equal, ALU.mult)
            return last
        roundB("vector", f)

        def f(v):
            for j in range(16):
                last = v.tensor_tensor(rscr[:, 16 + j:17 + j], rscr[:, j:j + 1],
                                       idcols[:, j:j + 1], ALU.add)
            return last
        roundB("vector", f)

        def f(v):
            for j in range(16):
                last = v.tensor_scalar_add(hidxf[:, j:j + 1], rscr[:, 16 + j:17 + j], -1.0)
            return last
        roundB("vector", f)
        roundB("vector", lambda v: v.tensor_copy(hidx32[:], hidxf[:]))

        NB_TOTAL = len(BQ)

        scope("P6_shared")
        swgb = [aab(16 * 128) for _ in range(2)]
        swub = [aab(16 * 128) for _ in range(2)]
        smid = aab(NSB * 1024)
        swdb = [aab(NSB * 512) for _ in range(2)]
        P6_END = _aoff[0]

        FF_CNT = [0]
        FT_CNT = [0]

        def p6_dma(isb, par):
            sem = sFF if par == 0 else sFT
            mmw = [0]
            def fd(gp, isb=isb, par=par, sem=sem):
                gp.wait_ge(rq, RN[0])
                if mmw[0]:
                    gp.wait_ge(sMM, mmw[0])
                gp.dma_start(out=swgb[par].rearrange("p (c n) -> p c n", n=128),
                             in_=swg.rearrange("(c p) n -> p c n", p=128)[:, :, isb * 128:(isb + 1) * 128]
                             ).then_inc(sem, 16)
                gp.dma_start(out=swub[par].rearrange("p (c n) -> p c n", n=128),
                             in_=swu.rearrange("(c p) n -> p c n", p=128)[:, :, isb * 128:(isb + 1) * 128]
                             ).then_inc(sem, 16)
                gp.engine_nop().then_inc(rq, 1)
                RN[0] += 1
            mmw[0] = MMC[0]
            block.gpsimd(fd)
            (FF_CNT if par == 0 else FT_CNT)[0] += 32

        sMM = sFV
        sSI = sFS
        sMD = sFQ
        MMC, SIC, MDC = [0], [0], [0]
        MD_OF_SILU = []
        RQ_H6 = RN[0]

        def p6_mmh(isb, half):
            par = isb % 2
            sem = sFF if par == 0 else sFT
            wv = (FF_CNT if par == 0 else FT_CNT)[0]
            r = MMC[0]
            def f(t, isb=isb, half=half, par=par, sem=sem, wv=wv, r=r):
                if r == 0:
                    t.wait_ge(rq, RQ_H6)
                t.wait_ge(sem, wv)
                if r >= 2:
                    t.wait_ge(sMD, r - 1)
                for hb in range(NHB):
                    t.matmul(pw[0][:, (r % 2) * 512:(r % 2 + 1) * 512],
                             swgb[par][:, hb * 128:(hb + 1) * 128],
                             h_fmb[:, hb * TOWN + half * 512:hb * TOWN + (half + 1) * 512],
                             start=(hb == 0), stop=(hb == NHB - 1))
                    mmu = t.matmul(pw[1][:, (r % 2) * 512:(r % 2 + 1) * 512],
                                   swub[par][:, hb * 128:(hb + 1) * 128],
                                   h_fmb[:, hb * TOWN + half * 512:hb * TOWN + (half + 1) * 512],
                                   start=(hb == 0), stop=(hb == NHB - 1))
                mmu.then_inc(sMM, 1)
            block.tensor(f)
            MMC[0] += 1
            return r

        def gu_epih(rmm, dst_mid, extra_rq=None):
            rs = SIC[0]
            war = MD_OF_SILU[rs - 2] + 1 if rs >= 2 else 0
            hh = rmm % 2
            def fs(sc, rmm=rmm, rs=rs, war=war, hh=hh):
                if extra_rq is not None:
                    sc.wait_ge(rq, extra_rq)
                sc.wait_ge(sMM, rmm + 1)
                if war:
                    sc.wait_ge(sMD, war)
                sc.activation(scr2[:, hh * 512:(hh + 1) * 512],
                              pw[0][:, hh * 512:(hh + 1) * 512],
                              AF.Silu).then_inc(sSI, 1)
                for _ in range(6):
                    sc.nop()
            block.scalar(fs)
            SIC[0] += 1

            def fv(v, rs=rs, hh=hh, dst_mid=dst_mid):
                v.wait_ge(sSI, rs + 1)
                v.tensor_tensor(dst_mid, scr2[:, hh * 512:(hh + 1) * 512],
                                pw[1][:, hh * 512:(hh + 1) * 512],
                                ALU.mult).then_inc(sMD, 1)
                for _ in range(6):
                    v.engine_nop()
            block.vector(fv)
            MD_OF_SILU.append(MDC[0])
            MDC[0] += 1

        def p6_epih(isb, half):
            gu_epih(2 * isb + half,
                    smid[:, (2 * isb + half) * 512:(2 * isb + half + 1) * 512])

        B_SCHED = [7, 7, 7, 7, 8, 5, 3, 2, 8, 2, 11, 3, 1] + [0] * (NSB - 13)

        p6_dma(0, 0)
        p6_dma(1, 1)
        emitB(5)
        for isb in range(NSB):
            p6_mmh(isb, 0)
            p6_mmh(isb, 1)
            if isb + 2 < NSB:
                p6_dma(isb + 2, isb % 2)
            p6_epih(isb, 0)
            p6_epih(isb, 1)
            emitB(B_SCHED[isb])

        if debug:
            def fjs(v):
                v.wait_ge(sMD, MDC[0])
                return v.engine_nop()
            round_("vector", fjs)
        tap("smid", smid[:], 128, BF16)

        def p6d_dma(hb2, par, first=False):
            sem = sFF if par == 0 else sFT
            mmw = 0 if first else MMC[0]
            def fd(gp, hb2=hb2, sem=sem, mmw=mmw):
                gp.wait_ge(rq, RN[0])
                if mmw:
                    gp.wait_ge(sMM, mmw)
                for isb in range(NSB):
                    gp.dma_start(out=swdb[par][:, isb * 512:(isb + 1) * 512],
                                 in_=swd[isb * 128:(isb + 1) * 128, hb2 * 512:(hb2 + 1) * 512]
                                 ).then_inc(sem, 16)
                gp.engine_nop().then_inc(rq, 1)
                RN[0] += 1
            block.gpsimd(fd)
            (FF_CNT if par == 0 else FT_CNT)[0] += NSB * 16

        def down_rounds(src_mid, src_wd, nblk, sem, wv, smid_rdy, store_to, wcol=None):
            r = MMC[0]
            def f(t, r=r, sem=sem, wv=wv, smid_rdy=smid_rdy):
                t.wait_ge(sem, wv)
                t.wait_ge(sMD, max(smid_rdy, r - 1))
                pt = pw[r % 2]
                for ib in range(nblk):
                    mmv = t.matmul(pt[:, 0:512], src_mid(ib), src_wd(ib),
                                   start=(ib == 0), stop=(ib == nblk - 1))
                mmv.then_inc(sMM, 1)
            block.tensor(f)
            MMC[0] += 1

            r2 = MDC[0]
            rqe = RN[0]
            def fv(v, r2=r2, rqe=rqe):
                v.wait_ge(sMM, r2 + 1)
                v.wait_ge(rq, rqe)
                if wcol is None:
                    out = scr2.bitcast(BF16)[:, (r2 % 2) * 512:(r2 % 2 + 1) * 512]
                    v.tensor_copy(out, pw[r2 % 2][:, 0:512]).then_inc(sMD, 1)
                else:
                    out = eoutst[:, (r2 % 2) * 512:(r2 % 2 + 1) * 512]
                    v.tensor_scalar(out, pw[r2 % 2][:, 0:512], wcol, 0.0,
                                    ALU.mult, ALU.add).then_inc(sMD, 1)
                for _ in range(6):
                    v.engine_nop()
            block.vector(fv)
            MDC[0] += 1

            def fd2(sync, r2=r2):
                sync.wait_ge(sMD, r2 + 1)
                sync.dma_start(out=h0_dram[0:1, 0:3],
                               in_=fcon[0:1, 0:3],
                               allow_slow_non_contiguous=True).then_inc(rq, 16)
                if wcol is None:
                    src = scr2.bitcast(BF16)[:, (r2 % 2) * 512:(r2 % 2 + 1) * 512]
                else:
                    src = eoutst[:, (r2 % 2) * 512:(r2 % 2 + 1) * 512]
                sync.dma_start(out=store_to, in_=src).then_inc(rq, 16)
                RN[0] += 32
            block.sync(fd2)

        p6d_dma(0, 0, first=True)
        p6d_dma(1, 1, first=True)
        SMID_RDY = MDC[0]
        for hb2 in range(4):
            par = hb2 % 2
            sem = sFF if par == 0 else sFT
            wv = (FF_CNT if par == 0 else FT_CNT)[0]
            for tb in range(8):
                down_rounds(
                    (lambda ib, tb=tb: smid[:, ib * 1024 + tb * 128:ib * 1024 + (tb + 1) * 128]),
                    (lambda ib, par=par: swdb[par][:, ib * 512:(ib + 1) * 512]),
                    NSB, sem, wv, SMID_RDY,
                    ysh_dram[tb * 128:(tb + 1) * 128, hb2 * 512:(hb2 + 1) * 512])
            if hb2 + 2 < 4:
                p6d_dma(hb2 + 2, par)
            if hb2 >= 2:
                emitB(6)
        emitB(len(BQ))
        RBF = RB[0]

        if debug:
            def fjoin(sync):
                sync.wait_ge(rB, RBF)
            block.sync(fjoin)
            tap("lgfm", lgfm[:], 16)
            tap("h_fmb", h_fmb[:], 128, BF16)
            tap("lgtm", lgtm[:])
            tap("vals", vals[:])
            tap("idx4", idx4[:])
            tap("w4", w4[:])
            tap("maskg", maskg[:])
            tap("cntrow", cntrow[:], 1)
            tap("carrow", carrow[:], 1)
            tap("rankg", rankg[:])
            tap("slot4", slot4[:])
            tap("destf", destf[:])
            tap("gidxf", gidxf[:])
            tap("gbsb", gbsb[:], 16)
            tap("myg", myg[:], 2)
            tap("idcols", idcols[:])
            tap("wcols", wcols[:])
            tap("hidxf", hidxf[:])
            tap("pairsA", pairsA[:])

        scope("P7_moe")
        _aoff[0] = FFN_MARK
        xg_fm = aab(NHB * 1024)
        xg_tm = aab(2 * HD)
        wgb = [aab(16 * 128) for _ in range(2)]
        wub = [aab(16 * 128) for _ in range(2)]
        wdbuf = [aab(NIB * 512) for _ in range(2)]
        midb = [aab(NIB * 1024) for _ in range(2)]
        eoutst = rscr2.bitcast(BF16)[:, 0:1024]

        for e01 in range(2):
            mm_e0 = MMC[0]
            for si in range(8):
                def f(v, si=si, e01=e01, mm_e0=mm_e0):
                    if e01 == 1 and si == 0:
                        v.wait_ge(sMM, mm_e0)
                    return v.memset(xg_tm[:, (si % 2) * HD:(si % 2 + 1) * HD], 0.0)
                round_("vector", f)

                def fg(gp, e01=e01, si=si):
                    gp.wait_ge(rq, RN[0])
                    gp.wait_ge(sCC, 1)
                    if e01 == 0 and si == 0:
                        gp.wait_ge(rB, RBF)
                    breg = gp.to_reg(NCORE * TOWN - 1)
                    gp.indirect_dma_start(
                        xg_tm[:, (si % 2) * HD:(si % 2 + 1) * HD], None, h_full[:],
                        IndirectOffsetOnAxis(hidx32[:, si * 2 + e01:si * 2 + e01 + 1], 0),
                        bounds_check=breg, oob_is_err=False,
                    ).then_inc(qG, 16)
                    QG[0] += 16
                    gp.engine_nop().then_inc(rq, 1)
                    RN[0] += 1
                block.gpsimd(fg)
                QGXG = QG[0]

                def f(t, si=si, e01=e01, QGXG=QGXG):
                    t.wait_ge(qG, QGXG)
                    if e01 == 0 and si == 0:
                        t.wait_ge(rB, RBF)
                    for hb in range(NHB):
                        g = hb // 4
                        last = t.transpose(p4[g].bitcast(BF16)[:, (hb % 4) * 256:(hb % 4) * 256 + 128],
                                           xg_tm[:, (si % 2) * HD + hb * 128:(si % 2) * HD + (hb + 1) * 128],
                                           cb[:, 0:128])
                    return last
                round_("tensor", f)

                def f(v, si=si):
                    for g in range(4):
                        dst = xg_fm.rearrange("p (hb t) -> p hb t", t=1024)[:, 4 * g:4 * g + 4,
                                                                           si * 128:(si + 1) * 128]
                        src = p4[g].bitcast(BF16).rearrange("p (k t) -> p k t", t=256)[:, :, 0:128]
                        last = v.tensor_copy(dst, src)
                    return last
                round_("vector", f)

            if e01 == 0:
                tap("xg_fm_e0", xg_fm[:], 128, BF16)

            def p7_dma(ib, par, e01=e01):
                sem = sFF if par == 0 else sFT
                mmw = MMC[0]
                def fd(gp, ib=ib, par=par, sem=sem, mmw=mmw):
                    gp.wait_ge(rq, RN[0])
                    if mmw:
                        gp.wait_ge(sMM, mmw)
                    gp.dma_start(out=wgb[par].rearrange("p (c n) -> p c n", n=128),
                                 in_=wg_my[e01].rearrange("(c p) n -> p c n", p=128)[:, :, ib * 128:(ib + 1) * 128]
                                 ).then_inc(sem, 16)
                    gp.dma_start(out=wub[par].rearrange("p (c n) -> p c n", n=128),
                                 in_=wu_my[e01].rearrange("(c p) n -> p c n", p=128)[:, :, ib * 128:(ib + 1) * 128]
                                 ).then_inc(sem, 16)
                    gp.engine_nop().then_inc(rq, 1)
                    RN[0] += 1
                block.gpsimd(fd)
                (FF_CNT if par == 0 else FT_CNT)[0] += 32

            def p7_mmh(ib, half, e01=e01):
                par = ib % 2
                sem = sFF if par == 0 else sFT
                wv = (FF_CNT if par == 0 else FT_CNT)[0]
                r = MMC[0]
                rqx = RN[0]
                mdw = MDC[0]
                def f(t, ib=ib, half=half, par=par, sem=sem, wv=wv, r=r, rqx=rqx,
                      mdw=mdw, e01=e01):
                    if ib == 0 and half == 0:
                        t.wait_ge(rq, rqx)
                        if e01 == 0:
                            t.wait_ge(sMD, mdw)
                    t.wait_ge(sem, wv)
                    if r >= 2:
                        t.wait_ge(sMD, r - 1)
                    for hb in range(NHB):
                        t.matmul(pw[0][:, (r % 2) * 512:(r % 2 + 1) * 512],
                                 wgb[par][:, hb * 128:(hb + 1) * 128],
                                 xg_fm[:, hb * 1024 + half * 512:hb * 1024 + (half + 1) * 512],
                                 start=(hb == 0), stop=(hb == NHB - 1))
                        mmu = t.matmul(pw[1][:, (r % 2) * 512:(r % 2 + 1) * 512],
                                       wub[par][:, hb * 128:(hb + 1) * 128],
                                       xg_fm[:, hb * 1024 + half * 512:hb * 1024 + (half + 1) * 512],
                                       start=(hb == 0), stop=(hb == NHB - 1))
                    mmu.then_inc(sMM, 1)
                block.tensor(f)
                MMC[0] += 1
                return r

            p7_dma(0, 0)
            p7_dma(1, 1)
            for ib in range(NIB):
                ra = p7_mmh(ib, 0)
                rb_ = p7_mmh(ib, 1)
                if ib + 2 < NIB:
                    p7_dma(ib + 2, ib % 2)
                gu_epih(ra, midb[e01][:, ib * 1024:ib * 1024 + 512],
                        extra_rq=(RN[0] if (e01 == 0 and ib == 0) else None))
                gu_epih(rb_, midb[e01][:, ib * 1024 + 512:ib * 1024 + 1024])

        def p7d_dma(s, first=False):
            hb2, e01, par = s >> 1, s & 1, s % 2
            sem = sFF if par == 0 else sFT
            mmw = 0 if first else MMC[0]
            def fd(gp, hb2=hb2, e01=e01, par=par, sem=sem, mmw=mmw):
                gp.wait_ge(rq, RN[0])
                if mmw:
                    gp.wait_ge(sMM, mmw)
                for ib in range(NIB):
                    gp.dma_start(out=wdbuf[par][:, ib * 512:(ib + 1) * 512],
                                 in_=wd_my[e01][ib * 128:(ib + 1) * 128, hb2 * 512:(hb2 + 1) * 512]
                                 ).then_inc(sem, 16)
                gp.engine_nop().then_inc(rq, 1)
                RN[0] += 1
            block.gpsimd(fd)
            (FF_CNT if par == 0 else FT_CNT)[0] += NIB * 16

        p7d_dma(0, first=True)
        p7d_dma(1, first=True)
        MID_RDY = MDC[0]
        for s in range(8):
            hb2, e01, par = s >> 1, s & 1, s % 2
            sem = sFF if par == 0 else sFT
            wv = (FF_CNT if par == 0 else FT_CNT)[0]
            for sb in range(8):
                down_rounds(
                    (lambda ib, e01=e01, sb=sb: midb[e01][:, ib * 1024 + sb * 128:ib * 1024 + (sb + 1) * 128]),
                    (lambda ib, par=par: wdbuf[par][:, ib * 512:(ib + 1) * 512]),
                    NIB, sem, wv, MID_RDY,
                    eout_myh[hb2][e01 * CAP + sb * 128:e01 * CAP + (sb + 1) * 128, :],
                    wcol=wcols[:, sb * 2 + e01:sb * 2 + e01 + 1])
            if s + 2 < 8:
                p7d_dma(s + 2)
            if e01 == 1:
                def fg(gp, hb2=hb2):
                    gp.wait_ge(rq, RN[0])
                    gp.collective_compute("AllGather", ALU.bypass,
                                          replica_groups=[list(range(NCORE))],
                                          ins=[eout_myh[hb2].opt()],
                                          outs=[eout_allh[hb2].opt()]).then_inc(sCC, 1)
                block.gpsimd(fg)

        def fg_drain(gp):
            gp.collective_compute("AllGather", ALU.bypass,
                                  replica_groups=[list(range(NCORE))],
                                  ins=[cnt_my.opt()],
                                  outs=[cnt_all.opt()]).then_inc(sCC, 1)
        block.gpsimd(fg_drain)

        scope("P8_final")
        _aoff[0] = FFN_MARK
        y4f = aa(8 * HD)
        y4s = aa(4 * HD)
        y4sb = y4s.bitcast(BF16)
        stats = aa(8 * 16)
        htile = y4s[:, 0:2 * HD]
        ysht = y4sb[:, 4 * HD:6 * HD]

        def f(v):
            for q in range(4):
                last = v.memset(y4f[:, q * HD:(q + 1) * HD], 0.0)
            return last
        round_("vector", f)

        def f(v):
            for q in range(4, 8):
                last = v.memset(y4f[:, q * HD:(q + 1) * HD], 0.0)
            return last
        round_("vector", f)

        for t0 in (0, 2, 4, 6):
            def fd(sync, t0=t0):
                sync.wait_ge(rq, RN[0])
                for j in range(2):
                    tt = t0 + j
                    sync.dma_start(out=htile[:, j * HD:(j + 1) * HD],
                                   in_=htm_f32[tt * 128:(tt + 1) * 128, :]).then_inc(rq, 16)
                    sync.dma_start(out=ysht[:, j * HD:(j + 1) * HD],
                                   in_=ysh_dram[tt * 128:(tt + 1) * 128, :]).then_inc(rq, 16)
                RN[0] += 64
            block.sync(fd)

            if t0 == 0:
                tap("ysht0", ysht[:, 0:HD], 128, BF16)
                tap("htile0", htile[:, 0:HD])

            def f(v, t0=t0):
                v.tensor_tensor(y4f[:, t0 * HD:(t0 + 1) * HD],
                                y4f[:, t0 * HD:(t0 + 1) * HD], htile[:, 0:HD], ALU.add)
                return v.tensor_tensor(y4f[:, (t0 + 1) * HD:(t0 + 2) * HD],
                                       y4f[:, (t0 + 1) * HD:(t0 + 2) * HD],
                                       htile[:, HD:2 * HD], ALU.add)
            round_("vector", f)

            def f(v, t0=t0):
                v.tensor_tensor(y4f[:, t0 * HD:(t0 + 1) * HD],
                                y4f[:, t0 * HD:(t0 + 1) * HD], ysht[:, 0:HD], ALU.add)
                return v.tensor_tensor(y4f[:, (t0 + 1) * HD:(t0 + 2) * HD],
                                       y4f[:, (t0 + 1) * HD:(t0 + 2) * HD],
                                       ysht[:, HD:2 * HD], ALU.add)
            round_("vector", f)

        round_("vector", lambda v: v.memset(y4s[:, 0:2 * HD], 0.0))
        round_("vector", lambda v: v.memset(y4s[:, 2 * HD:4 * HD], 0.0))

        for h in range(4):
            def fg(gp, h=h):
                gp.wait_ge(rq, RN[0])
                gp.wait_ge(sCC, 6)
                breg = gp.to_reg(EXP * CAP - 1)
                for tt in range(8):
                    for k in range(4):
                        s = tt * 4 + k
                        gp.indirect_dma_start(
                            y4sb[:, s * 512:(s + 1) * 512],
                            None, eout_allh[h][:],
                            IndirectOffsetOnAxis(gidx32[:, tt * 4 + k:tt * 4 + k + 1], 0),
                            bounds_check=breg, oob_is_err=False,
                        ).then_inc(qG, 16)
                QG[0] += 32 * 16
                gp.engine_nop().then_inc(rq, 1)
                RN[0] += 1
            block.gpsimd(fg)

            wv = QG[0]
            def fa(v, h=h, wv=wv):
                v.wait_ge(qG, wv)
                for k in range(4):
                    for tt in range(8):
                        s = tt * 4 + k
                        v.tensor_tensor(
                            y4f[:, tt * HD + h * 512:tt * HD + (h + 1) * 512],
                            y4f[:, tt * HD + h * 512:tt * HD + (h + 1) * 512],
                            y4sb[:, s * 512:(s + 1) * 512],
                            ALU.add)
                for q in range(4):
                    last = v.memset(y4s[:, q * HD:(q + 1) * HD], 0.0)
                return last
            round_("vector", fa)

        if debug:
            tap("y4a0", y4f[:, 0:HD])

        def f(v):
            for tt in range(8):
                last = v.tensor_reduce(stats[:, tt * 16 + 0:tt * 16 + 1],
                                       y4f[:, tt * HD:(tt + 1) * HD], AX.X, ALU.add)
            return last
        round_("vector", f)

        def f2(sc):
            for tt in range(4):
                last = sc.activation(y4s[:, tt * HD:(tt + 1) * HD],
                                     y4f[:, tt * HD:(tt + 1) * HD], AF.Square,
                                     accum_out=stats[:, tt * 16 + 1:tt * 16 + 2])
            return last
        round_("scalar", f2)

        def f2(sc):
            for tt in range(4, 8):
                last = sc.activation(y4s[:, (tt - 4) * HD:(tt - 3) * HD],
                                     y4f[:, tt * HD:(tt + 1) * HD], AF.Square,
                                     accum_out=stats[:, tt * 16 + 1:tt * 16 + 2])
            return last
        round_("scalar", f2)

        def f(v):
            for tt in range(8):
                v.tensor_scalar_mul(stats[:, tt * 16 + 2:tt * 16 + 3],
                                    stats[:, tt * 16 + 0:tt * 16 + 1], 1.0 / HD)
                last = v.tensor_scalar_mul(stats[:, tt * 16 + 3:tt * 16 + 4],
                                           stats[:, tt * 16 + 1:tt * 16 + 2], 1.0 / HD)
            return last
        round_("vector", f)

        def f(v):
            for tt in range(8):
                last = v.tensor_tensor(stats[:, tt * 16 + 4:tt * 16 + 5],
                                       stats[:, tt * 16 + 2:tt * 16 + 3],
                                       stats[:, tt * 16 + 2:tt * 16 + 3], ALU.mult)
            return last
        round_("vector", f)

        def f(v):
            for tt in range(8):
                last = v.tensor_tensor(stats[:, tt * 16 + 5:tt * 16 + 6],
                                       stats[:, tt * 16 + 3:tt * 16 + 4],
                                       stats[:, tt * 16 + 4:tt * 16 + 5], ALU.subtract)
            return last
        round_("vector", f)

        def f(sc):
            for tt in range(8):
                last = sc.activation(stats[:, tt * 16 + 6:tt * 16 + 7],
                                     stats[:, tt * 16 + 5:tt * 16 + 6], AF.Ln, bias=LN_EPS)
            return last
        round_("scalar", f)

        def f(sc):
            for tt in range(8):
                last = sc.activation(stats[:, tt * 16 + 7:tt * 16 + 8],
                                     stats[:, tt * 16 + 6:tt * 16 + 7], AF.Exp, scale=-0.5)
            return last
        round_("scalar", f)

        def f(v):
            for tt in range(8):
                last = v.tensor_scalar(y4f[:, tt * HD:(tt + 1) * HD],
                                       y4f[:, tt * HD:(tt + 1) * HD],
                                       stats[:, tt * 16 + 2:tt * 16 + 3],
                                       stats[:, tt * 16 + 7:tt * 16 + 8],
                                       ALU.subtract, ALU.mult)
            return last
        round_("vector", f)

        for _ in range(2):
            round_("vector", lambda v: v.engine_nop())

        def fd(sync):
            sync.wait_ge(rq, RN[0])
            for tt in range(8):
                sync.dma_start(out=out_own[tt * 128:(tt + 1) * 128, :],
                               in_=y4f[:, tt * HD:(tt + 1) * HD]).then_inc(rq, 16)
            RN[0] += 128
        block.sync(fd)

        tap("statsall", stats[:])

        def fin(sync):
            sync.wait_ge(rq, RN[0])
        block.sync(fin)
        scope(None)

    return nc


def _consts():
    c = {}
    c["ident_f"] = np.eye(128, dtype=np.float32)
    c["ident_b"] = np.eye(128, dtype=np.float32)
    c["m_triuinc"] = np.triu(np.ones((128, 128), np.float32))
    c["m_striu"] = np.triu(np.ones((128, 128), np.float32), 1)
    c["m_striun"] = -np.triu(np.ones((128, 128), np.float32), 1)
    c["m_striln"] = -np.tril(np.ones((128, 128), np.float32), -1)
    c["onescol"] = np.ones((128, 1), np.float32)
    e = np.zeros((128, 1), np.float32); e[127, 0] = 1.0
    c["e127c"] = e
    c["iota128"] = np.arange(128, dtype=np.float32).reshape(128, 1)
    c["iota16g"] = np.tile(np.arange(16, dtype=np.float32), (128, 1))
    c["iota64c"] = np.arange(64, dtype=np.float32).reshape(64, 1)
    c["fconst_d"] = np.tile(np.array([[1e-6, 1e-5, 1e-8]], np.float32), (128, 1))
    return c


def host_inmaps(inputs):
    import ml_dtypes
    x = np.asarray(inputs["L0_attn_x"])[0]
    Wk = np.asarray(inputs["L0_attn_Wk"]); Wv = np.asarray(inputs["L0_attn_Wv"])
    Wq = np.asarray(inputs["L0_attn_Wq"]); Wa = np.asarray(inputs["L0_attn_Wa_w"])
    wab = np.asarray(inputs["L0_attn_Wa_b"]).reshape(1, D)
    lam = np.asarray(inputs["L0_attn_lam"]).reshape(1, D)
    normw = np.asarray(inputs["L0_attn_norm_w"]).reshape(1, D)
    Wo = np.asarray(inputs["L0_attn_Wo"])
    gw = np.asarray(inputs["L0_ffn_gate_w"])
    w_gate = np.asarray(inputs["L0_ffn_w_gate"]); w_up = np.asarray(inputs["L0_ffn_w_up"])
    w_down = np.asarray(inputs["L0_ffn_w_down"])
    swg = np.ascontiguousarray(np.asarray(inputs["L0_ffn_sw_gate"]))
    swu = np.ascontiguousarray(np.asarray(inputs["L0_ffn_sw_up"]))
    swd = np.ascontiguousarray(np.asarray(inputs["L0_ffn_sw_down"]))

    wcatT = np.ascontiguousarray(np.concatenate([Wk, Wv, Wq, Wa], 0).T)
    woT = np.ascontiguousarray(Wo.T)
    gwT = np.ascontiguousarray(gw.T)
    rowp = np.ascontiguousarray(np.concatenate(
        [lam, wab, normw, np.zeros((1, D), np.float32)], 1))
    consts = _consts()

    in_maps = []
    for c in range(NCORE):
        pm = np.zeros((8, 1), np.float32); pm[:c] = 1.0
        sel = np.zeros((16, 2), np.float32)
        sel[2 * c, 0] = 1.0; sel[2 * c + 1, 1] = 1.0
        cv = np.array([[c * TOWN, c * NCH, c, 0]], np.float32)
        m = dict(
            x_own=np.ascontiguousarray(x[c * TOWN:(c + 1) * TOWN]),
            wcatT=wcatT, woT=woT, gwT=gwT,
            wg_my=np.ascontiguousarray(w_gate[2 * c:2 * c + 2]),
            wu_my=np.ascontiguousarray(w_up[2 * c:2 * c + 2]),
            wd_my=np.ascontiguousarray(w_down[2 * c:2 * c + 2]),
            swg=swg, swu=swu, swd=swd,
            rowp=rowp, premask=pm, esel=sel, cvec=cv,
            **consts,
        )
        m["ident_b"] = np.eye(128).astype(ml_dtypes.bfloat16)
        in_maps.append(m)
    return in_maps


def kernel(**inputs):
    in_maps = host_inmaps(inputs)
    nc = build_nc(debug=False)
    res = run_bass_kernel_spmd(nc, in_maps, core_ids=list(range(NCORE)))
    out = np.concatenate([r["out_own"] for r in res.results], 0)
    return out.reshape(1, NCORE * TOWN, HD).astype(np.float32)


# version 37 (final)
# speedup vs baseline: 1.0130x; 1.0121x over previous
import contextlib
import numpy as np
import concourse.bass as bass
import concourse.mybir as mybir
from concourse.bass_utils import run_bass_kernel_spmd
from concourse.bass_core_rust import IndirectOffsetOnAxis

F32 = mybir.dt.float32
BF16 = mybir.dt.bfloat16
I32 = mybir.dt.int32
AF = mybir.ActivationFunctionType
ALU = mybir.AluOpType
AX = mybir.AxisListType

NCORE = 8
TOWN = 1024
NCH = 8
L = 128
D = 64
HD = 2048
NHB = HD // 128
EXP = 16
CAP = 1024
IE = 1408
NIB = IE // 128
ISH = 2816
NSB = ISH // 128
RMS_EPS = 1e-6
LN_EPS = 1e-5
SURSC = 1.0 / (1.0 + 1e-6)
BIGF = 1.0e6


def build_nc(debug=False, poison=False):
    nc = bass.Bass("TRN2", target_bir_lowering=False)
    es = contextlib.ExitStack()

    dr = lambda n, s, d=F32, k="ExternalInput": nc.dram_tensor(n, s, d, kind=k).ap()
    x_own = dr("x_own", [TOWN, HD])
    wcatT = dr("wcatT", [HD, 256])
    woT = dr("woT", [D, HD])
    gwT = dr("gwT", [HD, EXP])
    wg_my = dr("wg_my", [2, HD, IE])
    wu_my = dr("wu_my", [2, HD, IE])
    wd_my = dr("wd_my", [2, IE, HD])
    swg = dr("swg", [HD, ISH])
    swu = dr("swu", [HD, ISH])
    swd = dr("swd", [ISH, HD])
    rowp = dr("rowp", [1, 4 * D])
    ident_f = dr("ident_f", [128, 128])
    ident_b = dr("ident_b", [128, 128], BF16)
    m_triuinc = dr("m_triuinc", [128, 128])
    m_striu = dr("m_striu", [128, 128])
    m_striun = dr("m_striun", [128, 128])
    m_striln = dr("m_striln", [128, 128])
    onescol = dr("onescol", [128, 1])
    e127c = dr("e127c", [128, 1])
    iota128 = dr("iota128", [128, 1])
    iota16g = dr("iota16g", [128, 16])
    iota64c = dr("iota64c", [64, 1])
    premask = dr("premask", [8, 1])
    esel = dr("esel", [16, 2])
    cvec = dr("cvec", [1, 4])
    fconst_d = dr("fconst_d", [128, 3])
    out_own = dr("out_own", [TOWN, HD], F32, "ExternalOutput")
    dbg = {}
    if debug:
        dbg["prep"] = dr("dbg_prep", [TOWN, 256], F32, "ExternalOutput")
        dbg["errs"] = dr("dbg_errs", [TOWN, 1], F32, "ExternalOutput")
        dbg["y"] = dr("dbg_y", [TOWN, D], F32, "ExternalOutput")
        dbg["h"] = dr("dbg_h", [TOWN, HD], F32, "ExternalOutput")
        dbg["rt"] = dr("dbg_rt", [TOWN, 12], F32, "ExternalOutput")
        dbg["gb"] = dr("dbg_gb", [EXP * 1025, 2], F32, "ExternalOutput")

    di = lambda n, s, d=F32: nc.dram_tensor(n, s, d, kind="Internal").ap()
    ds = lambda n, s, d=F32: nc.dram_tensor(n, s, d, kind="Internal",
                                            addr_space="Shared").ap()
    ab_my = di("ab_my", [NCH * D, 130])
    ab_allp = [ds(f"ab_all{p}", [NCORE * NCH * D, 130]) for p in range(2)]
    h0_dram = di("h0_dram", [64 * D, D])
    htm_bf_my = di("htm_bf_my", [TOWN, HD], BF16)
    htm_f32 = di("htm_f32", [TOWN, HD])
    ysh_dram = di("ysh_dram", [TOWN, HD], BF16)
    h_full = di("h_full", [NCORE * TOWN, HD], BF16)
    cnt_my = di("cnt_my", [1, EXP])
    cnt_all = ds("cnt_all", [NCORE, EXP])
    gbuf_my = di("gbuf_my", [EXP * 1025, 2])
    gbuf_all = ds("gbuf_all", [EXP * 1025, 2])
    eout_myh = [di(f"eout_myh{h}", [2 * CAP, 512], BF16) for h in range(4)]
    eout_allh = [di(f"eout_allh{h}", [EXP * CAP, 512], BF16) for h in range(4)]

    with es:
        sbA = lambda n, s, d=F32: es.enter_context(nc.sbuf_tensor(n, s, d))
        psA = lambda n, s, d=F32: es.enter_context(nc.psum_tensor(n, s, d))

        cf = sbA("cf", [128, 128 * 5])
        cb = sbA("cb", [128, 128], BF16)
        onesc = sbA("onesc", [128, 1])
        e127 = sbA("e127", [128, 1])
        iotac = sbA("iotac", [128, 1])
        iota16 = sbA("iota16", [128, 16])
        iota64 = sbA("iota64", [64, 1])
        prem = sbA("prem", [8, 1])
        eselt = sbA("eselt", [16, 2])
        cvt = sbA("cvt", [1, 4])
        rowpar = sbA("rowpar", [1, 4 * D])
        lnsig = sbA("lnsig", [1, D])
        bc64 = sbA("bc64", [128, 2 * D])
        fcon = sbA("fcon", [128, 3])
        gw_sb = sbA("gw_sb", [128, NHB * EXP])
        lgfm = sbA("lgfm", [16, TOWN])
        desti32 = sbA("desti32", [128, 32], I32)
        gidx32 = sbA("gidx32", [128, 32], I32)
        hidx32 = sbA("hidx32", [128, 16], I32)
        fmb = sbA("fmb", [128, NHB * TOWN], BF16)
        stS = sbA("stS", [128, NCH])
        errs = sbA("errs", [128, NCH])
        glc = sbA("glc", [64, NCH])
        idxg = sbA("idxg", [64, NCH], I32)
        scr = sbA("scr", [128, 1024])
        scr2 = sbA("scr2", [128, 2048])

        ARENA_COLS = 39300
        arena = sbA("arena", [128, ARENA_COLS])
        arena_bf = arena.bitcast(BF16)
        _aoff = [0]

        def aa(cols):
            s = arena[:, _aoff[0]:_aoff[0] + cols]
            _aoff[0] += cols
            assert _aoff[0] <= ARENA_COLS, f"arena overflow {_aoff[0]}"
            return s

        def aab(cols_bf):
            assert cols_bf % 2 == 0
            s = arena_bf[:, 2 * _aoff[0]:2 * _aoff[0] + cols_bf]
            _aoff[0] += cols_bf // 2
            return s

        def arena_reset():
            _aoff[0] = 0

        x_fm = aa(NHB * TOWN)
        x_fmb = fmb
        h_fmb = fmb
        MARK0 = _aoff[0]
        wcat_sb = aab(NHB * 256)
        xstg = aa(HD)
        proj = aa(NCH * 256)
        _aoff[0] = MARK0 + 6600
        stKV = aa(NCH * 128)
        stQt = aa(NCH * D)
        stQh = aa(NCH * D)
        stU0 = aa(NCH * D)
        stUc = aa(NCH * D)
        stG = aa(NCH * D)
        stLA = aa(NCH * D)
        fmQt = aa(NCH * L)[0:64, :]
        fmKt = aa(NCH * L)[0:64, :]
        fmQh = aa(NCH * L)[0:64, :]
        fmU = aa(NCH * L)[0:64, :]
        fmZ = aa(NCH * L)[0:64, :]
        solU = aa(NCH * L)
        solW = aa(NCH * L)
        solR = aa(NCH * 128)
        gP = aa(NCH * L)
        stM = aa(NCH * D)
        abATs = aa(NCH * D)[0:64, :]
        abBgs = aa(NCH * D)[0:64, :]
        h0my = aa(NCH * D)[0:64, :]
        SCAN_ARENA_END = _aoff[0]
        _aoff[0] = MARK0
        h0ping = aa(2 * D)[0:64, :]
        chABt = aa(2 * 130)[0:64, :]
        yattn = aa(NCH * D)
        ynfm = aab(TOWN)[0:64, :]
        wo_fm = aab(HD)[0:64, :]
        nwcol = aa(1)[0:64, :]
        rowa = aa(2048)[0:1, :]
        hstg_bf = aab(2 * HD)
        assert _aoff[0] <= MARK0 + 6600, f"overlay overflow {_aoff[0] - MARK0}"
        _aoff[0] = SCAN_ARENA_END

        p4 = [psA(f"p4_{i}", [128, 512]) for i in range(4)]
        pw = [psA(f"pw_{i}", [128, 1024]) for i in range(2)]

        def pslot(c):
            if c < 4:
                return p4[c], 0
            return pw[(c - 4) // 2], ((c - 4) % 2) * 512

        rq = es.enter_context(nc.semaphore("rq"))
        RN = [0]
        qG = es.enter_context(nc.semaphore("qG"))
        QG = [0]
        sXin = es.enter_context(nc.semaphore("sXin"))
        sTx = es.enter_context(nc.semaphore("sTx"))
        sVx = es.enter_context(nc.semaphore("sVx"))
        sFF = es.enter_context(nc.semaphore("sFF"))
        sFT = es.enter_context(nc.semaphore("sFT"))
        sFV = es.enter_context(nc.semaphore("sFV"))
        sFS = es.enter_context(nc.semaphore("sFS"))
        sFQ = es.enter_context(nc.semaphore("sFQ"))
        sEO = es.enter_context(nc.semaphore("sEO"))
        sCC = es.enter_context(nc.semaphore("sCC"))

        TAPS = {}

        nc.const_aps.register(F32, RMS_EPS, fcon[:, 0:1])
        nc.const_aps.register(F32, LN_EPS, fcon[:, 1:2])
        nc.const_aps.register(F32, 1e-8, fcon[:, 2:3])
        block = es.enter_context(nc.Block())
        IDE, TRI, SUU, SUN, SLN = 0, 128, 256, 384, 512

        def tap(name, ap, parts=128, dt=F32):
            if not debug:
                return
            cols = ap.shape[-1]
            t_d = nc.dram_tensor("tap_" + name, [parts, cols], dt, kind="ExternalOutput").ap()
            TAPS[name] = t_d

            def fdma(sync, ap=ap, t_d=t_d):
                sync.wait_ge(rq, RN[0])
                sync.dma_start(out=t_d[:], in_=ap, allow_slow_non_contiguous=True).then_inc(rq, 16)
                RN[0] += 16
            block.sync(fdma)

        def round_(engname, fn, inc=1):
            def body(eng):
                eng.wait_ge(rq, RN[0])
                last = fn(eng)
                last.then_inc(rq, inc)
                if engname == "vector":
                    for _ in range(6):
                        eng.engine_nop()
                elif engname == "scalar":
                    for _ in range(6):
                        eng.nop()
            getattr(block, engname)(body)
            RN[0] += inc

        if poison:
            PV = 1.0e38
            q4 = ARENA_COLS // 4
            for i in range(4):
                a0, a1 = i * q4, (ARENA_COLS if i == 3 else (i + 1) * q4)
                round_("vector", lambda v, a0=a0, a1=a1: v.memset(arena[:, a0:a1], PV))
            for tns in (scr, scr2, gw_sb, lgfm, stS, errs, glc, fcon, bc64, lnsig,
                        rowpar, cvt, eselt, prem, iota64, iota16, iotac, e127,
                        onesc, cf):
                round_("vector", lambda v, t_=tns: v.memset(t_[:], PV))
            round_("vector", lambda v: v.memset(cb[:], PV))
            round_("vector", lambda v: v.memset(fmb[:], PV))
            for it in (desti32, gidx32, hidx32, idxg):
                round_("vector", lambda v, t_=it: v.memset(t_[:], 1 << 30))
            for pt in p4 + pw:
                round_("vector", lambda v, t_=pt: v.memset(t_[:], PV))

            def poison_dram(sync):
                sync.wait_ge(rq, RN[0])
                n = 0
                s2b = scr2.bitcast(BF16)
                for t_, isbf in ((ab_my, 0), (ab_allp[0], 0), (ab_allp[1], 0),
                                 (h0_dram, 0), (htm_f32, 0),
                                 (cnt_my, 0), (cnt_all, 0), (gbuf_my, 0), (gbuf_all, 0),
                                 (htm_bf_my, 1), (ysh_dram, 1), (h_full, 1),
                                 *[(t, 1) for t in eout_myh],
                                 *[(t, 1) for t in eout_allh]):
                    rows, cols = int(t_.shape[0]), int(t_.shape[1])
                    src = s2b if isbf else scr2
                    for r0 in range(0, rows, 128):
                        r1 = min(rows, r0 + 128)
                        sync.dma_start(out=t_[r0:r1, :], in_=src[0:r1 - r0, 0:cols],
                                       allow_slow_non_contiguous=True).then_inc(rq, 16)
                        n += 16
                RN[0] += n
            block.sync(poison_dram)

        _scopes = []

        def scope(name):
            if _scopes:
                pn, sid = _scopes.pop()
                nc.leave_named_scope(pn, sid, False)
            if name is not None:
                sid, _ = nc.enter_named_scope(name, False)
                _scopes.append((name, sid))

        scope("P01_proj")

        def p0_consts(sync):
            sync.wait_ge(rq, RN[0])
            n = 0
            for dst, src in [(cf[:, 0:128], ident_f), (cf[:, 128:256], m_triuinc),
                             (cf[:, 256:384], m_striu), (cf[:, 384:512], m_striun),
                             (cf[:, 512:640], m_striln), (cb[:, 0:128], ident_b),
                             (onesc[:], onescol), (e127[:], e127c), (iotac[:], iota128),
                             (iota16[:], iota16g), (iota64[:], iota64c), (prem[:], premask),
                             (eselt[:], esel), (cvt[:], cvec), (rowpar[:], rowp),
                             (fcon[:], fconst_d)]:
                sync.dma_start(out=dst, in_=src[:]).then_inc(rq, 16)
                n += 16
            RN[0] += n
        block.sync(p0_consts)

        def p0_wcat(gp):
            gp.wait_ge(rq, RN[0])
            gp.dma_start(out=wcat_sb.rearrange("p (c n) -> p c n", n=256),
                         in_=wcatT.rearrange("(c p) n -> p c n", p=128)).then_inc(qG, 16)
            gp.dma_start(out=gw_sb.rearrange("p (c n) -> p c n", n=EXP),
                         in_=gwT.rearrange("(c p) n -> p c n", p=128)).then_inc(qG, 16)
            QG[0] += 32
            gp.engine_nop().then_inc(rq, 1)
            RN[0] += 1
        block.gpsimd(p0_wcat)

        round_("scalar", lambda sc: sc.activation(scr[0:1, 0:D], rowpar[:, 0:D], AF.Exp, scale=-1.0))

        round_("vector", lambda v: v.tensor_scalar_add(scr[0:1, 2 * D:3 * D], scr[0:1, 0:D], 1.0))
        round_("vector", lambda v: v.reciprocal(scr[0:1, D:2 * D], scr[0:1, 2 * D:3 * D]))
        round_("scalar", lambda sc: sc.activation(lnsig[:], scr[0:1, D:2 * D], AF.Ln, bias=1e-8))

        def f(t):
            t.matmul(p4[0][:, 0:D], cf[0:1, TRI:TRI + 128], lnsig[:], start=True, stop=True)
            return t.matmul(p4[0][:, D:2 * D], cf[0:1, TRI:TRI + 128], rowpar[0:1, D:2 * D],
                            start=True, stop=True)
        round_("tensor", f)
        round_("vector", lambda v: v.tensor_copy(bc64[:], p4[0][:, 0:2 * D]))
        tap("bc64", bc64[:])
        tap("lnsig", lnsig[:], 1)

        def p1_sync(sync):
            sync.wait_ge(rq, RN[0])
            for tc in range(NCH):
                if tc >= 1:
                    sync.wait_ge(sTx, tc * 4)
                sync.dma_start(out=xstg[:, 0:HD],
                               in_=x_own[tc * 128:(tc + 1) * 128, :]).then_inc(sXin, 16)
        block.sync(p1_sync)

        def p1_t(t):
            t.wait_ge(rq, RN[0])
            for tc in range(NCH):
                t.wait_ge(sXin, (tc + 1) * 16)
                for hb in range(NHB):
                    g = hb // 4
                    bank = p4[g]
                    if tc > 0 and hb % 4 == 0:
                        t.wait_ge(sVx, (tc - 1) * 4 + g + 1)
                    mm = t.transpose(bank[:, (hb % 4) * 128:(hb % 4 + 1) * 128],
                                     xstg[:, hb * 128:(hb + 1) * 128],
                                     cf[:, IDE:IDE + 128])
                    if hb % 4 == 3:
                        mm.then_inc(sTx, 1)
        block.tensor(p1_t)

        def p1_v(v):
            v.wait_ge(rq, RN[0])
            for tc in range(NCH):
                for g in range(4):
                    v.wait_ge(sTx, tc * 4 + g + 1)
                    dst = x_fm.rearrange("p (hb t) -> p hb t", t=TOWN)[:, 4 * g:4 * g + 4,
                                                                      tc * 128:(tc + 1) * 128]
                    src = p4[g].rearrange("p (k t) -> p k t", t=128)
                    v.tensor_copy(dst, src).then_inc(sVx, 1)
            v.wait_ge(sVx, 32)
            for g in range(4):
                v.tensor_copy(x_fmb[:, g * 4096:(g + 1) * 4096],
                              x_fm[:, g * 4096:(g + 1) * 4096])
            v.engine_nop().then_inc(rq, 1)
        block.vector(p1_v)
        RN[0] += 1

        def p1_tp(t):
            t.wait_ge(rq, RN[0])
            t.wait_ge(qG, 32)
            for tc in range(NCH):
                pt, off = pslot(tc)
                for hb in range(NHB):
                    mm = t.matmul(pt[:, off:off + 256],
                                  x_fmb[:, hb * TOWN + tc * 128:hb * TOWN + (tc + 1) * 128],
                                  wcat_sb[:, hb * 256:(hb + 1) * 256],
                                  start=(hb == 0), stop=(hb == NHB - 1))
            mm.then_inc(rq, 1)
            RN[0] += 1
        block.tensor(p1_tp)

        def f(v):
            for tc in range(NCH):
                pt, off = pslot(tc)
                last = v.tensor_copy(proj[:, tc * 256:(tc + 1) * 256], pt[:, off:off + 256])
            return last
        round_("vector", f)

        scope("P2_prep")
        def f(sc):
            for c in range(NCH):
                b = c * 256
                sc.activation(scr2[:, 2 * c * D:(2 * c + 1) * D], proj[:, b:b + D], AF.Square,
                              accum_out=scr[:, 512 + 2 * c:513 + 2 * c])
                last = sc.activation(scr2[:, (2 * c + 1) * D:(2 * c + 2) * D],
                                     proj[:, b + 128:b + 192], AF.Square,
                                     accum_out=scr[:, 513 + 2 * c:514 + 2 * c])
            return last
        round_("scalar", f)
        round_("scalar", lambda sc: sc.activation(scr[:, 528:528 + 2 * NCH],
                                                  scr[:, 512:512 + 2 * NCH], AF.Ln))
        round_("scalar", lambda sc: sc.activation(scr[:, 544:544 + 2 * NCH],
                                                  scr[:, 528:528 + 2 * NCH], AF.Exp, scale=-0.5))

        round_("vector", lambda v: v.tensor_scalar_min(scr[:, 560:560 + 2 * NCH],
                                                        scr[:, 544:544 + 2 * NCH], 1e12))

        def f(v):
            for c in range(NCH):
                b = c * 256
                v.tensor_scalar(scr2[:, c * 128:c * 128 + D], proj[:, b:b + D],
                                scr[:, 560 + 2 * c:561 + 2 * c], 0.0, ALU.mult, ALU.add)
                v.tensor_scalar(scr2[:, c * 128 + D:c * 128 + 2 * D], proj[:, b + 128:b + 192],
                                scr[:, 561 + 2 * c:562 + 2 * c], 0.0, ALU.mult, ALU.add)
                last = v.tensor_tensor(scr2[:, 1024 + c * D:1024 + (c + 1) * D],
                                       proj[:, b + 192:b + 256], bc64[:, D:2 * D], ALU.add)
            return last
        round_("vector", f)

        round_("scalar", lambda sc: sc.activation(scr[:, 0:NCH * D], scr2[:, 1024:1024 + NCH * D],
                                                  AF.Tanh, scale=0.5))

        def f(v):
            for c in range(NCH):
                last = v.tensor_scalar(scr[:, 512 + c * D:512 + (c + 1) * D],
                                       scr[:, c * D:(c + 1) * D], 0.5, 0.5, ALU.mult, ALU.add)
            return last
        round_("vector", f)

        def f(v):
            for c in range(NCH):
                last = v.scalar_tensor_tensor(stLA[:, c * D:(c + 1) * D],
                                              scr[:, 512 + c * D:512 + (c + 1) * D], 8.0,
                                              bc64[:, 0:D], ALU.mult, ALU.mult)
            return last
        round_("vector", f)

        tap("stLA", stLA[:])

        def f(t):
            for c in range(NCH):
                pt, off = pslot(c)
                last = t.matmul(pt[:, off:off + D], cf[:, TRI:TRI + 128],
                                stLA[:, c * D:(c + 1) * D], start=True, stop=True)
            return last
        round_("tensor", f)

        def f(sc):
            for c in range(NCH):
                pt, off = pslot(c)
                sc.activation(stG[:, c * D:(c + 1) * D], pt[:, off:off + D], AF.Exp)
                last = sc.activation(scr2[:, 1024 + c * D:1024 + (c + 1) * D],
                                     pt[:, off:off + D], AF.Exp, scale=-1.0)
            return last
        round_("scalar", f)

        def f(v):
            for c in range(NCH):
                pt, off = pslot(c)
                last = v.tensor_tensor(scr[:, c * D:(c + 1) * D], pt[:, off:off + D],
                                       stLA[:, c * D:(c + 1) * D], ALU.subtract)
            return last
        round_("vector", f)

        def f(sc):
            last = sc.activation(scr[:, 0:NCH * D], scr[:, 0:NCH * D], AF.Exp)
            return last
        round_("scalar", f)
        round_("scalar", lambda sc: sc.activation(scr2[:, 1536:1536 + NCH * D],
                                                  stLA[:, 0:NCH * D], AF.Exp))
        tap("gi", scr2[:, 1024:1536])
        tap("alpha", scr2[:, 1536:2048])
        tap("gprev", scr[:, 0:512])

        def f(v):
            kcol = lambda c: scr2[:, c * 128:c * 128 + D]
            qcol = lambda c: scr2[:, c * 128 + D:c * 128 + 2 * D]
            gp_ = lambda c: scr[:, c * D:(c + 1) * D]
            gi_ = lambda c: scr2[:, 1024 + c * D:1024 + (c + 1) * D]
            al_ = lambda c: scr2[:, 1536 + c * D:1536 + (c + 1) * D]
            for c in range(NCH):
                v.tensor_tensor(stKV[:, c * 128:c * 128 + D], kcol(c), gp_(c), ALU.mult)
                v.tensor_copy(stKV[:, c * 128 + D:c * 128 + 2 * D],
                              proj[:, c * 256 + D:c * 256 + 2 * D])
                v.tensor_tensor(stQt[:, c * D:(c + 1) * D], qcol(c), gp_(c), ALU.mult)
                v.tensor_tensor(stQh[:, c * D:(c + 1) * D], qcol(c),
                                stG[:, c * D:(c + 1) * D], ALU.mult)
            for c in range(NCH):
                last = v.tensor_scalar(scr[:, 512 + c * D:512 + (c + 1) * D], al_(c),
                                       -1.0, 1.0, ALU.mult, ALU.add)
            return last
        round_("vector", f)

        def f(v):
            kcol = lambda c: scr2[:, c * 128:c * 128 + D]
            for c in range(NCH):
                last = v.tensor_tensor(stUc[:, c * D:(c + 1) * D],
                                       scr[:, 512 + c * D:512 + (c + 1) * D], kcol(c), ALU.mult)
            return last
        round_("vector", f)

        def f(v):
            gi_ = lambda c: scr2[:, 1024 + c * D:1024 + (c + 1) * D]
            for c in range(NCH):
                last = v.tensor_tensor(stU0[:, c * D:(c + 1) * D],
                                       stUc[:, c * D:(c + 1) * D], gi_(c), ALU.mult)
            return last
        round_("vector", f)

        def f(t):
            for c in range(NCH):
                last = t.matmul(p4[0][0:1, c * D:(c + 1) * D], e127[:],
                                stG[:, c * D:(c + 1) * D], start=True, stop=True)
            return last
        round_("tensor", f)
        round_("vector", lambda v: v.tensor_copy(rowa[0:1, 0:NCH * D], p4[0][0:1, 0:NCH * D]))

        def f(t):
            for c in range(NCH):
                last = t.matmul(p4[1][0:64, c:c + 1], rowa[0:1, c * D:(c + 1) * D],
                                onesc[0:1, 0:1], start=True, stop=True)
            return last
        round_("tensor", f)
        round_("vector", lambda v: v.tensor_copy(glc[:], p4[1][0:64, 0:NCH]))

        def f(t):
            for c in range(NCH):
                pt, off = pslot(c)
                t.transpose(pt[0:64, off:off + 128], stQt[:, c * D:(c + 1) * D], cf[:, IDE:IDE + 128])
                t.transpose(pt[0:64, off + 128:off + 256], stKV[:, c * 128:c * 128 + D],
                            cf[:, IDE:IDE + 128])
                last = t.transpose(pt[0:64, off + 256:off + 384], stQh[:, c * D:(c + 1) * D],
                                   cf[:, IDE:IDE + 128])
            return last
        round_("tensor", f)

        def f(v):
            for c in range(NCH):
                pt, off = pslot(c)
                v.tensor_copy(fmQt[:, c * L:(c + 1) * L], pt[0:64, off:off + 128])
                v.tensor_copy(fmKt[:, c * L:(c + 1) * L], pt[0:64, off + 128:off + 256])
                last = v.tensor_copy(fmQh[:, c * L:(c + 1) * L], pt[0:64, off + 256:off + 384])
            return last
        round_("vector", f)

        tap("proj", proj[:])
        tap("stKV", stKV[:])
        tap("stQt", stQt[:])
        tap("stQh", stQh[:])
        tap("stU0", stU0[:])
        tap("stG", stG[:])
        tap("glc", glc[:], 64)
        tap("fmQt", fmQt[:], 64)

        scope("P3_scan")
        ch_base_col = 1
        for pas in range(2):
            if pas == 0:
                round_("vector", lambda v: v.tensor_copy(stUc[:], stU0[:]))
            else:
                def f(v):
                    for c in range(NCH):
                        last = v.tensor_scalar(stUc[:, c * D:(c + 1) * D],
                                               stU0[:, c * D:(c + 1) * D],
                                               stS[:, c:c + 1], 0.0, ALU.mult, ALU.add)
                    return last
                round_("vector", f)

            def f(t):
                for c in range(NCH):
                    pt, off = pslot(c)
                    last = t.transpose(pt[0:64, off:off + 128], stUc[:, c * D:(c + 1) * D],
                                       cf[:, IDE:IDE + 128])
                return last
            round_("tensor", f)

            def f(v):
                for c in range(NCH):
                    pt, off = pslot(c)
                    last = v.tensor_copy(fmU[:, c * L:(c + 1) * L], pt[0:64, off:off + 128])
                return last
            round_("vector", f)

            def f(t):
                for c in range(NCH):
                    pt, off = pslot(c)
                    t.matmul(pt[:, off:off + 128], fmKt[:, c * L:(c + 1) * L],
                             fmU[:, c * L:(c + 1) * L], start=True, stop=True)
                    last = t.matmul(pt[:, off + 128:off + 256], fmU[:, c * L:(c + 1) * L],
                                    fmKt[:, c * L:(c + 1) * L], start=True, stop=True)
                return last
            round_("tensor", f)

            def f(v):
                for c in range(NCH):
                    pt, off = pslot(c)
                    v.tensor_tensor(solW[:, c * L:(c + 1) * L], pt[:, off:off + 128],
                                    cf[:, SLN:SLN + 128], ALU.mult)
                    last = v.tensor_tensor(solU[:, c * L:(c + 1) * L], pt[:, off + 128:off + 256],
                                           cf[:, SUN:SUN + 128], ALU.mult)
                return last
            round_("vector", f)

            def f(t):
                for c in range(NCH):
                    pt, off = pslot(c)
                    rhs = fmQt if pas == 0 else fmQh
                    last = t.matmul(pt[:, off:off + 128], fmU[:, c * L:(c + 1) * L],
                                    rhs[:, c * L:(c + 1) * L], start=True, stop=True)
                return last
            round_("tensor", f)

            maskoff = SUU if pas == 0 else TRI
            def f(v):
                for c in range(NCH):
                    pt, off = pslot(c)
                    last = v.tensor_tensor(gP[:, c * L:(c + 1) * L], pt[:, off:off + 128],
                                           cf[:, maskoff:maskoff + 128], ALU.mult)
                return last
            round_("vector", f)

            round_("vector", lambda v: v.tensor_copy(solR[:], stKV[:]))

            for j in range(7):
                def f(t, j=j):
                    for c in range(NCH):
                        pt, off = pslot(c)
                        last = t.matmul(pt[:, off:off + 128], solU[:, c * L:(c + 1) * L],
                                        solR[:, c * 128:(c + 1) * 128], start=True, stop=True)
                        if j < 6:
                            t.matmul(pt[:, off + 128:off + 256], solU[:, c * L:(c + 1) * L],
                                     solW[:, c * L:(c + 1) * L], start=True, stop=True)
                            last = t.matmul(pt[:, off + 256:off + 384], solW[:, c * L:(c + 1) * L],
                                            solU[:, c * L:(c + 1) * L], start=True, stop=True)
                    return last
                round_("tensor", f)

                def f(v, j=j):
                    for c in range(NCH):
                        pt, off = pslot(c)
                        v.tensor_tensor(solR[:, c * 128:(c + 1) * 128], pt[:, off:off + 128],
                                        solR[:, c * 128:(c + 1) * 128], ALU.add)
                        if j < 6:
                            v.tensor_copy(solW[:, c * L:(c + 1) * L], pt[:, off + 128:off + 256])
                            last = v.tensor_copy(solU[:, c * L:(c + 1) * L], pt[:, off + 256:off + 384])
                        else:
                            last = v.engine_nop()
                    return last
                round_("vector", f)

            tap(f"solW0_p{pas}", solW[:])
            tap(f"solR_p{pas}", solR[:])

            def f(t):
                for c in range(NCH):
                    pt, off = pslot(c)
                    t.transpose(pt[0:64, off:off + 128], solR[:, c * 128:c * 128 + D],
                                cf[:, IDE:IDE + 128])
                    t.matmul(pt[0:64, off + 128:off + 192], solR[:, c * 128:c * 128 + D],
                             stUc[:, c * D:(c + 1) * D], start=True, stop=True)
                    last = t.matmul(pt[0:64, off + 192:off + 256], stUc[:, c * D:(c + 1) * D],
                                    solR[:, c * 128 + D:c * 128 + 2 * D], start=True, stop=True)
                return last
            round_("tensor", f)

            def f(v):
                for c in range(NCH):
                    pt, off = pslot(c)
                    v.tensor_scalar(fmZ[:, c * L:(c + 1) * L], pt[0:64, off:off + 128],
                                    -1.0, 0.0, ALU.mult, ALU.add)
                    v.tensor_tensor(abATs[:, c * D:(c + 1) * D], cf[0:64, IDE:IDE + D],
                                    pt[0:64, off + 128:off + 192], ALU.subtract)
                    last = v.tensor_scalar(abBgs[:, c * D:(c + 1) * D],
                                           pt[0:64, off + 192:off + 256],
                                           glc[:, c:c + 1], 0.0, ALU.mult, ALU.add)
                return last
            round_("vector", f)

            def p3_abdma(sync):
                sync.wait_ge(rq, RN[0])
                sync.dma_start(out=h0_dram[0:1, 0:3], in_=fcon[0:1, 0:3],
                               allow_slow_non_contiguous=True).then_inc(rq, 16)
                RN[0] += 16
                abv = ab_my.rearrange("(c r) n -> r c n", r=D)
                sync.dma_start(out=abv[:, :, 0:D],
                               in_=abATs.rearrange("r (c n) -> r c n", n=D),
                               allow_slow_non_contiguous=True).then_inc(rq, 16)
                sync.dma_start(out=abv[:, :, D:2 * D],
                               in_=abBgs.rearrange("r (c n) -> r c n", n=D),
                               allow_slow_non_contiguous=True).then_inc(rq, 16)
                sync.dma_start(out=abv[:, :, 2 * D:2 * D + 1],
                               in_=glc[:, 0:NCH].rearrange("r (c x) -> r c x", x=1),
                               allow_slow_non_contiguous=True).then_inc(rq, 16)
                sync.dma_start(out=abv[:, :, 2 * D + 1:2 * D + 2],
                               in_=glc[:, 0:NCH].rearrange("r (c x) -> r c x", x=1),
                               allow_slow_non_contiguous=True).then_inc(rq, 16)
                RN[0] += 16 * 4
            block.sync(p3_abdma)

            def p3_ag(gp):
                gp.wait_ge(rq, RN[0])
                gp.collective_compute(
                    "AllGather", ALU.bypass,
                    replica_groups=[list(range(NCORE))],
                    ins=[ab_my.opt()], outs=[ab_allp[pas].opt()],
                ).then_inc(rq, 1)
                RN[0] += 1
            block.gpsimd(p3_ag)

            h0ring = scr[0:64, 0:512]
            ab_ov = arena[0:64, MARK0:MARK0 + 50 * 130]
            ab_s2 = scr2[0:64, 0:14 * 130]

            def slotAB(gc):
                if gc < 50:
                    return ab_ov[:, gc * 130:(gc + 1) * 130]
                return ab_s2[:, (gc - 50) * 130:(gc - 49) * 130]

            round_("vector", lambda v: v.memset(h0ring[:, 0:D], 0.0))

            sx0 = NCH * 16 + pas * 160

            def p3_chain_dma(sync):
                sync.wait_ge(rq, RN[0])
                sync.dma_start(out=ab_ov.rearrange("r (c n) -> r c n", n=130),
                               in_=ab_allp[pas][0:50 * D, :].rearrange("(c r) n -> r c n", r=D),
                               allow_slow_non_contiguous=True).then_inc(sXin, 16)
                sync.dma_start(out=ab_s2.rearrange("r (c n) -> r c n", n=130),
                               in_=ab_allp[pas][50 * D:64 * D, :].rearrange("(c r) n -> r c n", r=D),
                               allow_slow_non_contiguous=True).then_inc(sXin, 16)
                for k in range(8):
                    sync.wait_ge(rq, RN[0] + 2 * (8 * k + 6) + 2)
                    sync.dma_start(out=h0_dram[k * 8 * D:(k + 1) * 8 * D, :]
                                   .rearrange("(g r) c -> r g c", r=D),
                                   in_=h0ring[:, 0:512].rearrange("r (g c) -> r g c", c=D),
                                   allow_slow_non_contiguous=True).then_inc(sXin, 16)
            block.sync(p3_chain_dma)

            def p3_chain_t(t):
                t.wait_ge(rq, RN[0])
                t.wait_ge(sXin, sx0 + 32)
                for gc in range(64):
                    bank = p4[gc % 2]
                    t.matmul(bank[0:64, 0:D], slotAB(gc)[:, 0:D],
                             h0ring[:, (gc % 8) * D:(gc % 8 + 1) * D],
                             start=True, stop=True).then_inc(rq, 1)
                    t.wait_ge(rq, RN[0] + (gc + 1) * 2)
                RN[0] += 64 * 2

            def p3_chain_v(v):
                base = RN[0] - 64 * 2
                for gc in range(64):
                    v.wait_ge(rq, base + gc * 2 + 1)
                    if gc % 8 == 7:
                        v.wait_ge(sXin, sx0 + 32 + ((gc // 8) + 1) * 16)
                    bank = p4[gc % 2]
                    v.scalar_tensor_tensor(h0ring[:, ((gc + 1) % 8) * D:((gc + 1) % 8 + 1) * D],
                                           bank[0:64, 0:D],
                                           slotAB(gc)[:, 2 * D:2 * D + 1],
                                           slotAB(gc)[:, D:2 * D],
                                           ALU.mult, ALU.add).then_inc(rq, 1)
            block.tensor(p3_chain_t)
            block.vector(p3_chain_v)

            def f(t):
                return t.matmul(p4[2][0:64, 0:1], cf[0:1, TRI:TRI + 64], cvt[0:1, 1:2],
                                start=True, stop=True)
            round_("tensor", f)

            def f(v):
                for c in range(NCH):
                    last = v.scalar_tensor_tensor(scr[0:64, c:c + 1], p4[2][0:64, 0:1], float(D),
                                                  iota64[:], ALU.mult, ALU.add)
                return last
            round_("vector", f)

            def f(v):
                for c in range(1, NCH):
                    v.tensor_scalar_add(scr[0:64, 8 + c:9 + c], scr[0:64, c:c + 1], float(c * D))
                return v.tensor_copy(scr[0:64, 8:9], scr[0:64, 0:1])
            round_("vector", f)
            round_("vector", lambda v: v.tensor_copy(idxg[:], scr[0:64, 8:8 + NCH]))

            def p3_h0g(gp):
                gp.wait_ge(rq, RN[0])
                gp.wait_ge(sXin, sx0 + 160)
                breg = gp.to_reg(64 * D - 1)
                for c in range(NCH):
                    gp.indirect_dma_start(
                        h0my[:, c * D:(c + 1) * D],
                        None, h0_dram[:],
                        IndirectOffsetOnAxis(idxg[:, c:c + 1], 0),
                        bounds_check=breg, oob_is_err=False,
                    ).then_inc(qG, 16)
                QG[0] += 16 * NCH
                gp.engine_nop().then_inc(rq, 1)
                RN[0] += 1
            block.gpsimd(p3_h0g)
            QGH0 = QG[0]

            def p3_h0w(t):
                t.wait_ge(qG, QGH0)
            block.tensor(p3_h0w)

            tap(f"h0my_p{pas}", h0my[:], 64)
            tap(f"abAT_p{pas}", abATs[:], 64)
            tap(f"abBg_p{pas}", abBgs[:], 64)

            def f(t):
                for c in range(NCH):
                    pt, off = pslot(c)
                    last = t.matmul(pt[:, off:off + D], fmZ[:, c * L:(c + 1) * L],
                                    h0my[:, c * D:(c + 1) * D], start=True, stop=True)
                return last
            round_("tensor", f)

            def f(v):
                for c in range(NCH):
                    pt, off = pslot(c)
                    last = v.tensor_tensor(stM[:, c * D:(c + 1) * D], pt[:, off:off + D],
                                           solR[:, c * 128 + D:c * 128 + 2 * D], ALU.add)
                return last
            round_("vector", f)

            def f(t):
                for c in range(NCH):
                    pt, off = pslot(c)
                    lhs = fmQt if pas == 0 else fmQh
                    t.matmul(pt[:, off:off + D], lhs[:, c * L:(c + 1) * L],
                             h0my[:, c * D:(c + 1) * D], start=True, stop=False)
                    last = t.matmul(pt[:, off:off + D], gP[:, c * L:(c + 1) * L],
                                    stM[:, c * D:(c + 1) * D], start=False, stop=True)
                return last
            round_("tensor", f)

            if pas == 0:
                def f(v):
                    for c in range(NCH):
                        pt, off = pslot(c)
                        last = v.tensor_tensor(scr[:, c * D:(c + 1) * D],
                                               stKV[:, c * 128 + D:c * 128 + 2 * D],
                                               pt[:, off:off + D], ALU.subtract)
                    return last
                round_("vector", f)

                def f(sc):
                    for c in range(NCH):
                        last = sc.activation(scr2[:, c * D:(c + 1) * D], scr[:, c * D:(c + 1) * D],
                                             AF.Square, accum_out=errs[:, c:c + 1])
                    return last
                round_("scalar", f)
                round_("scalar", lambda sc: sc.activation(scr[:, 0:NCH], errs[:, 0:NCH],
                                                          AF.Tanh, scale=SURSC / 2))
                round_("vector", lambda v: v.tensor_scalar(stS[:, 0:NCH], scr[:, 0:NCH],
                                                           0.5, 0.5, ALU.mult, ALU.add))
            else:
                def f(v):
                    for c in range(NCH):
                        pt, off = pslot(c)
                        last = v.tensor_copy(yattn[:, c * D:(c + 1) * D], pt[:, off:off + D])
                    return last
                round_("vector", f)

        tap("stM_p1", stM[:])
        tap("errs", errs[:])
        tap("stS", stS[:])
        tap("yattn", yattn[:])

        scope("P4_wo_ln")
        def p4_wo(sync):
            sync.wait_ge(rq, RN[0])
            sync.dma_start(out=scr2[0:64, 0:HD], in_=woT[:]).then_inc(rq, 16)
            RN[0] += 16
        block.sync(p4_wo)

        def f(t):
            return t.matmul(p4[0][0:64, 0:1], rowpar[0:1, 2 * D:3 * D], onesc[0:1, 0:1],
                            start=True, stop=True)
        round_("tensor", f)

        round_("vector", lambda v: v.tensor_copy(nwcol[:], p4[0][0:64, 0:1]))
        round_("vector", lambda v: v.tensor_scalar(wo_fm[:], scr2[0:64, 0:HD], nwcol[:],
                                                   0.0, ALU.mult, ALU.add))

        def f(sc):
            for c in range(NCH):
                last = sc.activation(scr2[:, c * D:(c + 1) * D],
                                     yattn[:, c * D:(c + 1) * D], AF.Square,
                                     accum_out=scr[:, 512 + c:513 + c])
            return last
        round_("scalar", f)
        round_("scalar", lambda sc: sc.activation(scr[:, 520:520 + NCH], scr[:, 512:512 + NCH],
                                                  AF.Ln, scale=1.0 / D, bias=RMS_EPS))
        round_("scalar", lambda sc: sc.activation(scr[:, 528:528 + NCH], scr[:, 520:520 + NCH],
                                                  AF.Exp, scale=-0.5))

        def f(v):
            for c in range(NCH):
                last = v.tensor_scalar(scr2[:, c * D:(c + 1) * D],
                                       yattn[:, c * D:(c + 1) * D],
                                       scr[:, 528 + c:529 + c], 0.0, ALU.mult, ALU.add)
            return last
        round_("vector", f)

        tap("ynm", scr2[:, 0:NCH * D])
        tap("rstdy", scr[:, 528:528 + NCH])
        tap("wofm", wo_fm[:], 64, BF16)

        def f(t):
            for c in range(NCH):
                pt, off = pslot(c)
                last = t.transpose(pt[0:64, off:off + 128],
                                   scr2[:, c * D:(c + 1) * D], cf[:, IDE:IDE + 128])
            return last
        round_("tensor", f)

        def f(v):
            for c in range(NCH):
                pt, off = pslot(c)
                last = v.tensor_copy(ynfm[:, c * 128:(c + 1) * 128],
                                     pt[0:64, off:off + 128])
            return last
        round_("vector", f)

        for ob2 in range(NHB // 2):
            def f(t, ob2=ob2):
                for k in range(2):
                    ob = ob2 * 2 + k
                    for half in range(2):
                        last = t.matmul(pw[k][:, half * 512:(half + 1) * 512],
                                        wo_fm[:, ob * 128:(ob + 1) * 128],
                                        ynfm[:, half * 512:(half + 1) * 512],
                                        start=True, stop=True)
                return last
            round_("tensor", f)

            def f(v, ob2=ob2):
                for k in range(2):
                    ob = ob2 * 2 + k
                    last = v.tensor_tensor(x_fm[:, ob * TOWN:(ob + 1) * TOWN],
                                           pw[k][:, 0:1024],
                                           x_fm[:, ob * TOWN:(ob + 1) * TOWN], ALU.add)
                return last
            round_("vector", f)

        tap("s1fm", x_fm[:, 0:2048])

        for tile in range(2):
            tsl = slice(tile * 512, (tile + 1) * 512)
            def f(t, tile=tile):
                for hb in range(NHB):
                    last = t.matmul(p4[0][0:1, 0:512],
                                    onesc[:], x_fm[:, hb * TOWN + tile * 512:hb * TOWN + (tile + 1) * 512],
                                    start=(hb == 0), stop=(hb == NHB - 1))
                return last
            round_("tensor", f)
            round_("vector", lambda v, tile=tile: v.tensor_copy(rowa[0:1, tile * 512:(tile + 1) * 512],
                                                                p4[0][0:1, 0:512]))
            for hb in range(NHB):
                round_("scalar", lambda sc, hb=hb, tile=tile: sc.activation(
                    scr2[:, 0:512],
                    x_fm[:, hb * TOWN + tile * 512:hb * TOWN + (tile + 1) * 512], AF.Square))
                round_("tensor", lambda t, hb=hb: t.matmul(
                    p4[1][0:1, 0:512], onesc[:], scr2[:, 0:512],
                    start=(hb == 0), stop=(hb == NHB - 1)))
            round_("vector", lambda v, tile=tile: v.tensor_copy(rowa[0:1, 1024 + tile * 512:1024 + (tile + 1) * 512],
                                                                p4[1][0:1, 0:512]))

        def f(v):
            v.tensor_scalar_mul(rowa[0:1, 0:1024], rowa[0:1, 0:1024], 1.0 / HD)
            return v.tensor_scalar_mul(rowa[0:1, 1024:2048], rowa[0:1, 1024:2048], 1.0 / HD)
        round_("vector", f)
        round_("vector", lambda v: v.tensor_tensor(scr[0:1, 0:1024], rowa[0:1, 0:1024],
                                                   rowa[0:1, 0:1024], ALU.mult))
        round_("vector", lambda v: v.tensor_tensor(rowa[0:1, 1024:2048], rowa[0:1, 1024:2048],
                                                   scr[0:1, 0:1024], ALU.subtract))
        round_("scalar", lambda sc: sc.activation(rowa[0:1, 1024:2048], rowa[0:1, 1024:2048],
                                                  AF.Ln, bias=LN_EPS))
        round_("scalar", lambda sc: sc.activation(rowa[0:1, 1024:2048], rowa[0:1, 1024:2048],
                                                  AF.Exp, scale=-0.5))
        def f(v):
            return v.tensor_tensor(rowa[0:1, 0:1024], rowa[0:1, 0:1024],
                                   rowa[0:1, 1024:2048], ALU.mult)
        round_("vector", f)

        def f(t):
            for half in range(2):
                t.matmul(pw[0][:, half * 512:(half + 1) * 512], cf[0:1, TRI:TRI + 128],
                         rowa[0:1, 1024 + half * 512:1024 + (half + 1) * 512],
                         start=True, stop=True)
                last = t.matmul(pw[1][:, half * 512:(half + 1) * 512], cf[0:1, TRI:TRI + 128],
                                rowa[0:1, half * 512:(half + 1) * 512],
                                start=True, stop=True)
            return last
        round_("tensor", f)

        def f(v):
            for hb in range(NHB):
                last = v.tensor_tensor(x_fm[:, hb * TOWN:(hb + 1) * TOWN],
                                       x_fm[:, hb * TOWN:(hb + 1) * TOWN], pw[0][:, 0:1024],
                                       ALU.mult)
            return last
        round_("vector", f)

        def f(v):
            for hb in range(NHB):
                last = v.tensor_tensor(x_fm[:, hb * TOWN:(hb + 1) * TOWN],
                                       x_fm[:, hb * TOWN:(hb + 1) * TOWN], pw[1][:, 0:1024],
                                       ALU.subtract)
            return last
        round_("vector", f)

        def f(v):
            for hb in range(NHB):
                last = v.tensor_copy(h_fmb[:, hb * TOWN:(hb + 1) * TOWN],
                                     x_fm[:, hb * TOWN:(hb + 1) * TOWN])
            return last
        round_("vector", f)

        tap("rowa", rowa[:], 1)

        def emit_h_tm():
            RQ_EH = RN[0]

            def pslot_h(tc, g):
                if tc % 2 == 0:
                    return p4[g][:, 0:512]
                return pw[g // 2][:, (g % 2) * 512:(g % 2 + 1) * 512]

            for tc in range(NCH):
                def ft(t, tc=tc):
                    if tc <= 1:
                        t.wait_ge(rq, RQ_EH)
                    if tc >= 2:
                        t.wait_ge(sVx, 32 + tc - 1)
                    for hb in range(NHB):
                        g = hb // 4
                        mm = t.transpose(pslot_h(tc, g)[:, (hb % 4) * 128:(hb % 4) * 128 + 128],
                                         x_fm[:, hb * TOWN + tc * 128:hb * TOWN + (tc + 1) * 128],
                                         cf[:, IDE:IDE + 128])
                    mm.then_inc(sTx, 1)
                block.tensor(ft)

                rqd = RN[0]
                def fv(v, tc=tc, rqd=rqd):
                    v.wait_ge(sTx, 32 + tc + 1)
                    v.wait_ge(rq, rqd)
                    for g in range(4):
                        v.tensor_copy(hstg_bf[:, g * 512:(g + 1) * 512], pslot_h(tc, g))
                    for g in range(4):
                        v.tensor_copy(scr2[:, g * 512:(g + 1) * 512], pslot_h(tc, g))
                    for _ in range(6):
                        v.engine_nop()
                    v.engine_nop().then_inc(sVx, 1)
                block.vector(fv)

                def fd(sync, tc=tc):
                    sync.wait_ge(sVx, 32 + tc + 1)
                    sync.dma_start(out=htm_bf_my[tc * 128:(tc + 1) * 128, :],
                                   in_=hstg_bf[:, 0:HD]).then_inc(rq, 16)
                    sync.dma_start(out=htm_f32[tc * 128:(tc + 1) * 128, :],
                                   in_=scr2[:, 0:HD]).then_inc(rq, 16)
                    RN[0] += 32
                block.sync(fd)
        emit_h_tm()

        def p4_hag(gp):
            gp.wait_ge(rq, RN[0])
            gp.collective_compute("AllGather", ALU.bypass,
                                  replica_groups=[list(range(NCORE))],
                                  ins=[htm_bf_my.opt()], outs=[h_full.opt()]).then_inc(sCC, 1)
        block.gpsimd(p4_hag)

        scope("P5_router")
        for tile in range(2):
            def f(t, tile=tile):
                t.wait_ge(qG, 32)
                if tile == 0:
                    t.wait_ge(sVx, 40)
                for hb in range(NHB):
                    last = t.matmul(p4[tile % 2][0:16, 0:512],
                                    gw_sb[:, hb * EXP:(hb + 1) * EXP],
                                    x_fm[:, hb * TOWN + tile * 512:hb * TOWN + (tile + 1) * 512],
                                    start=(hb == 0), stop=(hb == NHB - 1))
                return last
            round_("tensor", f)
            round_("vector", lambda v, tile=tile: v.tensor_copy(
                lgfm[:, tile * 512:(tile + 1) * 512], p4[tile % 2][0:16, 0:512]))

        arena_reset()
        rscr = aa(1040)
        rscr2 = aa(1032)
        lgtm = aa(8 * 16)
        vals = aa(32)
        e4t = aa(32)
        s4t = aa(8)
        lns4 = aa(8)
        negm4 = aa(8)
        negb4 = aa(8)
        idx4 = aa(32)
        w4 = aa(32)
        excl = aa(4 * 8 * 16)
        maskg = aa(8 * 16)
        rankg = aa(8 * 16)
        slot4 = aa(32)
        slotc4 = aa(32)
        destf = aa(32)
        gidxf = aa(32)
        tokid0 = aa(1)
        tokidc = aa(8)
        pairsA = aa(64)
        cntrow = aa(8 * 16)[0:1, :]
        carrow = aa(8 * 16)[0:1, :]
        prefr = aa(16)[0:1, :]
        cnasb = aa(16)[0:8, :]
        gbsb = aa(2052)[0:16, :]
        myg = aa(2052)[0:2, :]
        idcols = aa(16)
        wcols = aa(16)
        workt = rscr[:, 384:640]
        hidxf = aa(16)
        FFN_MARK = _aoff[0]

        rB = es.enter_context(nc.semaphore("rB"))
        RB = [0]
        RQ_P5A = RN[0]
        BQ = []
        BFIRST = [True]

        def roundB(engname, fn, inc=1):
            def thunk(engname=engname, fn=fn, inc=inc):
                first = BFIRST[0]
                BFIRST[0] = False

                def body(eng):
                    if first:
                        eng.wait_ge(rq, RQ_P5A)
                    eng.wait_ge(rB, RB[0])
                    last = fn(eng)
                    last.then_inc(rB, inc)
                    if engname == "vector":
                        for _ in range(6):
                            eng.engine_nop()
                    elif engname == "scalar":
                        for _ in range(6):
                            eng.nop()
                getattr(block, engname)(body)
                RB[0] += inc
            BQ.append(thunk)

        def rawB(engname, emitfn):
            def thunk(engname=engname, emitfn=emitfn):
                getattr(block, engname)(emitfn)
            BQ.append(thunk)

        def emitB(n):
            for _ in range(min(n, len(BQ))):
                BQ.pop(0)()

        def bslot(i):
            return p4[i % 4], (i // 4) * 16

        def f(t):
            for i in range(8):
                pt, off = bslot(i)
                last = t.transpose(pt[:, off:off + 16],
                                   lgfm[:, i * 128:(i + 1) * 128], cf[0:16, IDE:IDE + 16])
            return last
        roundB("tensor", f)

        def f(v):
            for i in range(8):
                pt, off = bslot(i)
                last = v.tensor_copy(lgtm[:, i * 16:(i + 1) * 16], pt[:, off:off + 16])
            return last
        roundB("vector", f)

        roundB("vector", lambda v: v.tensor_scalar_add(rscr[:, 960:976], iota16[:], BIGF))
        roundB("vector", lambda v: v.tensor_copy(workt[:, 0:128], lgtm[:]))
        roundB("vector", lambda v: v.memset(maskg[:], 0.0))

        for k in range(4):
            wsrc = workt[:, (k % 2) * 128:(k % 2) * 128 + 128]
            wdst = workt[:, ((k + 1) % 2) * 128:((k + 1) % 2) * 128 + 128]

            def mk(fn):
                def f(v, k=k, wsrc=wsrc, wdst=wdst, fn=fn):
                    for i in range(8):
                        last = fn(v, i, k, wsrc, wdst)
                    return last
                roundB("vector", f)
            mk(lambda v, i, k, ws, wd: v.tensor_reduce(vals[:, i * 4 + k:i * 4 + k + 1],
                                                       ws[:, i * 16:(i + 1) * 16], AX.X, ALU.max))
            mk(lambda v, i, k, ws, wd: v.tensor_scalar(rscr[:, i * 16:(i + 1) * 16],
                                                       ws[:, i * 16:(i + 1) * 16],
                                                       vals[:, i * 4 + k:i * 4 + k + 1], 0.0,
                                                       ALU.is_equal, ALU.add))
            mk(lambda v, i, k, ws, wd: v.scalar_tensor_tensor(
                rscr[:, 128 + i * 16:128 + (i + 1) * 16], rscr[:, i * 16:(i + 1) * 16], -BIGF,
                rscr[:, 960:976], ALU.mult, ALU.add))
            mk(lambda v, i, k, ws, wd: v.tensor_reduce(idx4[:, i * 4 + k:i * 4 + k + 1],
                                                       rscr[:, 128 + i * 16:128 + (i + 1) * 16],
                                                       AX.X, ALU.min))
            mk(lambda v, i, k, ws, wd: v.tensor_scalar(
                excl[:, (k * 8 + i) * 16:(k * 8 + i + 1) * 16], iota16[:],
                idx4[:, i * 4 + k:i * 4 + k + 1], 0.0, ALU.is_equal, ALU.add))
            mk(lambda v, i, k, ws, wd: v.scalar_tensor_tensor(
                wd[:, i * 16:(i + 1) * 16], excl[:, (k * 8 + i) * 16:(k * 8 + i + 1) * 16],
                -BIGF, ws[:, i * 16:(i + 1) * 16], ALU.mult, ALU.add))
            mk(lambda v, i, k, ws, wd: v.tensor_tensor(
                maskg[:, i * 16:(i + 1) * 16], maskg[:, i * 16:(i + 1) * 16],
                excl[:, (k * 8 + i) * 16:(k * 8 + i + 1) * 16], ALU.add))

        def f(v):
            for i in range(8):
                last = v.tensor_reduce(rscr[:, i:i + 1], vals[:, i * 4:(i + 1) * 4], AX.X, ALU.max)
            return last
        roundB("vector", f)

        def f(v):
            for i in range(8):
                last = v.tensor_scalar(negm4[:, i:i + 1], rscr[:, i:i + 1], -1.0, 0.0,
                                       ALU.mult, ALU.add)
            return last
        roundB("vector", f)

        def f(sc):
            for i in range(8):
                last = sc.activation(e4t[:, i * 4:(i + 1) * 4], vals[:, i * 4:(i + 1) * 4],
                                     AF.Exp, bias=negm4[:, i:i + 1])
            return last
        roundB("scalar", f)

        def f(v):
            for i in range(8):
                last = v.tensor_reduce(s4t[:, i:i + 1], e4t[:, i * 4:(i + 1) * 4], AX.X, ALU.add)
            return last
        roundB("vector", f)
        roundB("scalar", lambda sc: sc.activation(lns4[:], s4t[:], AF.Ln))

        def f(v):
            for i in range(8):
                last = v.tensor_tensor(rscr[:, i:i + 1], negm4[:, i:i + 1], lns4[:, i:i + 1],
                                       ALU.subtract)
            return last
        roundB("vector", f)
        roundB("vector", lambda v: v.tensor_copy(negb4[:], rscr[:, 0:8]))

        def f(sc):
            for i in range(8):
                last = sc.activation(w4[:, i * 4:(i + 1) * 4], vals[:, i * 4:(i + 1) * 4],
                                     AF.Exp, bias=negb4[:, i:i + 1])
            return last
        roundB("scalar", f)

        def f(t):
            for i in range(8):
                last = t.matmul(p4[0][0:1, i * 16:(i + 1) * 16], onesc[:],
                                maskg[:, i * 16:(i + 1) * 16], start=True, stop=True)
            return last
        roundB("tensor", f)
        roundB("vector", lambda v: v.tensor_copy(cntrow[:], p4[0][0:1, 0:128]))

        def f(v):
            v.tensor_tensor(rscr[0:1, 0:16], cntrow[0:1, 0:16], cntrow[0:1, 16:32], ALU.add)
            v.tensor_tensor(rscr[0:1, 16:32], cntrow[0:1, 32:48], cntrow[0:1, 48:64], ALU.add)
            v.tensor_tensor(rscr[0:1, 32:48], cntrow[0:1, 64:80], cntrow[0:1, 80:96], ALU.add)
            return v.tensor_tensor(rscr[0:1, 48:64], cntrow[0:1, 96:112], cntrow[0:1, 112:128],
                                   ALU.add)
        roundB("vector", f)

        def f(v):
            v.tensor_tensor(rscr[0:1, 64:80], rscr[0:1, 0:16], rscr[0:1, 16:32], ALU.add)
            return v.tensor_tensor(rscr[0:1, 80:96], rscr[0:1, 32:48], rscr[0:1, 48:64], ALU.add)
        roundB("vector", f)
        roundB("vector", lambda v: v.tensor_tensor(rscr[0:1, 96:112], rscr[0:1, 64:80],
                                                   rscr[0:1, 80:96], ALU.add))

        def fd(sync):
            sync.wait_ge(rB, RB[0])
            sync.dma_start(out=h0_dram[1:2, 0:3], in_=fcon[0:1, 0:3],
                           allow_slow_non_contiguous=True).then_inc(rB, 16)
            sync.dma_start(out=cnt_my[:], in_=rscr[0:1, 96:112]).then_inc(rB, 16)
            RB[0] += 32
        rawB("sync", fd)

        def fg(gp):
            gp.wait_ge(rB, RB[0])
            gp.collective_compute("AllGather", ALU.bypass,
                                  replica_groups=[list(range(NCORE))],
                                  ins=[cnt_my.opt()], outs=[cnt_all.opt()]).then_inc(rB, 1)
            RB[0] += 1
        rawB("gpsimd", fg)

        def fd(sync):
            sync.wait_ge(rB, RB[0])
            sync.dma_start(out=cnasb[:], in_=cnt_all[:]).then_inc(rB, 16)
            RB[0] += 16
        rawB("sync", fd)

        def f(t):
            return t.matmul(p4[1][0:1, 0:16], prem[:], cnasb[:], start=True, stop=True)
        roundB("tensor", f)
        roundB("vector", lambda v: v.tensor_copy(prefr[:], p4[1][0:1, 0:16]))

        roundB("vector", lambda v: v.tensor_copy(carrow[0:1, 0:16], prefr[:]))
        for i in range(1, 8):
            roundB("vector", lambda v, i=i: v.tensor_tensor(
                carrow[0:1, i * 16:(i + 1) * 16], carrow[0:1, (i - 1) * 16:i * 16],
                cntrow[0:1, (i - 1) * 16:i * 16], ALU.add))

        def f(t):
            for i in range(8):
                pt, off = bslot(i)
                t.matmul(pt[:, off:off + 16], cf[:, SUU:SUU + 128],
                         maskg[:, i * 16:(i + 1) * 16], start=True, stop=False)
                last = t.matmul(pt[:, off:off + 16], cf[0:1, TRI:TRI + 128],
                                carrow[0:1, i * 16:(i + 1) * 16], start=False, stop=True)
            return last
        roundB("tensor", f)

        def f(v):
            for i in range(8):
                pt, off = bslot(i)
                last = v.tensor_copy(rankg[:, i * 16:(i + 1) * 16], pt[:, off:off + 16])
            return last
        roundB("vector", f)

        def f(v):
            for i in range(8):
                for k in range(4):
                    last = v.tensor_tensor(rscr[:, (i * 4 + k) * 16:(i * 4 + k + 1) * 16],
                                           rankg[:, i * 16:(i + 1) * 16],
                                           excl[:, (k * 8 + i) * 16:(k * 8 + i + 1) * 16],
                                           ALU.mult)
            return last
        roundB("vector", f)

        def f(v):
            for j in range(32):
                last = v.tensor_reduce(slot4[:, j:j + 1], rscr[:, j * 16:(j + 1) * 16],
                                       AX.X, ALU.add)
            return last
        roundB("vector", f)

        def f(v):
            for j in range(32):
                last = v.tensor_scalar_min(slotc4[:, j:j + 1], slot4[:, j:j + 1], float(CAP))
            return last
        roundB("vector", f)

        def f(v):
            for j in range(32):
                v.scalar_tensor_tensor(destf[:, j:j + 1], idx4[:, j:j + 1], 1025.0,
                                       slotc4[:, j:j + 1], ALU.mult, ALU.add)
            for j in range(32):
                v.tensor_scalar(rscr[:, j:j + 1], slot4[:, j:j + 1], float(CAP), BIGF,
                                ALU.is_ge, ALU.mult)
            for j in range(32):
                last = v.scalar_tensor_tensor(rscr[:, 32 + j:33 + j], idx4[:, j:j + 1], 1024.0,
                                              slot4[:, j:j + 1], ALU.mult, ALU.add)
            return last
        roundB("vector", f)

        def f(v):
            for j in range(32):
                last = v.tensor_tensor(gidxf[:, j:j + 1], rscr[:, 32 + j:33 + j],
                                       rscr[:, j:j + 1], ALU.add)
            return last
        roundB("vector", f)

        def f(t):
            return t.matmul(p4[2][:, 0:1], cf[0:1, TRI:TRI + 128], cvt[0:1, 0:1],
                            start=True, stop=True)
        roundB("tensor", f)

        roundB("vector", lambda v: v.scalar_tensor_tensor(tokid0[:], p4[2][:, 0:1], 1.0,
                                                          iotac[:], ALU.mult, ALU.add))

        def f(v):
            for i in range(8):
                last = v.tensor_scalar_add(tokidc[:, i:i + 1], tokid0[:], float(i * 128 + 1))
            return last
        roundB("vector", f)

        def f(v):
            for i in range(8):
                for k in range(4):
                    j = i * 4 + k
                    v.tensor_copy(pairsA[:, 2 * j:2 * j + 1], tokidc[:, i:i + 1])
                    last = v.tensor_copy(pairsA[:, 2 * j + 1:2 * j + 2], w4[:, j:j + 1])
            return last
        roundB("vector", f)

        def f(v):
            last = v.tensor_copy(desti32[:], destf[:])
            return last
        roundB("vector", f)
        roundB("vector", lambda v: v.tensor_copy(gidx32[:], gidxf[:]))

        roundB("vector", lambda v: v.memset(rscr2[:, 0:258], 0.0))

        def fd(sync):
            sync.wait_ge(rB, RB[0])
            sync.dma_start(out=h0_dram[1:2, 0:3], in_=fcon[0:1, 0:3],
                           allow_slow_non_contiguous=True).then_inc(rB, 16)
            sync.dma_start(out=gbuf_my[0:16384, :].rearrange("(p n) c -> p (n c)", p=128),
                           in_=rscr2[:, 0:256]).then_inc(rB, 16)
            sync.dma_start(out=gbuf_my[16384:16400, :], in_=rscr2[0:16, 256:258]).then_inc(rB, 16)
            RB[0] += 48
        rawB("sync", fd)

        def fg(gp):
            gp.wait_ge(rB, RB[0])
            qbase = QG[0]
            for j in range(32):
                if j:
                    gp.wait_ge(qG, qbase + j * 16)
                gp.indirect_dma_start(
                    gbuf_my[:], IndirectOffsetOnAxis(desti32[:, j:j + 1], 0),
                    pairsA[:, 2 * j:2 * j + 2], None,
                ).then_inc(qG, 16)
            QG[0] += 32 * 16
            gp.engine_nop().then_inc(rB, 1)
            RB[0] += 1
        rawB("gpsimd", fg)

        QG_SCAT = [0]

        def fg(gp):
            gp.wait_ge(rB, RB[0])
            gp.wait_ge(qG, QG_SCAT[0])
            gp.collective_compute("AllReduce", ALU.add,
                                  replica_groups=[list(range(NCORE))],
                                  ins=[gbuf_my.opt()], outs=[gbuf_all.opt()]).then_inc(rB, 1)
            RB[0] += 1
        def thunk_ar(fg=fg):
            QG_SCAT[0] = QG[0]
            block.gpsimd(fg)
        BQ.append(thunk_ar)

        def fd(sync):
            sync.wait_ge(rB, RB[0])
            sync.dma_start(out=gbsb[:, 0:2050],
                           in_=gbuf_all.rearrange("(e s) c -> e (s c)", e=16)).then_inc(rB, 16)
            RB[0] += 16
        rawB("sync", fd)

        def f(t):
            for j in range(4):
                last = t.matmul(p4[j][0:2, 0:512], eselt[:], gbsb[:, j * 512:(j + 1) * 512],
                                start=True, stop=True)
            return last
        roundB("tensor", f)

        def f(v):
            for j in range(4):
                last = v.tensor_copy(myg[:, j * 512:(j + 1) * 512], p4[j][0:2, 0:512])
            return last
        roundB("vector", f)

        roundB("tensor", lambda t: t.matmul(p4[0][0:2, 0:2], eselt[:], gbsb[:, 2048:2050],
                                            start=True, stop=True))
        roundB("vector", lambda v: v.tensor_copy(myg[:, 2048:2050], p4[0][0:2, 0:2]))

        def f(v):
            mygv = myg[:, 0:2050].rearrange("p (s c) -> p s c", c=2)
            v.tensor_copy(rscr2[0:2, 0:1025], mygv[:, :, 0])
            last = v.tensor_copy(gbsb[0:2, 0:1025], mygv[:, :, 1])
            return last
        roundB("vector", f)

        def f(t):
            for si in range(8):
                pt, off = p4[si % 4], (si // 4) * 4
                t.transpose(pt[:, off:off + 2], rscr2[0:2, si * 128:(si + 1) * 128],
                            cf[0:2, IDE:IDE + 2])
                last = t.transpose(pt[:, off + 2:off + 4],
                                   gbsb[0:2, si * 128:(si + 1) * 128],
                                   cf[0:2, IDE:IDE + 2])
            return last
        roundB("tensor", f)

        def f(v):
            for si in range(8):
                pt, off = p4[si % 4], (si // 4) * 4
                v.tensor_copy(idcols[:, si * 2:si * 2 + 2], pt[:, off:off + 2])
                last = v.tensor_copy(wcols[:, si * 2:si * 2 + 2], pt[:, off + 2:off + 4])
            return last
        roundB("vector", f)

        def f(v):
            for j in range(16):
                last = v.tensor_scalar(rscr[:, j:j + 1], idcols[:, j:j + 1], 0.0, BIGF,
                                       ALU.is_equal, ALU.mult)
            return last
        roundB("vector", f)

        def f(v):
            for j in range(16):
                last = v.tensor_tensor(rscr[:, 16 + j:17 + j], rscr[:, j:j + 1],
                                       idcols[:, j:j + 1], ALU.add)
            return last
        roundB("vector", f)

        def f(v):
            for j in range(16):
                last = v.tensor_scalar_add(hidxf[:, j:j + 1], rscr[:, 16 + j:17 + j], -1.0)
            return last
        roundB("vector", f)
        roundB("vector", lambda v: v.tensor_copy(hidx32[:], hidxf[:]))

        NB_TOTAL = len(BQ)

        scope("P6_shared")
        swgb = [aab(16 * 128) for _ in range(2)]
        swub = [aab(16 * 128) for _ in range(2)]
        smid = aab(NSB * 1024)
        swdb = [aab(NSB * 512) for _ in range(2)]
        P6_END = _aoff[0]

        FF_CNT = [0]
        FT_CNT = [0]

        def p6_dma(isb, par):
            sem = sFF if par == 0 else sFT
            mmw = [0]
            def fd(gp, isb=isb, par=par, sem=sem):
                gp.wait_ge(rq, RN[0])
                if mmw[0]:
                    gp.wait_ge(sMM, mmw[0])
                gp.dma_start(out=swgb[par].rearrange("p (c n) -> p c n", n=128),
                             in_=swg.rearrange("(c p) n -> p c n", p=128)[:, :, isb * 128:(isb + 1) * 128]
                             ).then_inc(sem, 16)
                gp.dma_start(out=swub[par].rearrange("p (c n) -> p c n", n=128),
                             in_=swu.rearrange("(c p) n -> p c n", p=128)[:, :, isb * 128:(isb + 1) * 128]
                             ).then_inc(sem, 16)
                gp.engine_nop().then_inc(rq, 1)
                RN[0] += 1
            mmw[0] = MMC[0]
            block.gpsimd(fd)
            (FF_CNT if par == 0 else FT_CNT)[0] += 32

        sMM = sFV
        sSI = sFS
        sMD = sFQ
        MMC, SIC, MDC = [0], [0], [0]
        MD_OF_SILU = []
        RQ_H6 = RN[0]

        def p6_mmh(isb, half):
            par = isb % 2
            sem = sFF if par == 0 else sFT
            wv = (FF_CNT if par == 0 else FT_CNT)[0]
            r = MMC[0]
            def f(t, isb=isb, half=half, par=par, sem=sem, wv=wv, r=r):
                if r == 0:
                    t.wait_ge(rq, RQ_H6)
                t.wait_ge(sem, wv)
                if r >= 2:
                    t.wait_ge(sMD, r - 1)
                for hb in range(NHB):
                    t.matmul(pw[0][:, (r % 2) * 512:(r % 2 + 1) * 512],
                             swgb[par][:, hb * 128:(hb + 1) * 128],
                             h_fmb[:, hb * TOWN + half * 512:hb * TOWN + (half + 1) * 512],
                             start=(hb == 0), stop=(hb == NHB - 1))
                    mmu = t.matmul(pw[1][:, (r % 2) * 512:(r % 2 + 1) * 512],
                                   swub[par][:, hb * 128:(hb + 1) * 128],
                                   h_fmb[:, hb * TOWN + half * 512:hb * TOWN + (half + 1) * 512],
                                   start=(hb == 0), stop=(hb == NHB - 1))
                mmu.then_inc(sMM, 1)
            block.tensor(f)
            MMC[0] += 1
            return r

        def gu_epih(rmm, dst_mid, extra_rq=None):
            rs = SIC[0]
            war = MD_OF_SILU[rs - 2] + 1 if rs >= 2 else 0
            hh = rmm % 2
            def fs(sc, rmm=rmm, rs=rs, war=war, hh=hh):
                if extra_rq is not None:
                    sc.wait_ge(rq, extra_rq)
                sc.wait_ge(sMM, rmm + 1)
                if war:
                    sc.wait_ge(sMD, war)
                sc.activation(scr2[:, hh * 512:(hh + 1) * 512],
                              pw[0][:, hh * 512:(hh + 1) * 512],
                              AF.Silu).then_inc(sSI, 1)
                for _ in range(6):
                    sc.nop()
            block.scalar(fs)
            SIC[0] += 1

            def fv(v, rs=rs, hh=hh, dst_mid=dst_mid):
                v.wait_ge(sSI, rs + 1)
                v.tensor_tensor(dst_mid, scr2[:, hh * 512:(hh + 1) * 512],
                                pw[1][:, hh * 512:(hh + 1) * 512],
                                ALU.mult).then_inc(sMD, 1)
                for _ in range(6):
                    v.engine_nop()
            block.vector(fv)
            MD_OF_SILU.append(MDC[0])
            MDC[0] += 1

        def p6_epih(isb, half):
            gu_epih(2 * isb + half,
                    smid[:, (2 * isb + half) * 512:(2 * isb + half + 1) * 512])

        B_SCHED = [7, 7, 7, 7, 8, 5, 3, 2, 8, 2, 11, 3, 1] + [0] * (NSB - 13)

        p6_dma(0, 0)
        p6_dma(1, 1)
        emitB(5)
        for isb in range(NSB):
            p6_mmh(isb, 0)
            p6_mmh(isb, 1)
            if isb + 2 < NSB:
                p6_dma(isb + 2, isb % 2)
            p6_epih(isb, 0)
            p6_epih(isb, 1)
            emitB(B_SCHED[isb])

        if debug:
            def fjs(v):
                v.wait_ge(sMD, MDC[0])
                return v.engine_nop()
            round_("vector", fjs)
        tap("smid", smid[:], 128, BF16)

        def p6d_dma(hb2, par, first=False):
            sem = sFF if par == 0 else sFT
            mmw = 0 if first else MMC[0]
            def fd(gp, hb2=hb2, sem=sem, mmw=mmw):
                gp.wait_ge(rq, RN[0])
                if mmw:
                    gp.wait_ge(sMM, mmw)
                for isb in range(NSB):
                    gp.dma_start(out=swdb[par][:, isb * 512:(isb + 1) * 512],
                                 in_=swd[isb * 128:(isb + 1) * 128, hb2 * 512:(hb2 + 1) * 512]
                                 ).then_inc(sem, 16)
                gp.engine_nop().then_inc(rq, 1)
                RN[0] += 1
            block.gpsimd(fd)
            (FF_CNT if par == 0 else FT_CNT)[0] += NSB * 16

        def down_rounds(src_mid, src_wd, nblk, sem, wv, smid_rdy, store_to, wcol=None):
            r = MMC[0]
            def f(t, r=r, sem=sem, wv=wv, smid_rdy=smid_rdy):
                t.wait_ge(sem, wv)
                t.wait_ge(sMD, max(smid_rdy, r - 1))
                pt = pw[r % 2]
                for ib in range(nblk):
                    mmv = t.matmul(pt[:, 0:512], src_mid(ib), src_wd(ib),
                                   start=(ib == 0), stop=(ib == nblk - 1))
                mmv.then_inc(sMM, 1)
            block.tensor(f)
            MMC[0] += 1

            r2 = MDC[0]
            rqe = RN[0]
            def fv(v, r2=r2, rqe=rqe):
                v.wait_ge(sMM, r2 + 1)
                v.wait_ge(rq, rqe)
                if wcol is None:
                    out = scr2.bitcast(BF16)[:, (r2 % 2) * 512:(r2 % 2 + 1) * 512]
                    v.tensor_copy(out, pw[r2 % 2][:, 0:512]).then_inc(sMD, 1)
                else:
                    out = eoutst[:, (r2 % 2) * 512:(r2 % 2 + 1) * 512]
                    v.tensor_scalar(out, pw[r2 % 2][:, 0:512], wcol, 0.0,
                                    ALU.mult, ALU.add).then_inc(sMD, 1)
                for _ in range(6):
                    v.engine_nop()
            block.vector(fv)
            MDC[0] += 1

            def fd2(sync, r2=r2):
                sync.wait_ge(sMD, r2 + 1)
                if wcol is None:
                    src = scr2.bitcast(BF16)[:, (r2 % 2) * 512:(r2 % 2 + 1) * 512]
                else:
                    src = eoutst[:, (r2 % 2) * 512:(r2 % 2 + 1) * 512]
                sync.dma_start(out=store_to, in_=src).then_inc(rq, 16)
                RN[0] += 16
            block.sync(fd2)

        p6d_dma(0, 0, first=True)
        p6d_dma(1, 1, first=True)
        SMID_RDY = MDC[0]
        for hb2 in range(4):
            par = hb2 % 2
            sem = sFF if par == 0 else sFT
            wv = (FF_CNT if par == 0 else FT_CNT)[0]
            for tb in range(8):
                down_rounds(
                    (lambda ib, tb=tb: smid[:, ib * 1024 + tb * 128:ib * 1024 + (tb + 1) * 128]),
                    (lambda ib, par=par: swdb[par][:, ib * 512:(ib + 1) * 512]),
                    NSB, sem, wv, SMID_RDY,
                    ysh_dram[tb * 128:(tb + 1) * 128, hb2 * 512:(hb2 + 1) * 512])
            if hb2 + 2 < 4:
                p6d_dma(hb2 + 2, par)
            if hb2 >= 2:
                emitB(6)
        emitB(len(BQ))
        RBF = RB[0]

        if debug:
            def fjoin(sync):
                sync.wait_ge(rB, RBF)
            block.sync(fjoin)
            tap("lgfm", lgfm[:], 16)
            tap("h_fmb", h_fmb[:], 128, BF16)
            tap("lgtm", lgtm[:])
            tap("vals", vals[:])
            tap("idx4", idx4[:])
            tap("w4", w4[:])
            tap("maskg", maskg[:])
            tap("cntrow", cntrow[:], 1)
            tap("carrow", carrow[:], 1)
            tap("rankg", rankg[:])
            tap("slot4", slot4[:])
            tap("destf", destf[:])
            tap("gidxf", gidxf[:])
            tap("gbsb", gbsb[:], 16)
            tap("myg", myg[:], 2)
            tap("idcols", idcols[:])
            tap("wcols", wcols[:])
            tap("hidxf", hidxf[:])
            tap("pairsA", pairsA[:])

        scope("P7_moe")
        _aoff[0] = FFN_MARK
        xg_fm = aab(NHB * 1024)
        xg_tm = aab(2 * HD)
        wgb = [aab(16 * 128) for _ in range(2)]
        wub = [aab(16 * 128) for _ in range(2)]
        wdbuf = [aab(NIB * 512) for _ in range(2)]
        midb = [aab(NIB * 1024) for _ in range(2)]
        eoutst = rscr2.bitcast(BF16)[:, 0:1024]

        for e01 in range(2):
            mm_e0 = MMC[0]
            for si in range(8):
                def f(v, si=si, e01=e01, mm_e0=mm_e0):
                    if e01 == 1 and si == 0:
                        v.wait_ge(sMM, mm_e0)
                    return v.memset(xg_tm[:, (si % 2) * HD:(si % 2 + 1) * HD], 0.0)
                round_("vector", f)

                def fg(gp, e01=e01, si=si):
                    gp.wait_ge(rq, RN[0])
                    gp.wait_ge(sCC, 1)
                    if e01 == 0 and si == 0:
                        gp.wait_ge(rB, RBF)
                    breg = gp.to_reg(NCORE * TOWN - 1)
                    gp.indirect_dma_start(
                        xg_tm[:, (si % 2) * HD:(si % 2 + 1) * HD], None, h_full[:],
                        IndirectOffsetOnAxis(hidx32[:, si * 2 + e01:si * 2 + e01 + 1], 0),
                        bounds_check=breg, oob_is_err=False,
                    ).then_inc(qG, 16)
                    QG[0] += 16
                    gp.engine_nop().then_inc(rq, 1)
                    RN[0] += 1
                block.gpsimd(fg)
                QGXG = QG[0]

                def f(t, si=si, e01=e01, QGXG=QGXG):
                    t.wait_ge(qG, QGXG)
                    if e01 == 0 and si == 0:
                        t.wait_ge(rB, RBF)
                    for hb in range(NHB):
                        g = hb // 4
                        last = t.transpose(p4[g].bitcast(BF16)[:, (hb % 4) * 256:(hb % 4) * 256 + 128],
                                           xg_tm[:, (si % 2) * HD + hb * 128:(si % 2) * HD + (hb + 1) * 128],
                                           cb[:, 0:128])
                    return last
                round_("tensor", f)

                def f(v, si=si):
                    for g in range(4):
                        dst = xg_fm.rearrange("p (hb t) -> p hb t", t=1024)[:, 4 * g:4 * g + 4,
                                                                           si * 128:(si + 1) * 128]
                        src = p4[g].bitcast(BF16).rearrange("p (k t) -> p k t", t=256)[:, :, 0:128]
                        last = v.tensor_copy(dst, src)
                    return last
                round_("vector", f)

            if e01 == 0:
                tap("xg_fm_e0", xg_fm[:], 128, BF16)

            def p7_dma(ib, par, e01=e01):
                sem = sFF if par == 0 else sFT
                mmw = MMC[0]
                def fd(gp, ib=ib, par=par, sem=sem, mmw=mmw):
                    gp.wait_ge(rq, RN[0])
                    if mmw:
                        gp.wait_ge(sMM, mmw)
                    gp.dma_start(out=wgb[par].rearrange("p (c n) -> p c n", n=128),
                                 in_=wg_my[e01].rearrange("(c p) n -> p c n", p=128)[:, :, ib * 128:(ib + 1) * 128]
                                 ).then_inc(sem, 16)
                    gp.dma_start(out=wub[par].rearrange("p (c n) -> p c n", n=128),
                                 in_=wu_my[e01].rearrange("(c p) n -> p c n", p=128)[:, :, ib * 128:(ib + 1) * 128]
                                 ).then_inc(sem, 16)
                    gp.engine_nop().then_inc(rq, 1)
                    RN[0] += 1
                block.gpsimd(fd)
                (FF_CNT if par == 0 else FT_CNT)[0] += 32

            def p7_mmh(ib, half, e01=e01):
                par = ib % 2
                sem = sFF if par == 0 else sFT
                wv = (FF_CNT if par == 0 else FT_CNT)[0]
                r = MMC[0]
                rqx = RN[0]
                mdw = MDC[0]
                def f(t, ib=ib, half=half, par=par, sem=sem, wv=wv, r=r, rqx=rqx,
                      mdw=mdw, e01=e01):
                    if ib == 0 and half == 0:
                        t.wait_ge(rq, rqx)
                        if e01 == 0:
                            t.wait_ge(sMD, mdw)
                    t.wait_ge(sem, wv)
                    if r >= 2:
                        t.wait_ge(sMD, r - 1)
                    for hb in range(NHB):
                        t.matmul(pw[0][:, (r % 2) * 512:(r % 2 + 1) * 512],
                                 wgb[par][:, hb * 128:(hb + 1) * 128],
                                 xg_fm[:, hb * 1024 + half * 512:hb * 1024 + (half + 1) * 512],
                                 start=(hb == 0), stop=(hb == NHB - 1))
                        mmu = t.matmul(pw[1][:, (r % 2) * 512:(r % 2 + 1) * 512],
                                       wub[par][:, hb * 128:(hb + 1) * 128],
                                       xg_fm[:, hb * 1024 + half * 512:hb * 1024 + (half + 1) * 512],
                                       start=(hb == 0), stop=(hb == NHB - 1))
                    mmu.then_inc(sMM, 1)
                block.tensor(f)
                MMC[0] += 1
                return r

            p7_dma(0, 0)
            p7_dma(1, 1)
            for ib in range(NIB):
                ra = p7_mmh(ib, 0)
                rb_ = p7_mmh(ib, 1)
                if ib + 2 < NIB:
                    p7_dma(ib + 2, ib % 2)
                gu_epih(ra, midb[e01][:, ib * 1024:ib * 1024 + 512],
                        extra_rq=(RN[0] if (e01 == 0 and ib == 0) else None))
                gu_epih(rb_, midb[e01][:, ib * 1024 + 512:ib * 1024 + 1024])

        def p7d_dma(s, first=False):
            hb2, e01, par = s >> 1, s & 1, s % 2
            sem = sFF if par == 0 else sFT
            mmw = 0 if first else MMC[0]
            def fd(gp, hb2=hb2, e01=e01, par=par, sem=sem, mmw=mmw):
                gp.wait_ge(rq, RN[0])
                if mmw:
                    gp.wait_ge(sMM, mmw)
                for ib in range(NIB):
                    gp.dma_start(out=wdbuf[par][:, ib * 512:(ib + 1) * 512],
                                 in_=wd_my[e01][ib * 128:(ib + 1) * 128, hb2 * 512:(hb2 + 1) * 512]
                                 ).then_inc(sem, 16)
                gp.engine_nop().then_inc(rq, 1)
                RN[0] += 1
            block.gpsimd(fd)
            (FF_CNT if par == 0 else FT_CNT)[0] += NIB * 16

        p7d_dma(0, first=True)
        p7d_dma(1, first=True)
        MID_RDY = MDC[0]
        for s in range(8):
            hb2, e01, par = s >> 1, s & 1, s % 2
            sem = sFF if par == 0 else sFT
            wv = (FF_CNT if par == 0 else FT_CNT)[0]
            for sb in range(8):
                down_rounds(
                    (lambda ib, e01=e01, sb=sb: midb[e01][:, ib * 1024 + sb * 128:ib * 1024 + (sb + 1) * 128]),
                    (lambda ib, par=par: wdbuf[par][:, ib * 512:(ib + 1) * 512]),
                    NIB, sem, wv, MID_RDY,
                    eout_myh[hb2][e01 * CAP + sb * 128:e01 * CAP + (sb + 1) * 128, :],
                    wcol=wcols[:, sb * 2 + e01:sb * 2 + e01 + 1])
            if s + 2 < 8:
                p7d_dma(s + 2)
            if e01 == 1:
                def fg(gp, hb2=hb2):
                    gp.wait_ge(rq, RN[0])
                    gp.collective_compute("AllGather", ALU.bypass,
                                          replica_groups=[list(range(NCORE))],
                                          ins=[eout_myh[hb2].opt()],
                                          outs=[eout_allh[hb2].opt()]).then_inc(sCC, 1)
                block.gpsimd(fg)

        def fg_drain(gp):
            gp.collective_compute("AllGather", ALU.bypass,
                                  replica_groups=[list(range(NCORE))],
                                  ins=[cnt_my.opt()],
                                  outs=[cnt_all.opt()]).then_inc(sCC, 1)
        block.gpsimd(fg_drain)

        scope("P8_final")
        _aoff[0] = FFN_MARK
        y4f = aa(8 * HD)
        y4s = aa(4 * HD)
        y4sb = y4s.bitcast(BF16)
        stats = aa(8 * 16)
        htile = y4s[:, 0:2 * HD]
        ysht = y4sb[:, 4 * HD:6 * HD]

        def f(v):
            for q in range(4):
                last = v.memset(y4f[:, q * HD:(q + 1) * HD], 0.0)
            return last
        round_("vector", f)

        def f(v):
            for q in range(4, 8):
                last = v.memset(y4f[:, q * HD:(q + 1) * HD], 0.0)
            return last
        round_("vector", f)

        for t0 in (0, 2, 4, 6):
            def fd(sync, t0=t0):
                sync.wait_ge(rq, RN[0])
                for j in range(2):
                    tt = t0 + j
                    sync.dma_start(out=htile[:, j * HD:(j + 1) * HD],
                                   in_=htm_f32[tt * 128:(tt + 1) * 128, :]).then_inc(rq, 16)
                    sync.dma_start(out=ysht[:, j * HD:(j + 1) * HD],
                                   in_=ysh_dram[tt * 128:(tt + 1) * 128, :]).then_inc(rq, 16)
                RN[0] += 64
            block.sync(fd)

            if t0 == 0:
                tap("ysht0", ysht[:, 0:HD], 128, BF16)
                tap("htile0", htile[:, 0:HD])

            def f(v, t0=t0):
                v.tensor_tensor(y4f[:, t0 * HD:(t0 + 1) * HD],
                                y4f[:, t0 * HD:(t0 + 1) * HD], htile[:, 0:HD], ALU.add)
                return v.tensor_tensor(y4f[:, (t0 + 1) * HD:(t0 + 2) * HD],
                                       y4f[:, (t0 + 1) * HD:(t0 + 2) * HD],
                                       htile[:, HD:2 * HD], ALU.add)
            round_("vector", f)

            def f(v, t0=t0):
                v.tensor_tensor(y4f[:, t0 * HD:(t0 + 1) * HD],
                                y4f[:, t0 * HD:(t0 + 1) * HD], ysht[:, 0:HD], ALU.add)
                return v.tensor_tensor(y4f[:, (t0 + 1) * HD:(t0 + 2) * HD],
                                       y4f[:, (t0 + 1) * HD:(t0 + 2) * HD],
                                       ysht[:, HD:2 * HD], ALU.add)
            round_("vector", f)

        round_("vector", lambda v: v.memset(y4s[:, 0:2 * HD], 0.0))
        round_("vector", lambda v: v.memset(y4s[:, 2 * HD:4 * HD], 0.0))

        for h in range(4):
            def fg(gp, h=h):
                gp.wait_ge(rq, RN[0])
                gp.wait_ge(sCC, 6)
                breg = gp.to_reg(EXP * CAP - 1)
                for tt in range(8):
                    for k in range(4):
                        s = tt * 4 + k
                        gp.indirect_dma_start(
                            y4sb[:, s * 512:(s + 1) * 512],
                            None, eout_allh[h][:],
                            IndirectOffsetOnAxis(gidx32[:, tt * 4 + k:tt * 4 + k + 1], 0),
                            bounds_check=breg, oob_is_err=False,
                        ).then_inc(qG, 16)
                QG[0] += 32 * 16
                gp.engine_nop().then_inc(rq, 1)
                RN[0] += 1
            block.gpsimd(fg)

            wv = QG[0]
            def fa(v, h=h, wv=wv):
                v.wait_ge(qG, wv)
                for k in range(4):
                    for tt in range(8):
                        s = tt * 4 + k
                        v.tensor_tensor(
                            y4f[:, tt * HD + h * 512:tt * HD + (h + 1) * 512],
                            y4f[:, tt * HD + h * 512:tt * HD + (h + 1) * 512],
                            y4sb[:, s * 512:(s + 1) * 512],
                            ALU.add)
                for q in range(4):
                    last = v.memset(y4s[:, q * HD:(q + 1) * HD], 0.0)
                return last
            round_("vector", fa)

        if debug:
            tap("y4a0", y4f[:, 0:HD])

        def f(v):
            for tt in range(8):
                last = v.tensor_reduce(stats[:, tt * 16 + 0:tt * 16 + 1],
                                       y4f[:, tt * HD:(tt + 1) * HD], AX.X, ALU.add)
            return last
        round_("vector", f)

        def f2(sc):
            for tt in range(4):
                last = sc.activation(y4s[:, tt * HD:(tt + 1) * HD],
                                     y4f[:, tt * HD:(tt + 1) * HD], AF.Square,
                                     accum_out=stats[:, tt * 16 + 1:tt * 16 + 2])
            return last
        round_("scalar", f2)

        def f2(sc):
            for tt in range(4, 8):
                last = sc.activation(y4s[:, (tt - 4) * HD:(tt - 3) * HD],
                                     y4f[:, tt * HD:(tt + 1) * HD], AF.Square,
                                     accum_out=stats[:, tt * 16 + 1:tt * 16 + 2])
            return last
        round_("scalar", f2)

        def f(v):
            for tt in range(8):
                v.tensor_scalar_mul(stats[:, tt * 16 + 2:tt * 16 + 3],
                                    stats[:, tt * 16 + 0:tt * 16 + 1], 1.0 / HD)
                last = v.tensor_scalar_mul(stats[:, tt * 16 + 3:tt * 16 + 4],
                                           stats[:, tt * 16 + 1:tt * 16 + 2], 1.0 / HD)
            return last
        round_("vector", f)

        def f(v):
            for tt in range(8):
                last = v.tensor_tensor(stats[:, tt * 16 + 4:tt * 16 + 5],
                                       stats[:, tt * 16 + 2:tt * 16 + 3],
                                       stats[:, tt * 16 + 2:tt * 16 + 3], ALU.mult)
            return last
        round_("vector", f)

        def f(v):
            for tt in range(8):
                last = v.tensor_tensor(stats[:, tt * 16 + 5:tt * 16 + 6],
                                       stats[:, tt * 16 + 3:tt * 16 + 4],
                                       stats[:, tt * 16 + 4:tt * 16 + 5], ALU.subtract)
            return last
        round_("vector", f)

        def f(sc):
            for tt in range(8):
                last = sc.activation(stats[:, tt * 16 + 6:tt * 16 + 7],
                                     stats[:, tt * 16 + 5:tt * 16 + 6], AF.Ln, bias=LN_EPS)
            return last
        round_("scalar", f)

        def f(sc):
            for tt in range(8):
                last = sc.activation(stats[:, tt * 16 + 7:tt * 16 + 8],
                                     stats[:, tt * 16 + 6:tt * 16 + 7], AF.Exp, scale=-0.5)
            return last
        round_("scalar", f)

        def f(v):
            for tt in range(8):
                last = v.tensor_scalar(y4f[:, tt * HD:(tt + 1) * HD],
                                       y4f[:, tt * HD:(tt + 1) * HD],
                                       stats[:, tt * 16 + 2:tt * 16 + 3],
                                       stats[:, tt * 16 + 7:tt * 16 + 8],
                                       ALU.subtract, ALU.mult)
            return last
        round_("vector", f)

        for _ in range(2):
            round_("vector", lambda v: v.engine_nop())

        def fd(sync):
            sync.wait_ge(rq, RN[0])
            for tt in range(8):
                sync.dma_start(out=out_own[tt * 128:(tt + 1) * 128, :],
                               in_=y4f[:, tt * HD:(tt + 1) * HD]).then_inc(rq, 16)
            RN[0] += 128
        block.sync(fd)

        tap("statsall", stats[:])

        def fin(sync):
            sync.wait_ge(rq, RN[0])
        block.sync(fin)
        scope(None)

    return nc


def _consts():
    c = {}
    c["ident_f"] = np.eye(128, dtype=np.float32)
    c["ident_b"] = np.eye(128, dtype=np.float32)
    c["m_triuinc"] = np.triu(np.ones((128, 128), np.float32))
    c["m_striu"] = np.triu(np.ones((128, 128), np.float32), 1)
    c["m_striun"] = -np.triu(np.ones((128, 128), np.float32), 1)
    c["m_striln"] = -np.tril(np.ones((128, 128), np.float32), -1)
    c["onescol"] = np.ones((128, 1), np.float32)
    e = np.zeros((128, 1), np.float32); e[127, 0] = 1.0
    c["e127c"] = e
    c["iota128"] = np.arange(128, dtype=np.float32).reshape(128, 1)
    c["iota16g"] = np.tile(np.arange(16, dtype=np.float32), (128, 1))
    c["iota64c"] = np.arange(64, dtype=np.float32).reshape(64, 1)
    c["fconst_d"] = np.tile(np.array([[1e-6, 1e-5, 1e-8]], np.float32), (128, 1))
    return c


def host_inmaps(inputs):
    import ml_dtypes
    x = np.asarray(inputs["L0_attn_x"])[0]
    Wk = np.asarray(inputs["L0_attn_Wk"]); Wv = np.asarray(inputs["L0_attn_Wv"])
    Wq = np.asarray(inputs["L0_attn_Wq"]); Wa = np.asarray(inputs["L0_attn_Wa_w"])
    wab = np.asarray(inputs["L0_attn_Wa_b"]).reshape(1, D)
    lam = np.asarray(inputs["L0_attn_lam"]).reshape(1, D)
    normw = np.asarray(inputs["L0_attn_norm_w"]).reshape(1, D)
    Wo = np.asarray(inputs["L0_attn_Wo"])
    gw = np.asarray(inputs["L0_ffn_gate_w"])
    w_gate = np.asarray(inputs["L0_ffn_w_gate"]); w_up = np.asarray(inputs["L0_ffn_w_up"])
    w_down = np.asarray(inputs["L0_ffn_w_down"])
    swg = np.ascontiguousarray(np.asarray(inputs["L0_ffn_sw_gate"]))
    swu = np.ascontiguousarray(np.asarray(inputs["L0_ffn_sw_up"]))
    swd = np.ascontiguousarray(np.asarray(inputs["L0_ffn_sw_down"]))

    wcatT = np.ascontiguousarray(np.concatenate([Wk, Wv, Wq, Wa], 0).T)
    woT = np.ascontiguousarray(Wo.T)
    gwT = np.ascontiguousarray(gw.T)
    rowp = np.ascontiguousarray(np.concatenate(
        [lam, wab, normw, np.zeros((1, D), np.float32)], 1))
    consts = _consts()

    in_maps = []
    for c in range(NCORE):
        pm = np.zeros((8, 1), np.float32); pm[:c] = 1.0
        sel = np.zeros((16, 2), np.float32)
        sel[2 * c, 0] = 1.0; sel[2 * c + 1, 1] = 1.0
        cv = np.array([[c * TOWN, c * NCH, c, 0]], np.float32)
        m = dict(
            x_own=np.ascontiguousarray(x[c * TOWN:(c + 1) * TOWN]),
            wcatT=wcatT, woT=woT, gwT=gwT,
            wg_my=np.ascontiguousarray(w_gate[2 * c:2 * c + 2]),
            wu_my=np.ascontiguousarray(w_up[2 * c:2 * c + 2]),
            wd_my=np.ascontiguousarray(w_down[2 * c:2 * c + 2]),
            swg=swg, swu=swu, swd=swd,
            rowp=rowp, premask=pm, esel=sel, cvec=cv,
            **consts,
        )
        m["ident_b"] = np.eye(128).astype(ml_dtypes.bfloat16)
        in_maps.append(m)
    return in_maps


def kernel(**inputs):
    in_maps = host_inmaps(inputs)
    nc = build_nc(debug=False)
    res = run_bass_kernel_spmd(nc, in_maps, core_ids=list(range(NCORE)))
    out = np.concatenate([r["out_own"] for r in res.results], 0)
    return out.reshape(1, NCORE * TOWN, HD).astype(np.float32)


# version 38
# speedup vs baseline: 1.0842x; 1.0703x over previous
import contextlib
import numpy as np
import concourse.bass as bass
import concourse.mybir as mybir
from concourse.bass_utils import run_bass_kernel_spmd
from concourse.bass_core_rust import IndirectOffsetOnAxis

F32 = mybir.dt.float32
BF16 = mybir.dt.bfloat16
I32 = mybir.dt.int32
AF = mybir.ActivationFunctionType
ALU = mybir.AluOpType
AX = mybir.AxisListType

NCORE = 8
TOWN = 1024
NCH = 8
L = 128
D = 64
HD = 2048
NHB = HD // 128
EXP = 16
CAP = 1024
IE = 1408
NIB = IE // 128
ISH = 2816
NSB = ISH // 128
RMS_EPS = 1e-6
LN_EPS = 1e-5
SURSC = 1.0 / (1.0 + 1e-6)
BIGF = 1.0e6


def build_nc(debug=False, poison=False):
    nc = bass.Bass("TRN2", target_bir_lowering=False)
    es = contextlib.ExitStack()

    dr = lambda n, s, d=F32, k="ExternalInput": nc.dram_tensor(n, s, d, kind=k).ap()
    x_own = dr("x_own", [TOWN, HD])
    wcatT = dr("wcatT", [HD, 256])
    woT = dr("woT", [D, HD])
    gwT = dr("gwT", [HD, EXP])
    wg_my = dr("wg_my", [2, HD, IE])
    wu_my = dr("wu_my", [2, HD, IE])
    wd_my = dr("wd_my", [2, IE, HD])
    swg = dr("swg", [HD, ISH])
    swu = dr("swu", [HD, ISH])
    swd = dr("swd", [ISH, HD])
    rowp = dr("rowp", [1, 4 * D])
    ident_f = dr("ident_f", [128, 128])
    ident_b = dr("ident_b", [128, 128], BF16)
    m_triuinc = dr("m_triuinc", [128, 128])
    m_striu = dr("m_striu", [128, 128])
    m_striun = dr("m_striun", [128, 128])
    m_striln = dr("m_striln", [128, 128])
    onescol = dr("onescol", [128, 1])
    e127c = dr("e127c", [128, 1])
    iota128 = dr("iota128", [128, 1])
    iota16g = dr("iota16g", [128, 16])
    iota64c = dr("iota64c", [64, 1])
    premask = dr("premask", [8, 1])
    esel = dr("esel", [16, 2])
    cvec = dr("cvec", [1, 4])
    fconst_d = dr("fconst_d", [128, 3])
    out_own = dr("out_own", [TOWN, HD], F32, "ExternalOutput")
    dbg = {}
    if debug:
        dbg["prep"] = dr("dbg_prep", [TOWN, 256], F32, "ExternalOutput")
        dbg["errs"] = dr("dbg_errs", [TOWN, 1], F32, "ExternalOutput")
        dbg["y"] = dr("dbg_y", [TOWN, D], F32, "ExternalOutput")
        dbg["h"] = dr("dbg_h", [TOWN, HD], F32, "ExternalOutput")
        dbg["rt"] = dr("dbg_rt", [TOWN, 12], F32, "ExternalOutput")
        dbg["gb"] = dr("dbg_gb", [EXP * 1025, 2], F32, "ExternalOutput")

    di = lambda n, s, d=F32: nc.dram_tensor(n, s, d, kind="Internal").ap()
    ds = lambda n, s, d=F32: nc.dram_tensor(n, s, d, kind="Internal",
                                            addr_space="Shared").ap()
    ab_my = di("ab_my", [NCH * D, 130])
    ab_allp = [ds(f"ab_all{p}", [NCORE * NCH * D, 130]) for p in range(2)]
    h0_dram = di("h0_dram", [64 * D, D])
    htm_bf_my = di("htm_bf_my", [TOWN, HD], BF16)
    htm_f32 = di("htm_f32", [TOWN, HD])
    ysh_dram = di("ysh_dram", [TOWN, HD], BF16)
    h_full = ds("h_full", [NCORE * TOWN, HD], BF16)
    cnt_my = di("cnt_my", [1, EXP])
    cnt_all = ds("cnt_all", [NCORE, EXP])
    gbuf_my = di("gbuf_my", [EXP * 1025, 2])
    gbuf_all = ds("gbuf_all", [EXP * 1025, 2])
    eout_myh = [di(f"eout_myh{h}", [2 * CAP, 512], BF16) for h in range(4)]
    eout_allh = [ds(f"eout_allh{h}", [EXP * CAP, 512], BF16) for h in range(4)]

    with es:
        sbA = lambda n, s, d=F32: es.enter_context(nc.sbuf_tensor(n, s, d))
        psA = lambda n, s, d=F32: es.enter_context(nc.psum_tensor(n, s, d))

        cf = sbA("cf", [128, 128 * 5])
        cb = sbA("cb", [128, 128], BF16)
        onesc = sbA("onesc", [128, 1])
        e127 = sbA("e127", [128, 1])
        iotac = sbA("iotac", [128, 1])
        iota16 = sbA("iota16", [128, 16])
        iota64 = sbA("iota64", [64, 1])
        prem = sbA("prem", [8, 1])
        eselt = sbA("eselt", [16, 2])
        cvt = sbA("cvt", [1, 4])
        rowpar = sbA("rowpar", [1, 4 * D])
        lnsig = sbA("lnsig", [1, D])
        bc64 = sbA("bc64", [128, 2 * D])
        fcon = sbA("fcon", [128, 3])
        gw_sb = sbA("gw_sb", [128, NHB * EXP])
        lgfm = sbA("lgfm", [16, TOWN])
        desti32 = sbA("desti32", [128, 32], I32)
        gidx32 = sbA("gidx32", [128, 32], I32)
        hidx32 = sbA("hidx32", [128, 16], I32)
        fmb = sbA("fmb", [128, NHB * TOWN], BF16)
        stS = sbA("stS", [128, NCH])
        errs = sbA("errs", [128, NCH])
        glc = sbA("glc", [64, NCH])
        idxg = sbA("idxg", [64, NCH], I32)
        scr = sbA("scr", [128, 1024])
        scr2 = sbA("scr2", [128, 2048])

        ARENA_COLS = 39300
        arena = sbA("arena", [128, ARENA_COLS])
        arena_bf = arena.bitcast(BF16)
        _aoff = [0]

        def aa(cols):
            s = arena[:, _aoff[0]:_aoff[0] + cols]
            _aoff[0] += cols
            assert _aoff[0] <= ARENA_COLS, f"arena overflow {_aoff[0]}"
            return s

        def aab(cols_bf):
            assert cols_bf % 2 == 0
            s = arena_bf[:, 2 * _aoff[0]:2 * _aoff[0] + cols_bf]
            _aoff[0] += cols_bf // 2
            return s

        def arena_reset():
            _aoff[0] = 0

        x_fm = aa(NHB * TOWN)
        x_fmb = fmb
        h_fmb = fmb
        MARK0 = _aoff[0]
        wcat_sb = aab(NHB * 256)
        xstg = aa(HD)
        proj = aa(NCH * 256)
        _aoff[0] = MARK0 + 6600
        stKV = aa(NCH * 128)
        stQt = aa(NCH * D)
        stQh = aa(NCH * D)
        stU0 = aa(NCH * D)
        stUc = aa(NCH * D)
        stG = aa(NCH * D)
        stLA = aa(NCH * D)
        fmQt = aa(NCH * L)[0:64, :]
        fmKt = aa(NCH * L)[0:64, :]
        fmQh = aa(NCH * L)[0:64, :]
        fmU = aa(NCH * L)[0:64, :]
        fmZ = aa(NCH * L)[0:64, :]
        solU = aa(NCH * L)
        solW = aa(NCH * L)
        solR = aa(NCH * 128)
        gP = aa(NCH * L)
        stM = aa(NCH * D)
        abATs = aa(NCH * D)[0:64, :]
        abBgs = aa(NCH * D)[0:64, :]
        h0my = aa(NCH * D)[0:64, :]
        SCAN_ARENA_END = _aoff[0]
        _aoff[0] = MARK0
        h0ping = aa(2 * D)[0:64, :]
        chABt = aa(2 * 130)[0:64, :]
        yattn = aa(NCH * D)
        ynfm = aab(TOWN)[0:64, :]
        wo_fm = aab(HD)[0:64, :]
        nwcol = aa(1)[0:64, :]
        rowa = aa(2048)[0:1, :]
        hstg_bf = aab(2 * HD)
        assert _aoff[0] <= MARK0 + 6600, f"overlay overflow {_aoff[0] - MARK0}"
        _aoff[0] = SCAN_ARENA_END

        p4 = [psA(f"p4_{i}", [128, 512]) for i in range(4)]
        pw = [psA(f"pw_{i}", [128, 1024]) for i in range(2)]

        def pslot(c):
            if c < 4:
                return p4[c], 0
            return pw[(c - 4) // 2], ((c - 4) % 2) * 512

        rq = es.enter_context(nc.semaphore("rq"))
        RN = [0]
        qG = es.enter_context(nc.semaphore("qG"))
        QG = [0]
        sXin = es.enter_context(nc.semaphore("sXin"))
        sTx = es.enter_context(nc.semaphore("sTx"))
        sVx = es.enter_context(nc.semaphore("sVx"))
        sFF = es.enter_context(nc.semaphore("sFF"))
        sFT = es.enter_context(nc.semaphore("sFT"))
        sFV = es.enter_context(nc.semaphore("sFV"))
        sFS = es.enter_context(nc.semaphore("sFS"))
        sFQ = es.enter_context(nc.semaphore("sFQ"))
        sEO = es.enter_context(nc.semaphore("sEO"))
        sCC = es.enter_context(nc.semaphore("sCC"))

        TAPS = {}

        nc.const_aps.register(F32, RMS_EPS, fcon[:, 0:1])
        nc.const_aps.register(F32, LN_EPS, fcon[:, 1:2])
        nc.const_aps.register(F32, 1e-8, fcon[:, 2:3])
        block = es.enter_context(nc.Block())
        IDE, TRI, SUU, SUN, SLN = 0, 128, 256, 384, 512

        def tap(name, ap, parts=128, dt=F32):
            if not debug:
                return
            cols = ap.shape[-1]
            t_d = nc.dram_tensor("tap_" + name, [parts, cols], dt, kind="ExternalOutput").ap()
            TAPS[name] = t_d

            def fdma(sync, ap=ap, t_d=t_d):
                sync.wait_ge(rq, RN[0])
                sync.dma_start(out=t_d[:], in_=ap, allow_slow_non_contiguous=True).then_inc(rq, 16)
                RN[0] += 16
            block.sync(fdma)

        def round_(engname, fn, inc=1):
            def body(eng):
                eng.wait_ge(rq, RN[0])
                last = fn(eng)
                last.then_inc(rq, inc)
                if engname == "vector":
                    for _ in range(6):
                        eng.engine_nop()
                elif engname == "scalar":
                    for _ in range(6):
                        eng.nop()
            getattr(block, engname)(body)
            RN[0] += inc

        if poison:
            PV = 1.0e38
            q4 = ARENA_COLS // 4
            for i in range(4):
                a0, a1 = i * q4, (ARENA_COLS if i == 3 else (i + 1) * q4)
                round_("vector", lambda v, a0=a0, a1=a1: v.memset(arena[:, a0:a1], PV))
            for tns in (scr, scr2, gw_sb, lgfm, stS, errs, glc, fcon, bc64, lnsig,
                        rowpar, cvt, eselt, prem, iota64, iota16, iotac, e127,
                        onesc, cf):
                round_("vector", lambda v, t_=tns: v.memset(t_[:], PV))
            round_("vector", lambda v: v.memset(cb[:], PV))
            round_("vector", lambda v: v.memset(fmb[:], PV))
            for it in (desti32, gidx32, hidx32, idxg):
                round_("vector", lambda v, t_=it: v.memset(t_[:], 1 << 30))
            for pt in p4 + pw:
                round_("vector", lambda v, t_=pt: v.memset(t_[:], PV))

            def poison_dram(sync):
                sync.wait_ge(rq, RN[0])
                n = 0
                s2b = scr2.bitcast(BF16)
                for t_, isbf in ((ab_my, 0), (ab_allp[0], 0), (ab_allp[1], 0),
                                 (h0_dram, 0), (htm_f32, 0),
                                 (cnt_my, 0), (cnt_all, 0), (gbuf_my, 0), (gbuf_all, 0),
                                 (htm_bf_my, 1), (ysh_dram, 1), (h_full, 1),
                                 *[(t, 1) for t in eout_myh],
                                 *[(t, 1) for t in eout_allh]):
                    rows, cols = int(t_.shape[0]), int(t_.shape[1])
                    src = s2b if isbf else scr2
                    for r0 in range(0, rows, 128):
                        r1 = min(rows, r0 + 128)
                        sync.dma_start(out=t_[r0:r1, :], in_=src[0:r1 - r0, 0:cols],
                                       allow_slow_non_contiguous=True).then_inc(rq, 16)
                        n += 16
                RN[0] += n
            block.sync(poison_dram)

        _scopes = []

        def scope(name):
            if _scopes:
                pn, sid = _scopes.pop()
                nc.leave_named_scope(pn, sid, False)
            if name is not None:
                sid, _ = nc.enter_named_scope(name, False)
                _scopes.append((name, sid))

        scope("P01_proj")

        def p0_consts(sync):
            sync.wait_ge(rq, RN[0])
            n = 0
            for dst, src in [(cf[:, 0:128], ident_f), (cf[:, 128:256], m_triuinc),
                             (cf[:, 256:384], m_striu), (cf[:, 384:512], m_striun),
                             (cf[:, 512:640], m_striln), (cb[:, 0:128], ident_b),
                             (onesc[:], onescol), (e127[:], e127c), (iotac[:], iota128),
                             (iota16[:], iota16g), (iota64[:], iota64c), (prem[:], premask),
                             (eselt[:], esel), (cvt[:], cvec), (rowpar[:], rowp),
                             (fcon[:], fconst_d)]:
                sync.dma_start(out=dst, in_=src[:]).then_inc(rq, 16)
                n += 16
            RN[0] += n
        block.sync(p0_consts)

        def p0_wcat(gp):
            gp.wait_ge(rq, RN[0])
            gp.dma_start(out=wcat_sb.rearrange("p (c n) -> p c n", n=256),
                         in_=wcatT.rearrange("(c p) n -> p c n", p=128)).then_inc(qG, 16)
            gp.dma_start(out=gw_sb.rearrange("p (c n) -> p c n", n=EXP),
                         in_=gwT.rearrange("(c p) n -> p c n", p=128)).then_inc(qG, 16)
            QG[0] += 32
            gp.engine_nop().then_inc(rq, 1)
            RN[0] += 1
        block.gpsimd(p0_wcat)

        round_("scalar", lambda sc: sc.activation(scr[0:1, 0:D], rowpar[:, 0:D], AF.Exp, scale=-1.0))

        round_("vector", lambda v: v.tensor_scalar_add(scr[0:1, 2 * D:3 * D], scr[0:1, 0:D], 1.0))
        round_("vector", lambda v: v.reciprocal(scr[0:1, D:2 * D], scr[0:1, 2 * D:3 * D]))
        round_("scalar", lambda sc: sc.activation(lnsig[:], scr[0:1, D:2 * D], AF.Ln, bias=1e-8))

        def f(t):
            t.matmul(p4[0][:, 0:D], cf[0:1, TRI:TRI + 128], lnsig[:], start=True, stop=True)
            return t.matmul(p4[0][:, D:2 * D], cf[0:1, TRI:TRI + 128], rowpar[0:1, D:2 * D],
                            start=True, stop=True)
        round_("tensor", f)
        round_("vector", lambda v: v.tensor_copy(bc64[:], p4[0][:, 0:2 * D]))
        tap("bc64", bc64[:])
        tap("lnsig", lnsig[:], 1)

        def p1_sync(sync):
            sync.wait_ge(rq, RN[0])
            for tc in range(NCH):
                if tc >= 1:
                    sync.wait_ge(sTx, tc * 4)
                sync.dma_start(out=xstg[:, 0:HD],
                               in_=x_own[tc * 128:(tc + 1) * 128, :]).then_inc(sXin, 16)
        block.sync(p1_sync)

        def p1_t(t):
            t.wait_ge(rq, RN[0])
            for tc in range(NCH):
                t.wait_ge(sXin, (tc + 1) * 16)
                for hb in range(NHB):
                    g = hb // 4
                    bank = p4[g]
                    if tc > 0 and hb % 4 == 0:
                        t.wait_ge(sVx, (tc - 1) * 4 + g + 1)
                    mm = t.transpose(bank[:, (hb % 4) * 128:(hb % 4 + 1) * 128],
                                     xstg[:, hb * 128:(hb + 1) * 128],
                                     cf[:, IDE:IDE + 128])
                    if hb % 4 == 3:
                        mm.then_inc(sTx, 1)
        block.tensor(p1_t)

        def p1_v(v):
            v.wait_ge(rq, RN[0])
            for tc in range(NCH):
                for g in range(4):
                    v.wait_ge(sTx, tc * 4 + g + 1)
                    dst = x_fm.rearrange("p (hb t) -> p hb t", t=TOWN)[:, 4 * g:4 * g + 4,
                                                                      tc * 128:(tc + 1) * 128]
                    src = p4[g].rearrange("p (k t) -> p k t", t=128)
                    v.tensor_copy(dst, src).then_inc(sVx, 1)
            v.wait_ge(sVx, 32)
            for g in range(4):
                v.tensor_copy(x_fmb[:, g * 4096:(g + 1) * 4096],
                              x_fm[:, g * 4096:(g + 1) * 4096])
            v.engine_nop().then_inc(rq, 1)
        block.vector(p1_v)
        RN[0] += 1

        def p1_tp(t):
            t.wait_ge(rq, RN[0])
            t.wait_ge(qG, 32)
            for tc in range(NCH):
                pt, off = pslot(tc)
                for hb in range(NHB):
                    mm = t.matmul(pt[:, off:off + 256],
                                  x_fmb[:, hb * TOWN + tc * 128:hb * TOWN + (tc + 1) * 128],
                                  wcat_sb[:, hb * 256:(hb + 1) * 256],
                                  start=(hb == 0), stop=(hb == NHB - 1))
            mm.then_inc(rq, 1)
            RN[0] += 1
        block.tensor(p1_tp)

        def f(v):
            for tc in range(NCH):
                pt, off = pslot(tc)
                last = v.tensor_copy(proj[:, tc * 256:(tc + 1) * 256], pt[:, off:off + 256])
            return last
        round_("vector", f)

        scope("P2_prep")
        def f(sc):
            for c in range(NCH):
                b = c * 256
                sc.activation(scr2[:, 2 * c * D:(2 * c + 1) * D], proj[:, b:b + D], AF.Square,
                              accum_out=scr[:, 512 + 2 * c:513 + 2 * c])
                last = sc.activation(scr2[:, (2 * c + 1) * D:(2 * c + 2) * D],
                                     proj[:, b + 128:b + 192], AF.Square,
                                     accum_out=scr[:, 513 + 2 * c:514 + 2 * c])
            return last
        round_("scalar", f)
        round_("scalar", lambda sc: sc.activation(scr[:, 528:528 + 2 * NCH],
                                                  scr[:, 512:512 + 2 * NCH], AF.Ln))
        round_("scalar", lambda sc: sc.activation(scr[:, 544:544 + 2 * NCH],
                                                  scr[:, 528:528 + 2 * NCH], AF.Exp, scale=-0.5))

        round_("vector", lambda v: v.tensor_scalar_min(scr[:, 560:560 + 2 * NCH],
                                                        scr[:, 544:544 + 2 * NCH], 1e12))

        def f(v):
            for c in range(NCH):
                b = c * 256
                v.tensor_scalar(scr2[:, c * 128:c * 128 + D], proj[:, b:b + D],
                                scr[:, 560 + 2 * c:561 + 2 * c], 0.0, ALU.mult, ALU.add)
                v.tensor_scalar(scr2[:, c * 128 + D:c * 128 + 2 * D], proj[:, b + 128:b + 192],
                                scr[:, 561 + 2 * c:562 + 2 * c], 0.0, ALU.mult, ALU.add)
                last = v.tensor_tensor(scr2[:, 1024 + c * D:1024 + (c + 1) * D],
                                       proj[:, b + 192:b + 256], bc64[:, D:2 * D], ALU.add)
            return last
        round_("vector", f)

        round_("scalar", lambda sc: sc.activation(scr[:, 0:NCH * D], scr2[:, 1024:1024 + NCH * D],
                                                  AF.Tanh, scale=0.5))

        def f(v):
            for c in range(NCH):
                last = v.tensor_scalar(scr[:, 512 + c * D:512 + (c + 1) * D],
                                       scr[:, c * D:(c + 1) * D], 0.5, 0.5, ALU.mult, ALU.add)
            return last
        round_("vector", f)

        def f(v):
            for c in range(NCH):
                last = v.scalar_tensor_tensor(stLA[:, c * D:(c + 1) * D],
                                              scr[:, 512 + c * D:512 + (c + 1) * D], 8.0,
                                              bc64[:, 0:D], ALU.mult, ALU.mult)
            return last
        round_("vector", f)

        tap("stLA", stLA[:])

        def f(t):
            for c in range(NCH):
                pt, off = pslot(c)
                last = t.matmul(pt[:, off:off + D], cf[:, TRI:TRI + 128],
                                stLA[:, c * D:(c + 1) * D], start=True, stop=True)
            return last
        round_("tensor", f)

        def f(sc):
            for c in range(NCH):
                pt, off = pslot(c)
                sc.activation(stG[:, c * D:(c + 1) * D], pt[:, off:off + D], AF.Exp)
                last = sc.activation(scr2[:, 1024 + c * D:1024 + (c + 1) * D],
                                     pt[:, off:off + D], AF.Exp, scale=-1.0)
            return last
        round_("scalar", f)

        def f(v):
            for c in range(NCH):
                pt, off = pslot(c)
                last = v.tensor_tensor(scr[:, c * D:(c + 1) * D], pt[:, off:off + D],
                                       stLA[:, c * D:(c + 1) * D], ALU.subtract)
            return last
        round_("vector", f)

        def f(sc):
            last = sc.activation(scr[:, 0:NCH * D], scr[:, 0:NCH * D], AF.Exp)
            return last
        round_("scalar", f)
        round_("scalar", lambda sc: sc.activation(scr2[:, 1536:1536 + NCH * D],
                                                  stLA[:, 0:NCH * D], AF.Exp))
        tap("gi", scr2[:, 1024:1536])
        tap("alpha", scr2[:, 1536:2048])
        tap("gprev", scr[:, 0:512])

        def f(v):
            kcol = lambda c: scr2[:, c * 128:c * 128 + D]
            qcol = lambda c: scr2[:, c * 128 + D:c * 128 + 2 * D]
            gp_ = lambda c: scr[:, c * D:(c + 1) * D]
            gi_ = lambda c: scr2[:, 1024 + c * D:1024 + (c + 1) * D]
            al_ = lambda c: scr2[:, 1536 + c * D:1536 + (c + 1) * D]
            for c in range(NCH):
                v.tensor_tensor(stKV[:, c * 128:c * 128 + D], kcol(c), gp_(c), ALU.mult)
                v.tensor_copy(stKV[:, c * 128 + D:c * 128 + 2 * D],
                              proj[:, c * 256 + D:c * 256 + 2 * D])
                v.tensor_tensor(stQt[:, c * D:(c + 1) * D], qcol(c), gp_(c), ALU.mult)
                v.tensor_tensor(stQh[:, c * D:(c + 1) * D], qcol(c),
                                stG[:, c * D:(c + 1) * D], ALU.mult)
            for c in range(NCH):
                last = v.tensor_scalar(scr[:, 512 + c * D:512 + (c + 1) * D], al_(c),
                                       -1.0, 1.0, ALU.mult, ALU.add)
            return last
        round_("vector", f)

        def f(v):
            kcol = lambda c: scr2[:, c * 128:c * 128 + D]
            for c in range(NCH):
                last = v.tensor_tensor(stUc[:, c * D:(c + 1) * D],
                                       scr[:, 512 + c * D:512 + (c + 1) * D], kcol(c), ALU.mult)
            return last
        round_("vector", f)

        def f(v):
            gi_ = lambda c: scr2[:, 1024 + c * D:1024 + (c + 1) * D]
            for c in range(NCH):
                last = v.tensor_tensor(stU0[:, c * D:(c + 1) * D],
                                       stUc[:, c * D:(c + 1) * D], gi_(c), ALU.mult)
            return last
        round_("vector", f)

        def f(t):
            for c in range(NCH):
                last = t.matmul(p4[0][0:1, c * D:(c + 1) * D], e127[:],
                                stG[:, c * D:(c + 1) * D], start=True, stop=True)
            return last
        round_("tensor", f)
        round_("vector", lambda v: v.tensor_copy(rowa[0:1, 0:NCH * D], p4[0][0:1, 0:NCH * D]))

        def f(t):
            for c in range(NCH):
                last = t.matmul(p4[1][0:64, c:c + 1], rowa[0:1, c * D:(c + 1) * D],
                                onesc[0:1, 0:1], start=True, stop=True)
            return last
        round_("tensor", f)
        round_("vector", lambda v: v.tensor_copy(glc[:], p4[1][0:64, 0:NCH]))

        def f(t):
            for c in range(NCH):
                pt, off = pslot(c)
                t.transpose(pt[0:64, off:off + 128], stQt[:, c * D:(c + 1) * D], cf[:, IDE:IDE + 128])
                t.transpose(pt[0:64, off + 128:off + 256], stKV[:, c * 128:c * 128 + D],
                            cf[:, IDE:IDE + 128])
                last = t.transpose(pt[0:64, off + 256:off + 384], stQh[:, c * D:(c + 1) * D],
                                   cf[:, IDE:IDE + 128])
            return last
        round_("tensor", f)

        def f(v):
            for c in range(NCH):
                pt, off = pslot(c)
                v.tensor_copy(fmQt[:, c * L:(c + 1) * L], pt[0:64, off:off + 128])
                v.tensor_copy(fmKt[:, c * L:(c + 1) * L], pt[0:64, off + 128:off + 256])
                last = v.tensor_copy(fmQh[:, c * L:(c + 1) * L], pt[0:64, off + 256:off + 384])
            return last
        round_("vector", f)

        tap("proj", proj[:])
        tap("stKV", stKV[:])
        tap("stQt", stQt[:])
        tap("stQh", stQh[:])
        tap("stU0", stU0[:])
        tap("stG", stG[:])
        tap("glc", glc[:], 64)
        tap("fmQt", fmQt[:], 64)

        scope("P3_scan")
        ch_base_col = 1
        for pas in range(2):
            if pas == 0:
                round_("vector", lambda v: v.tensor_copy(stUc[:], stU0[:]))
            else:
                def f(v):
                    for c in range(NCH):
                        last = v.tensor_scalar(stUc[:, c * D:(c + 1) * D],
                                               stU0[:, c * D:(c + 1) * D],
                                               stS[:, c:c + 1], 0.0, ALU.mult, ALU.add)
                    return last
                round_("vector", f)

            def f(t):
                for c in range(NCH):
                    pt, off = pslot(c)
                    last = t.transpose(pt[0:64, off:off + 128], stUc[:, c * D:(c + 1) * D],
                                       cf[:, IDE:IDE + 128])
                return last
            round_("tensor", f)

            def f(v):
                for c in range(NCH):
                    pt, off = pslot(c)
                    last = v.tensor_copy(fmU[:, c * L:(c + 1) * L], pt[0:64, off:off + 128])
                return last
            round_("vector", f)

            def f(t):
                for c in range(NCH):
                    pt, off = pslot(c)
                    t.matmul(pt[:, off:off + 128], fmKt[:, c * L:(c + 1) * L],
                             fmU[:, c * L:(c + 1) * L], start=True, stop=True)
                    last = t.matmul(pt[:, off + 128:off + 256], fmU[:, c * L:(c + 1) * L],
                                    fmKt[:, c * L:(c + 1) * L], start=True, stop=True)
                return last
            round_("tensor", f)

            def f(v):
                for c in range(NCH):
                    pt, off = pslot(c)
                    v.tensor_tensor(solW[:, c * L:(c + 1) * L], pt[:, off:off + 128],
                                    cf[:, SLN:SLN + 128], ALU.mult)
                    last = v.tensor_tensor(solU[:, c * L:(c + 1) * L], pt[:, off + 128:off + 256],
                                           cf[:, SUN:SUN + 128], ALU.mult)
                return last
            round_("vector", f)

            def f(t):
                for c in range(NCH):
                    pt, off = pslot(c)
                    rhs = fmQt if pas == 0 else fmQh
                    last = t.matmul(pt[:, off:off + 128], fmU[:, c * L:(c + 1) * L],
                                    rhs[:, c * L:(c + 1) * L], start=True, stop=True)
                return last
            round_("tensor", f)

            maskoff = SUU if pas == 0 else TRI
            def f(v):
                for c in range(NCH):
                    pt, off = pslot(c)
                    last = v.tensor_tensor(gP[:, c * L:(c + 1) * L], pt[:, off:off + 128],
                                           cf[:, maskoff:maskoff + 128], ALU.mult)
                return last
            round_("vector", f)

            round_("vector", lambda v: v.tensor_copy(solR[:], stKV[:]))

            for j in range(7):
                def f(t, j=j):
                    for c in range(NCH):
                        pt, off = pslot(c)
                        last = t.matmul(pt[:, off:off + 128], solU[:, c * L:(c + 1) * L],
                                        solR[:, c * 128:(c + 1) * 128], start=True, stop=True)
                        if j < 6:
                            t.matmul(pt[:, off + 128:off + 256], solU[:, c * L:(c + 1) * L],
                                     solW[:, c * L:(c + 1) * L], start=True, stop=True)
                            last = t.matmul(pt[:, off + 256:off + 384], solW[:, c * L:(c + 1) * L],
                                            solU[:, c * L:(c + 1) * L], start=True, stop=True)
                    return last
                round_("tensor", f)

                def f(v, j=j):
                    for c in range(NCH):
                        pt, off = pslot(c)
                        v.tensor_tensor(solR[:, c * 128:(c + 1) * 128], pt[:, off:off + 128],
                                        solR[:, c * 128:(c + 1) * 128], ALU.add)
                        if j < 6:
                            v.tensor_copy(solW[:, c * L:(c + 1) * L], pt[:, off + 128:off + 256])
                            last = v.tensor_copy(solU[:, c * L:(c + 1) * L], pt[:, off + 256:off + 384])
                        else:
                            last = v.engine_nop()
                    return last
                round_("vector", f)

            tap(f"solW0_p{pas}", solW[:])
            tap(f"solR_p{pas}", solR[:])

            def f(t):
                for c in range(NCH):
                    pt, off = pslot(c)
                    t.transpose(pt[0:64, off:off + 128], solR[:, c * 128:c * 128 + D],
                                cf[:, IDE:IDE + 128])
                    t.matmul(pt[0:64, off + 128:off + 192], solR[:, c * 128:c * 128 + D],
                             stUc[:, c * D:(c + 1) * D], start=True, stop=True)
                    last = t.matmul(pt[0:64, off + 192:off + 256], stUc[:, c * D:(c + 1) * D],
                                    solR[:, c * 128 + D:c * 128 + 2 * D], start=True, stop=True)
                return last
            round_("tensor", f)

            def f(v):
                for c in range(NCH):
                    pt, off = pslot(c)
                    v.tensor_scalar(fmZ[:, c * L:(c + 1) * L], pt[0:64, off:off + 128],
                                    -1.0, 0.0, ALU.mult, ALU.add)
                    v.tensor_tensor(abATs[:, c * D:(c + 1) * D], cf[0:64, IDE:IDE + D],
                                    pt[0:64, off + 128:off + 192], ALU.subtract)
                    last = v.tensor_scalar(abBgs[:, c * D:(c + 1) * D],
                                           pt[0:64, off + 192:off + 256],
                                           glc[:, c:c + 1], 0.0, ALU.mult, ALU.add)
                return last
            round_("vector", f)

            def p3_abdma(sync):
                sync.wait_ge(rq, RN[0])
                sync.dma_start(out=h0_dram[0:1, 0:3], in_=fcon[0:1, 0:3],
                               allow_slow_non_contiguous=True).then_inc(rq, 16)
                RN[0] += 16
                abv = ab_my.rearrange("(c r) n -> r c n", r=D)
                sync.dma_start(out=abv[:, :, 0:D],
                               in_=abATs.rearrange("r (c n) -> r c n", n=D),
                               allow_slow_non_contiguous=True).then_inc(rq, 16)
                sync.dma_start(out=abv[:, :, D:2 * D],
                               in_=abBgs.rearrange("r (c n) -> r c n", n=D),
                               allow_slow_non_contiguous=True).then_inc(rq, 16)
                sync.dma_start(out=abv[:, :, 2 * D:2 * D + 1],
                               in_=glc[:, 0:NCH].rearrange("r (c x) -> r c x", x=1),
                               allow_slow_non_contiguous=True).then_inc(rq, 16)
                sync.dma_start(out=abv[:, :, 2 * D + 1:2 * D + 2],
                               in_=glc[:, 0:NCH].rearrange("r (c x) -> r c x", x=1),
                               allow_slow_non_contiguous=True).then_inc(rq, 16)
                RN[0] += 16 * 4
            block.sync(p3_abdma)

            def p3_ag(gp):
                gp.wait_ge(rq, RN[0])
                gp.collective_compute(
                    "AllGather", ALU.bypass,
                    replica_groups=[list(range(NCORE))],
                    ins=[ab_my.opt()], outs=[ab_allp[pas].opt()],
                ).then_inc(rq, 1)
                RN[0] += 1
            block.gpsimd(p3_ag)

            h0ring = scr[0:64, 0:512]
            ab_ov = arena[0:64, MARK0:MARK0 + 50 * 130]
            ab_s2 = scr2[0:64, 0:14 * 130]

            def slotAB(gc):
                if gc < 50:
                    return ab_ov[:, gc * 130:(gc + 1) * 130]
                return ab_s2[:, (gc - 50) * 130:(gc - 49) * 130]

            round_("vector", lambda v: v.memset(h0ring[:, 0:D], 0.0))

            sx0 = NCH * 16 + pas * 160

            def p3_chain_dma(sync):
                sync.wait_ge(rq, RN[0])
                sync.dma_start(out=ab_ov.rearrange("r (c n) -> r c n", n=130),
                               in_=ab_allp[pas][0:50 * D, :].rearrange("(c r) n -> r c n", r=D),
                               allow_slow_non_contiguous=True).then_inc(sXin, 16)
                sync.dma_start(out=ab_s2.rearrange("r (c n) -> r c n", n=130),
                               in_=ab_allp[pas][50 * D:64 * D, :].rearrange("(c r) n -> r c n", r=D),
                               allow_slow_non_contiguous=True).then_inc(sXin, 16)
                for k in range(8):
                    sync.wait_ge(rq, RN[0] + 2 * (8 * k + 6) + 2)
                    sync.dma_start(out=h0_dram[k * 8 * D:(k + 1) * 8 * D, :]
                                   .rearrange("(g r) c -> r g c", r=D),
                                   in_=h0ring[:, 0:512].rearrange("r (g c) -> r g c", c=D),
                                   allow_slow_non_contiguous=True).then_inc(sXin, 16)
            block.sync(p3_chain_dma)

            def p3_chain_t(t):
                t.wait_ge(rq, RN[0])
                t.wait_ge(sXin, sx0 + 32)
                for gc in range(64):
                    bank = p4[gc % 2]
                    t.matmul(bank[0:64, 0:D], slotAB(gc)[:, 0:D],
                             h0ring[:, (gc % 8) * D:(gc % 8 + 1) * D],
                             start=True, stop=True).then_inc(rq, 1)
                    t.wait_ge(rq, RN[0] + (gc + 1) * 2)
                RN[0] += 64 * 2

            def p3_chain_v(v):
                base = RN[0] - 64 * 2
                for gc in range(64):
                    v.wait_ge(rq, base + gc * 2 + 1)
                    if gc % 8 == 7:
                        v.wait_ge(sXin, sx0 + 32 + ((gc // 8) + 1) * 16)
                    bank = p4[gc % 2]
                    v.scalar_tensor_tensor(h0ring[:, ((gc + 1) % 8) * D:((gc + 1) % 8 + 1) * D],
                                           bank[0:64, 0:D],
                                           slotAB(gc)[:, 2 * D:2 * D + 1],
                                           slotAB(gc)[:, D:2 * D],
                                           ALU.mult, ALU.add).then_inc(rq, 1)
            block.tensor(p3_chain_t)
            block.vector(p3_chain_v)

            def f(t):
                return t.matmul(p4[2][0:64, 0:1], cf[0:1, TRI:TRI + 64], cvt[0:1, 1:2],
                                start=True, stop=True)
            round_("tensor", f)

            def f(v):
                for c in range(NCH):
                    last = v.scalar_tensor_tensor(scr[0:64, c:c + 1], p4[2][0:64, 0:1], float(D),
                                                  iota64[:], ALU.mult, ALU.add)
                return last
            round_("vector", f)

            def f(v):
                for c in range(1, NCH):
                    v.tensor_scalar_add(scr[0:64, 8 + c:9 + c], scr[0:64, c:c + 1], float(c * D))
                return v.tensor_copy(scr[0:64, 8:9], scr[0:64, 0:1])
            round_("vector", f)
            round_("vector", lambda v: v.tensor_copy(idxg[:], scr[0:64, 8:8 + NCH]))

            def p3_h0g(gp):
                gp.wait_ge(rq, RN[0])
                gp.wait_ge(sXin, sx0 + 160)
                breg = gp.to_reg(64 * D - 1)
                for c in range(NCH):
                    gp.indirect_dma_start(
                        h0my[:, c * D:(c + 1) * D],
                        None, h0_dram[:],
                        IndirectOffsetOnAxis(idxg[:, c:c + 1], 0),
                        bounds_check=breg, oob_is_err=False,
                    ).then_inc(qG, 16)
                QG[0] += 16 * NCH
                gp.engine_nop().then_inc(rq, 1)
                RN[0] += 1
            block.gpsimd(p3_h0g)
            QGH0 = QG[0]

            def p3_h0w(t):
                t.wait_ge(qG, QGH0)
            block.tensor(p3_h0w)

            tap(f"h0my_p{pas}", h0my[:], 64)
            tap(f"abAT_p{pas}", abATs[:], 64)
            tap(f"abBg_p{pas}", abBgs[:], 64)

            def f(t):
                for c in range(NCH):
                    pt, off = pslot(c)
                    last = t.matmul(pt[:, off:off + D], fmZ[:, c * L:(c + 1) * L],
                                    h0my[:, c * D:(c + 1) * D], start=True, stop=True)
                return last
            round_("tensor", f)

            def f(v):
                for c in range(NCH):
                    pt, off = pslot(c)
                    last = v.tensor_tensor(stM[:, c * D:(c + 1) * D], pt[:, off:off + D],
                                           solR[:, c * 128 + D:c * 128 + 2 * D], ALU.add)
                return last
            round_("vector", f)

            def f(t):
                for c in range(NCH):
                    pt, off = pslot(c)
                    lhs = fmQt if pas == 0 else fmQh
                    t.matmul(pt[:, off:off + D], lhs[:, c * L:(c + 1) * L],
                             h0my[:, c * D:(c + 1) * D], start=True, stop=False)
                    last = t.matmul(pt[:, off:off + D], gP[:, c * L:(c + 1) * L],
                                    stM[:, c * D:(c + 1) * D], start=False, stop=True)
                return last
            round_("tensor", f)

            if pas == 0:
                def f(v):
                    for c in range(NCH):
                        pt, off = pslot(c)
                        last = v.tensor_tensor(scr[:, c * D:(c + 1) * D],
                                               stKV[:, c * 128 + D:c * 128 + 2 * D],
                                               pt[:, off:off + D], ALU.subtract)
                    return last
                round_("vector", f)

                def f(sc):
                    for c in range(NCH):
                        last = sc.activation(scr2[:, c * D:(c + 1) * D], scr[:, c * D:(c + 1) * D],
                                             AF.Square, accum_out=errs[:, c:c + 1])
                    return last
                round_("scalar", f)
                round_("scalar", lambda sc: sc.activation(scr[:, 0:NCH], errs[:, 0:NCH],
                                                          AF.Tanh, scale=SURSC / 2))
                round_("vector", lambda v: v.tensor_scalar(stS[:, 0:NCH], scr[:, 0:NCH],
                                                           0.5, 0.5, ALU.mult, ALU.add))
            else:
                def f(v):
                    for c in range(NCH):
                        pt, off = pslot(c)
                        last = v.tensor_copy(yattn[:, c * D:(c + 1) * D], pt[:, off:off + D])
                    return last
                round_("vector", f)

        tap("stM_p1", stM[:])
        tap("errs", errs[:])
        tap("stS", stS[:])
        tap("yattn", yattn[:])

        scope("P4_wo_ln")
        def p4_wo(sync):
            sync.wait_ge(rq, RN[0])
            sync.dma_start(out=scr2[0:64, 0:HD], in_=woT[:]).then_inc(rq, 16)
            RN[0] += 16
        block.sync(p4_wo)

        def f(t):
            return t.matmul(p4[0][0:64, 0:1], rowpar[0:1, 2 * D:3 * D], onesc[0:1, 0:1],
                            start=True, stop=True)
        round_("tensor", f)

        round_("vector", lambda v: v.tensor_copy(nwcol[:], p4[0][0:64, 0:1]))
        round_("vector", lambda v: v.tensor_scalar(wo_fm[:], scr2[0:64, 0:HD], nwcol[:],
                                                   0.0, ALU.mult, ALU.add))

        def f(sc):
            for c in range(NCH):
                last = sc.activation(scr2[:, c * D:(c + 1) * D],
                                     yattn[:, c * D:(c + 1) * D], AF.Square,
                                     accum_out=scr[:, 512 + c:513 + c])
            return last
        round_("scalar", f)
        round_("scalar", lambda sc: sc.activation(scr[:, 520:520 + NCH], scr[:, 512:512 + NCH],
                                                  AF.Ln, scale=1.0 / D, bias=RMS_EPS))
        round_("scalar", lambda sc: sc.activation(scr[:, 528:528 + NCH], scr[:, 520:520 + NCH],
                                                  AF.Exp, scale=-0.5))

        def f(v):
            for c in range(NCH):
                last = v.tensor_scalar(scr2[:, c * D:(c + 1) * D],
                                       yattn[:, c * D:(c + 1) * D],
                                       scr[:, 528 + c:529 + c], 0.0, ALU.mult, ALU.add)
            return last
        round_("vector", f)

        tap("ynm", scr2[:, 0:NCH * D])
        tap("rstdy", scr[:, 528:528 + NCH])
        tap("wofm", wo_fm[:], 64, BF16)

        def f(t):
            for c in range(NCH):
                pt, off = pslot(c)
                last = t.transpose(pt[0:64, off:off + 128],
                                   scr2[:, c * D:(c + 1) * D], cf[:, IDE:IDE + 128])
            return last
        round_("tensor", f)

        def f(v):
            for c in range(NCH):
                pt, off = pslot(c)
                last = v.tensor_copy(ynfm[:, c * 128:(c + 1) * 128],
                                     pt[0:64, off:off + 128])
            return last
        round_("vector", f)

        for ob2 in range(NHB // 2):
            def f(t, ob2=ob2):
                for k in range(2):
                    ob = ob2 * 2 + k
                    for half in range(2):
                        last = t.matmul(pw[k][:, half * 512:(half + 1) * 512],
                                        wo_fm[:, ob * 128:(ob + 1) * 128],
                                        ynfm[:, half * 512:(half + 1) * 512],
                                        start=True, stop=True)
                return last
            round_("tensor", f)

            def f(v, ob2=ob2):
                for k in range(2):
                    ob = ob2 * 2 + k
                    last = v.tensor_tensor(x_fm[:, ob * TOWN:(ob + 1) * TOWN],
                                           pw[k][:, 0:1024],
                                           x_fm[:, ob * TOWN:(ob + 1) * TOWN], ALU.add)
                return last
            round_("vector", f)

        tap("s1fm", x_fm[:, 0:2048])

        for tile in range(2):
            tsl = slice(tile * 512, (tile + 1) * 512)
            def f(t, tile=tile):
                for hb in range(NHB):
                    last = t.matmul(p4[0][0:1, 0:512],
                                    onesc[:], x_fm[:, hb * TOWN + tile * 512:hb * TOWN + (tile + 1) * 512],
                                    start=(hb == 0), stop=(hb == NHB - 1))
                return last
            round_("tensor", f)
            round_("vector", lambda v, tile=tile: v.tensor_copy(rowa[0:1, tile * 512:(tile + 1) * 512],
                                                                p4[0][0:1, 0:512]))
            for hb in range(NHB):
                round_("scalar", lambda sc, hb=hb, tile=tile: sc.activation(
                    scr2[:, 0:512],
                    x_fm[:, hb * TOWN + tile * 512:hb * TOWN + (tile + 1) * 512], AF.Square))
                round_("tensor", lambda t, hb=hb: t.matmul(
                    p4[1][0:1, 0:512], onesc[:], scr2[:, 0:512],
                    start=(hb == 0), stop=(hb == NHB - 1)))
            round_("vector", lambda v, tile=tile: v.tensor_copy(rowa[0:1, 1024 + tile * 512:1024 + (tile + 1) * 512],
                                                                p4[1][0:1, 0:512]))

        def f(v):
            v.tensor_scalar_mul(rowa[0:1, 0:1024], rowa[0:1, 0:1024], 1.0 / HD)
            return v.tensor_scalar_mul(rowa[0:1, 1024:2048], rowa[0:1, 1024:2048], 1.0 / HD)
        round_("vector", f)
        round_("vector", lambda v: v.tensor_tensor(scr[0:1, 0:1024], rowa[0:1, 0:1024],
                                                   rowa[0:1, 0:1024], ALU.mult))
        round_("vector", lambda v: v.tensor_tensor(rowa[0:1, 1024:2048], rowa[0:1, 1024:2048],
                                                   scr[0:1, 0:1024], ALU.subtract))
        round_("scalar", lambda sc: sc.activation(rowa[0:1, 1024:2048], rowa[0:1, 1024:2048],
                                                  AF.Ln, bias=LN_EPS))
        round_("scalar", lambda sc: sc.activation(rowa[0:1, 1024:2048], rowa[0:1, 1024:2048],
                                                  AF.Exp, scale=-0.5))
        def f(v):
            return v.tensor_tensor(rowa[0:1, 0:1024], rowa[0:1, 0:1024],
                                   rowa[0:1, 1024:2048], ALU.mult)
        round_("vector", f)

        def f(t):
            for half in range(2):
                t.matmul(pw[0][:, half * 512:(half + 1) * 512], cf[0:1, TRI:TRI + 128],
                         rowa[0:1, 1024 + half * 512:1024 + (half + 1) * 512],
                         start=True, stop=True)
                last = t.matmul(pw[1][:, half * 512:(half + 1) * 512], cf[0:1, TRI:TRI + 128],
                                rowa[0:1, half * 512:(half + 1) * 512],
                                start=True, stop=True)
            return last
        round_("tensor", f)

        def f(v):
            for hb in range(NHB):
                last = v.tensor_tensor(x_fm[:, hb * TOWN:(hb + 1) * TOWN],
                                       x_fm[:, hb * TOWN:(hb + 1) * TOWN], pw[0][:, 0:1024],
                                       ALU.mult)
            return last
        round_("vector", f)

        def f(v):
            for hb in range(NHB):
                last = v.tensor_tensor(x_fm[:, hb * TOWN:(hb + 1) * TOWN],
                                       x_fm[:, hb * TOWN:(hb + 1) * TOWN], pw[1][:, 0:1024],
                                       ALU.subtract)
            return last
        round_("vector", f)

        def f(v):
            for hb in range(NHB):
                last = v.tensor_copy(h_fmb[:, hb * TOWN:(hb + 1) * TOWN],
                                     x_fm[:, hb * TOWN:(hb + 1) * TOWN])
            return last
        round_("vector", f)

        tap("rowa", rowa[:], 1)

        def emit_h_tm():
            RQ_EH = RN[0]

            def pslot_h(tc, g):
                if tc % 2 == 0:
                    return p4[g][:, 0:512]
                return pw[g // 2][:, (g % 2) * 512:(g % 2 + 1) * 512]

            for tc in range(NCH):
                def ft(t, tc=tc):
                    if tc <= 1:
                        t.wait_ge(rq, RQ_EH)
                    if tc >= 2:
                        t.wait_ge(sVx, 32 + tc - 1)
                    for hb in range(NHB):
                        g = hb // 4
                        mm = t.transpose(pslot_h(tc, g)[:, (hb % 4) * 128:(hb % 4) * 128 + 128],
                                         x_fm[:, hb * TOWN + tc * 128:hb * TOWN + (tc + 1) * 128],
                                         cf[:, IDE:IDE + 128])
                    mm.then_inc(sTx, 1)
                block.tensor(ft)

                rqd = RN[0]
                def fv(v, tc=tc, rqd=rqd):
                    v.wait_ge(sTx, 32 + tc + 1)
                    v.wait_ge(rq, rqd)
                    for g in range(4):
                        v.tensor_copy(hstg_bf[:, g * 512:(g + 1) * 512], pslot_h(tc, g))
                    for g in range(4):
                        v.tensor_copy(scr2[:, g * 512:(g + 1) * 512], pslot_h(tc, g))
                    for _ in range(6):
                        v.engine_nop()
                    v.engine_nop().then_inc(sVx, 1)
                block.vector(fv)

                def fd(sync, tc=tc):
                    sync.wait_ge(sVx, 32 + tc + 1)
                    sync.dma_start(out=htm_bf_my[tc * 128:(tc + 1) * 128, :],
                                   in_=hstg_bf[:, 0:HD]).then_inc(rq, 16)
                    sync.dma_start(out=htm_f32[tc * 128:(tc + 1) * 128, :],
                                   in_=scr2[:, 0:HD]).then_inc(rq, 16)
                    RN[0] += 32
                block.sync(fd)
        emit_h_tm()

        def p4_hag(gp):
            gp.wait_ge(rq, RN[0])
            gp.collective_compute("AllGather", ALU.bypass,
                                  replica_groups=[list(range(NCORE))],
                                  ins=[htm_bf_my.opt()], outs=[h_full.opt()]).then_inc(sCC, 1)
        block.gpsimd(p4_hag)

        scope("P5_router")
        for tile in range(2):
            def f(t, tile=tile):
                t.wait_ge(qG, 32)
                if tile == 0:
                    t.wait_ge(sVx, 40)
                for hb in range(NHB):
                    last = t.matmul(p4[tile % 2][0:16, 0:512],
                                    gw_sb[:, hb * EXP:(hb + 1) * EXP],
                                    x_fm[:, hb * TOWN + tile * 512:hb * TOWN + (tile + 1) * 512],
                                    start=(hb == 0), stop=(hb == NHB - 1))
                return last
            round_("tensor", f)
            round_("vector", lambda v, tile=tile: v.tensor_copy(
                lgfm[:, tile * 512:(tile + 1) * 512], p4[tile % 2][0:16, 0:512]))

        arena_reset()
        rscr = aa(1040)
        rscr2 = aa(1032)
        lgtm = aa(8 * 16)
        vals = aa(32)
        e4t = aa(32)
        s4t = aa(8)
        lns4 = aa(8)
        negm4 = aa(8)
        negb4 = aa(8)
        idx4 = aa(32)
        w4 = aa(32)
        excl = aa(4 * 8 * 16)
        maskg = aa(8 * 16)
        rankg = aa(8 * 16)
        slot4 = aa(32)
        slotc4 = aa(32)
        destf = aa(32)
        gidxf = aa(32)
        tokid0 = aa(1)
        tokidc = aa(8)
        pairsA = aa(64)
        cntrow = aa(8 * 16)[0:1, :]
        carrow = aa(8 * 16)[0:1, :]
        prefr = aa(16)[0:1, :]
        cnasb = aa(16)[0:8, :]
        gbsb = aa(2052)[0:16, :]
        myg = aa(2052)[0:2, :]
        idcols = aa(16)
        wcols = aa(16)
        workt = rscr[:, 384:640]
        hidxf = aa(16)
        FFN_MARK = _aoff[0]

        rB = es.enter_context(nc.semaphore("rB"))
        RB = [0]
        RQ_P5A = RN[0]
        BQ = []
        BFIRST = [True]

        def roundB(engname, fn, inc=1):
            def thunk(engname=engname, fn=fn, inc=inc):
                first = BFIRST[0]
                BFIRST[0] = False

                def body(eng):
                    if first:
                        eng.wait_ge(rq, RQ_P5A)
                    eng.wait_ge(rB, RB[0])
                    last = fn(eng)
                    last.then_inc(rB, inc)
                    if engname == "vector":
                        for _ in range(6):
                            eng.engine_nop()
                    elif engname == "scalar":
                        for _ in range(6):
                            eng.nop()
                getattr(block, engname)(body)
                RB[0] += inc
            BQ.append(thunk)

        def rawB(engname, emitfn):
            def thunk(engname=engname, emitfn=emitfn):
                getattr(block, engname)(emitfn)
            BQ.append(thunk)

        def emitB(n):
            for _ in range(min(n, len(BQ))):
                BQ.pop(0)()

        def bslot(i):
            return p4[i % 4], (i // 4) * 16

        def f(t):
            for i in range(8):
                pt, off = bslot(i)
                last = t.transpose(pt[:, off:off + 16],
                                   lgfm[:, i * 128:(i + 1) * 128], cf[0:16, IDE:IDE + 16])
            return last
        roundB("tensor", f)

        def f(v):
            for i in range(8):
                pt, off = bslot(i)
                last = v.tensor_copy(lgtm[:, i * 16:(i + 1) * 16], pt[:, off:off + 16])
            return last
        roundB("vector", f)

        roundB("vector", lambda v: v.tensor_scalar_add(rscr[:, 960:976], iota16[:], BIGF))
        roundB("vector", lambda v: v.tensor_copy(workt[:, 0:128], lgtm[:]))
        roundB("vector", lambda v: v.memset(maskg[:], 0.0))

        for k in range(4):
            wsrc = workt[:, (k % 2) * 128:(k % 2) * 128 + 128]
            wdst = workt[:, ((k + 1) % 2) * 128:((k + 1) % 2) * 128 + 128]

            def mk(fn):
                def f(v, k=k, wsrc=wsrc, wdst=wdst, fn=fn):
                    for i in range(8):
                        last = fn(v, i, k, wsrc, wdst)
                    return last
                roundB("vector", f)
            mk(lambda v, i, k, ws, wd: v.tensor_reduce(vals[:, i * 4 + k:i * 4 + k + 1],
                                                       ws[:, i * 16:(i + 1) * 16], AX.X, ALU.max))
            mk(lambda v, i, k, ws, wd: v.tensor_scalar(rscr[:, i * 16:(i + 1) * 16],
                                                       ws[:, i * 16:(i + 1) * 16],
                                                       vals[:, i * 4 + k:i * 4 + k + 1], 0.0,
                                                       ALU.is_equal, ALU.add))
            mk(lambda v, i, k, ws, wd: v.scalar_tensor_tensor(
                rscr[:, 128 + i * 16:128 + (i + 1) * 16], rscr[:, i * 16:(i + 1) * 16], -BIGF,
                rscr[:, 960:976], ALU.mult, ALU.add))
            mk(lambda v, i, k, ws, wd: v.tensor_reduce(idx4[:, i * 4 + k:i * 4 + k + 1],
                                                       rscr[:, 128 + i * 16:128 + (i + 1) * 16],
                                                       AX.X, ALU.min))
            mk(lambda v, i, k, ws, wd: v.tensor_scalar(
                excl[:, (k * 8 + i) * 16:(k * 8 + i + 1) * 16], iota16[:],
                idx4[:, i * 4 + k:i * 4 + k + 1], 0.0, ALU.is_equal, ALU.add))
            mk(lambda v, i, k, ws, wd: v.scalar_tensor_tensor(
                wd[:, i * 16:(i + 1) * 16], excl[:, (k * 8 + i) * 16:(k * 8 + i + 1) * 16],
                -BIGF, ws[:, i * 16:(i + 1) * 16], ALU.mult, ALU.add))
            mk(lambda v, i, k, ws, wd: v.tensor_tensor(
                maskg[:, i * 16:(i + 1) * 16], maskg[:, i * 16:(i + 1) * 16],
                excl[:, (k * 8 + i) * 16:(k * 8 + i + 1) * 16], ALU.add))

        def f(v):
            for i in range(8):
                last = v.tensor_reduce(rscr[:, i:i + 1], vals[:, i * 4:(i + 1) * 4], AX.X, ALU.max)
            return last
        roundB("vector", f)

        def f(v):
            for i in range(8):
                last = v.tensor_scalar(negm4[:, i:i + 1], rscr[:, i:i + 1], -1.0, 0.0,
                                       ALU.mult, ALU.add)
            return last
        roundB("vector", f)

        def f(sc):
            for i in range(8):
                last = sc.activation(e4t[:, i * 4:(i + 1) * 4], vals[:, i * 4:(i + 1) * 4],
                                     AF.Exp, bias=negm4[:, i:i + 1])
            return last
        roundB("scalar", f)

        def f(v):
            for i in range(8):
                last = v.tensor_reduce(s4t[:, i:i + 1], e4t[:, i * 4:(i + 1) * 4], AX.X, ALU.add)
            return last
        roundB("vector", f)
        roundB("scalar", lambda sc: sc.activation(lns4[:], s4t[:], AF.Ln))

        def f(v):
            for i in range(8):
                last = v.tensor_tensor(rscr[:, i:i + 1], negm4[:, i:i + 1], lns4[:, i:i + 1],
                                       ALU.subtract)
            return last
        roundB("vector", f)
        roundB("vector", lambda v: v.tensor_copy(negb4[:], rscr[:, 0:8]))

        def f(sc):
            for i in range(8):
                last = sc.activation(w4[:, i * 4:(i + 1) * 4], vals[:, i * 4:(i + 1) * 4],
                                     AF.Exp, bias=negb4[:, i:i + 1])
            return last
        roundB("scalar", f)

        def f(t):
            for i in range(8):
                last = t.matmul(p4[0][0:1, i * 16:(i + 1) * 16], onesc[:],
                                maskg[:, i * 16:(i + 1) * 16], start=True, stop=True)
            return last
        roundB("tensor", f)
        roundB("vector", lambda v: v.tensor_copy(cntrow[:], p4[0][0:1, 0:128]))

        def f(v):
            v.tensor_tensor(rscr[0:1, 0:16], cntrow[0:1, 0:16], cntrow[0:1, 16:32], ALU.add)
            v.tensor_tensor(rscr[0:1, 16:32], cntrow[0:1, 32:48], cntrow[0:1, 48:64], ALU.add)
            v.tensor_tensor(rscr[0:1, 32:48], cntrow[0:1, 64:80], cntrow[0:1, 80:96], ALU.add)
            return v.tensor_tensor(rscr[0:1, 48:64], cntrow[0:1, 96:112], cntrow[0:1, 112:128],
                                   ALU.add)
        roundB("vector", f)

        def f(v):
            v.tensor_tensor(rscr[0:1, 64:80], rscr[0:1, 0:16], rscr[0:1, 16:32], ALU.add)
            return v.tensor_tensor(rscr[0:1, 80:96], rscr[0:1, 32:48], rscr[0:1, 48:64], ALU.add)
        roundB("vector", f)
        roundB("vector", lambda v: v.tensor_tensor(rscr[0:1, 96:112], rscr[0:1, 64:80],
                                                   rscr[0:1, 80:96], ALU.add))

        def fd(sync):
            sync.wait_ge(rB, RB[0])
            sync.dma_start(out=h0_dram[1:2, 0:3], in_=fcon[0:1, 0:3],
                           allow_slow_non_contiguous=True).then_inc(rB, 16)
            sync.dma_start(out=cnt_my[:], in_=rscr[0:1, 96:112]).then_inc(rB, 16)
            RB[0] += 32
        rawB("sync", fd)

        def fg(gp):
            gp.wait_ge(rB, RB[0])
            gp.collective_compute("AllGather", ALU.bypass,
                                  replica_groups=[list(range(NCORE))],
                                  ins=[cnt_my.opt()], outs=[cnt_all.opt()]).then_inc(rB, 1)
            RB[0] += 1
        rawB("gpsimd", fg)

        def fd(sync):
            sync.wait_ge(rB, RB[0])
            sync.dma_start(out=cnasb[:], in_=cnt_all[:]).then_inc(rB, 16)
            RB[0] += 16
        rawB("sync", fd)

        def f(t):
            return t.matmul(p4[1][0:1, 0:16], prem[:], cnasb[:], start=True, stop=True)
        roundB("tensor", f)
        roundB("vector", lambda v: v.tensor_copy(prefr[:], p4[1][0:1, 0:16]))

        roundB("vector", lambda v: v.tensor_copy(carrow[0:1, 0:16], prefr[:]))
        for i in range(1, 8):
            roundB("vector", lambda v, i=i: v.tensor_tensor(
                carrow[0:1, i * 16:(i + 1) * 16], carrow[0:1, (i - 1) * 16:i * 16],
                cntrow[0:1, (i - 1) * 16:i * 16], ALU.add))

        def f(t):
            for i in range(8):
                pt, off = bslot(i)
                t.matmul(pt[:, off:off + 16], cf[:, SUU:SUU + 128],
                         maskg[:, i * 16:(i + 1) * 16], start=True, stop=False)
                last = t.matmul(pt[:, off:off + 16], cf[0:1, TRI:TRI + 128],
                                carrow[0:1, i * 16:(i + 1) * 16], start=False, stop=True)
            return last
        roundB("tensor", f)

        def f(v):
            for i in range(8):
                pt, off = bslot(i)
                last = v.tensor_copy(rankg[:, i * 16:(i + 1) * 16], pt[:, off:off + 16])
            return last
        roundB("vector", f)

        def f(v):
            for i in range(8):
                for k in range(4):
                    last = v.tensor_tensor(rscr[:, (i * 4 + k) * 16:(i * 4 + k + 1) * 16],
                                           rankg[:, i * 16:(i + 1) * 16],
                                           excl[:, (k * 8 + i) * 16:(k * 8 + i + 1) * 16],
                                           ALU.mult)
            return last
        roundB("vector", f)

        def f(v):
            for j in range(32):
                last = v.tensor_reduce(slot4[:, j:j + 1], rscr[:, j * 16:(j + 1) * 16],
                                       AX.X, ALU.add)
            return last
        roundB("vector", f)

        def f(v):
            for j in range(32):
                last = v.tensor_scalar_min(slotc4[:, j:j + 1], slot4[:, j:j + 1], float(CAP))
            return last
        roundB("vector", f)

        def f(v):
            for j in range(32):
                v.scalar_tensor_tensor(destf[:, j:j + 1], idx4[:, j:j + 1], 1025.0,
                                       slotc4[:, j:j + 1], ALU.mult, ALU.add)
            for j in range(32):
                v.tensor_scalar(rscr[:, j:j + 1], slot4[:, j:j + 1], float(CAP), BIGF,
                                ALU.is_ge, ALU.mult)
            for j in range(32):
                last = v.scalar_tensor_tensor(rscr[:, 32 + j:33 + j], idx4[:, j:j + 1], 1024.0,
                                              slot4[:, j:j + 1], ALU.mult, ALU.add)
            return last
        roundB("vector", f)

        def f(v):
            for j in range(32):
                last = v.tensor_tensor(gidxf[:, j:j + 1], rscr[:, 32 + j:33 + j],
                                       rscr[:, j:j + 1], ALU.add)
            return last
        roundB("vector", f)

        def f(t):
            return t.matmul(p4[2][:, 0:1], cf[0:1, TRI:TRI + 128], cvt[0:1, 0:1],
                            start=True, stop=True)
        roundB("tensor", f)

        roundB("vector", lambda v: v.scalar_tensor_tensor(tokid0[:], p4[2][:, 0:1], 1.0,
                                                          iotac[:], ALU.mult, ALU.add))

        def f(v):
            for i in range(8):
                last = v.tensor_scalar_add(tokidc[:, i:i + 1], tokid0[:], float(i * 128 + 1))
            return last
        roundB("vector", f)

        def f(v):
            for i in range(8):
                for k in range(4):
                    j = i * 4 + k
                    v.tensor_copy(pairsA[:, 2 * j:2 * j + 1], tokidc[:, i:i + 1])
                    last = v.tensor_copy(pairsA[:, 2 * j + 1:2 * j + 2], w4[:, j:j + 1])
            return last
        roundB("vector", f)

        def f(v):
            last = v.tensor_copy(desti32[:], destf[:])
            return last
        roundB("vector", f)
        roundB("vector", lambda v: v.tensor_copy(gidx32[:], gidxf[:]))

        roundB("vector", lambda v: v.memset(rscr2[:, 0:258], 0.0))

        def fd(sync):
            sync.wait_ge(rB, RB[0])
            sync.dma_start(out=h0_dram[1:2, 0:3], in_=fcon[0:1, 0:3],
                           allow_slow_non_contiguous=True).then_inc(rB, 16)
            sync.dma_start(out=gbuf_my[0:16384, :].rearrange("(p n) c -> p (n c)", p=128),
                           in_=rscr2[:, 0:256]).then_inc(rB, 16)
            sync.dma_start(out=gbuf_my[16384:16400, :], in_=rscr2[0:16, 256:258]).then_inc(rB, 16)
            RB[0] += 48
        rawB("sync", fd)

        def fg(gp):
            gp.wait_ge(rB, RB[0])
            qbase = QG[0]
            for j in range(32):
                if j:
                    gp.wait_ge(qG, qbase + j * 16)
                gp.indirect_dma_start(
                    gbuf_my[:], IndirectOffsetOnAxis(desti32[:, j:j + 1], 0),
                    pairsA[:, 2 * j:2 * j + 2], None,
                ).then_inc(qG, 16)
            QG[0] += 32 * 16
            gp.engine_nop().then_inc(rB, 1)
            RB[0] += 1
        rawB("gpsimd", fg)

        QG_SCAT = [0]

        def fg(gp):
            gp.wait_ge(rB, RB[0])
            gp.wait_ge(qG, QG_SCAT[0])
            gp.collective_compute("AllReduce", ALU.add,
                                  replica_groups=[list(range(NCORE))],
                                  ins=[gbuf_my.opt()], outs=[gbuf_all.opt()]).then_inc(rB, 1)
            RB[0] += 1
        def thunk_ar(fg=fg):
            QG_SCAT[0] = QG[0]
            block.gpsimd(fg)
        BQ.append(thunk_ar)

        def fd(sync):
            sync.wait_ge(rB, RB[0])
            sync.dma_start(out=gbsb[:, 0:2050],
                           in_=gbuf_all.rearrange("(e s) c -> e (s c)", e=16)).then_inc(rB, 16)
            RB[0] += 16
        rawB("sync", fd)

        def f(t):
            for j in range(4):
                last = t.matmul(p4[j][0:2, 0:512], eselt[:], gbsb[:, j * 512:(j + 1) * 512],
                                start=True, stop=True)
            return last
        roundB("tensor", f)

        def f(v):
            for j in range(4):
                last = v.tensor_copy(myg[:, j * 512:(j + 1) * 512], p4[j][0:2, 0:512])
            return last
        roundB("vector", f)

        roundB("tensor", lambda t: t.matmul(p4[0][0:2, 0:2], eselt[:], gbsb[:, 2048:2050],
                                            start=True, stop=True))
        roundB("vector", lambda v: v.tensor_copy(myg[:, 2048:2050], p4[0][0:2, 0:2]))

        def f(v):
            mygv = myg[:, 0:2050].rearrange("p (s c) -> p s c", c=2)
            v.tensor_copy(rscr2[0:2, 0:1025], mygv[:, :, 0])
            last = v.tensor_copy(gbsb[0:2, 0:1025], mygv[:, :, 1])
            return last
        roundB("vector", f)

        def f(t):
            for si in range(8):
                pt, off = p4[si % 4], (si // 4) * 4
                t.transpose(pt[:, off:off + 2], rscr2[0:2, si * 128:(si + 1) * 128],
                            cf[0:2, IDE:IDE + 2])
                last = t.transpose(pt[:, off + 2:off + 4],
                                   gbsb[0:2, si * 128:(si + 1) * 128],
                                   cf[0:2, IDE:IDE + 2])
            return last
        roundB("tensor", f)

        def f(v):
            for si in range(8):
                pt, off = p4[si % 4], (si // 4) * 4
                v.tensor_copy(idcols[:, si * 2:si * 2 + 2], pt[:, off:off + 2])
                last = v.tensor_copy(wcols[:, si * 2:si * 2 + 2], pt[:, off + 2:off + 4])
            return last
        roundB("vector", f)

        def f(v):
            for j in range(16):
                last = v.tensor_scalar(rscr[:, j:j + 1], idcols[:, j:j + 1], 0.0, BIGF,
                                       ALU.is_equal, ALU.mult)
            return last
        roundB("vector", f)

        def f(v):
            for j in range(16):
                last = v.tensor_tensor(rscr[:, 16 + j:17 + j], rscr[:, j:j + 1],
                                       idcols[:, j:j + 1], ALU.add)
            return last
        roundB("vector", f)

        def f(v):
            for j in range(16):
                last = v.tensor_scalar_add(hidxf[:, j:j + 1], rscr[:, 16 + j:17 + j], -1.0)
            return last
        roundB("vector", f)
        roundB("vector", lambda v: v.tensor_copy(hidx32[:], hidxf[:]))

        NB_TOTAL = len(BQ)

        scope("P6_shared")
        swgb = [aab(16 * 128) for _ in range(2)]
        swub = [aab(16 * 128) for _ in range(2)]
        smid = aab(NSB * 1024)
        swdb = [aab(NSB * 512) for _ in range(2)]
        P6_END = _aoff[0]

        FF_CNT = [0]
        FT_CNT = [0]

        def p6_dma(isb, par):
            sem = sFF if par == 0 else sFT
            mmw = [0]
            def fd(gp, isb=isb, par=par, sem=sem):
                gp.wait_ge(rq, RN[0])
                if mmw[0]:
                    gp.wait_ge(sMM, mmw[0])
                gp.dma_start(out=swgb[par].rearrange("p (c n) -> p c n", n=128),
                             in_=swg.rearrange("(c p) n -> p c n", p=128)[:, :, isb * 128:(isb + 1) * 128]
                             ).then_inc(sem, 16)
                gp.dma_start(out=swub[par].rearrange("p (c n) -> p c n", n=128),
                             in_=swu.rearrange("(c p) n -> p c n", p=128)[:, :, isb * 128:(isb + 1) * 128]
                             ).then_inc(sem, 16)
                gp.engine_nop().then_inc(rq, 1)
                RN[0] += 1
            mmw[0] = MMC[0]
            block.gpsimd(fd)
            (FF_CNT if par == 0 else FT_CNT)[0] += 32

        sMM = sFV
        sSI = sFS
        sMD = sFQ
        MMC, SIC, MDC = [0], [0], [0]
        MD_OF_SILU = []
        RQ_H6 = RN[0]

        def p6_mmh(isb, half):
            par = isb % 2
            sem = sFF if par == 0 else sFT
            wv = (FF_CNT if par == 0 else FT_CNT)[0]
            r = MMC[0]
            def f(t, isb=isb, half=half, par=par, sem=sem, wv=wv, r=r):
                if r == 0:
                    t.wait_ge(rq, RQ_H6)
                t.wait_ge(sem, wv)
                if r >= 2:
                    t.wait_ge(sMD, r - 1)
                for hb in range(NHB):
                    t.matmul(pw[0][:, (r % 2) * 512:(r % 2 + 1) * 512],
                             swgb[par][:, hb * 128:(hb + 1) * 128],
                             h_fmb[:, hb * TOWN + half * 512:hb * TOWN + (half + 1) * 512],
                             start=(hb == 0), stop=(hb == NHB - 1))
                    mmu = t.matmul(pw[1][:, (r % 2) * 512:(r % 2 + 1) * 512],
                                   swub[par][:, hb * 128:(hb + 1) * 128],
                                   h_fmb[:, hb * TOWN + half * 512:hb * TOWN + (half + 1) * 512],
                                   start=(hb == 0), stop=(hb == NHB - 1))
                mmu.then_inc(sMM, 1)
            block.tensor(f)
            MMC[0] += 1
            return r

        def gu_epih(rmm, dst_mid, extra_rq=None):
            rs = SIC[0]
            war = MD_OF_SILU[rs - 2] + 1 if rs >= 2 else 0
            hh = rmm % 2
            def fs(sc, rmm=rmm, rs=rs, war=war, hh=hh):
                if extra_rq is not None:
                    sc.wait_ge(rq, extra_rq)
                sc.wait_ge(sMM, rmm + 1)
                if war:
                    sc.wait_ge(sMD, war)
                sc.activation(scr2[:, hh * 512:(hh + 1) * 512],
                              pw[0][:, hh * 512:(hh + 1) * 512],
                              AF.Silu).then_inc(sSI, 1)
                for _ in range(6):
                    sc.nop()
            block.scalar(fs)
            SIC[0] += 1

            def fv(v, rs=rs, hh=hh, dst_mid=dst_mid):
                v.wait_ge(sSI, rs + 1)
                v.tensor_tensor(dst_mid, scr2[:, hh * 512:(hh + 1) * 512],
                                pw[1][:, hh * 512:(hh + 1) * 512],
                                ALU.mult).then_inc(sMD, 1)
                for _ in range(6):
                    v.engine_nop()
            block.vector(fv)
            MD_OF_SILU.append(MDC[0])
            MDC[0] += 1

        def p6_epih(isb, half):
            gu_epih(2 * isb + half,
                    smid[:, (2 * isb + half) * 512:(2 * isb + half + 1) * 512])

        B_SCHED = [7, 7, 7, 7, 8, 5, 3, 2, 8, 2, 11, 3, 1] + [0] * (NSB - 13)

        p6_dma(0, 0)
        p6_dma(1, 1)
        emitB(5)
        for isb in range(NSB):
            p6_mmh(isb, 0)
            p6_mmh(isb, 1)
            if isb + 2 < NSB:
                p6_dma(isb + 2, isb % 2)
            p6_epih(isb, 0)
            p6_epih(isb, 1)
            emitB(B_SCHED[isb])

        if debug:
            def fjs(v):
                v.wait_ge(sMD, MDC[0])
                return v.engine_nop()
            round_("vector", fjs)
        tap("smid", smid[:], 128, BF16)

        def p6d_dma(hb2, par, first=False):
            sem = sFF if par == 0 else sFT
            mmw = 0 if first else MMC[0]
            def fd(gp, hb2=hb2, sem=sem, mmw=mmw):
                gp.wait_ge(rq, RN[0])
                if mmw:
                    gp.wait_ge(sMM, mmw)
                for isb in range(NSB):
                    gp.dma_start(out=swdb[par][:, isb * 512:(isb + 1) * 512],
                                 in_=swd[isb * 128:(isb + 1) * 128, hb2 * 512:(hb2 + 1) * 512]
                                 ).then_inc(sem, 16)
                gp.engine_nop().then_inc(rq, 1)
                RN[0] += 1
            block.gpsimd(fd)
            (FF_CNT if par == 0 else FT_CNT)[0] += NSB * 16

        def down_rounds(src_mid, src_wd, nblk, sem, wv, smid_rdy, store_to, wcol=None):
            r = MMC[0]
            def f(t, r=r, sem=sem, wv=wv, smid_rdy=smid_rdy):
                t.wait_ge(sem, wv)
                t.wait_ge(sMD, max(smid_rdy, r - 1))
                pt = pw[r % 2]
                for ib in range(nblk):
                    mmv = t.matmul(pt[:, 0:512], src_mid(ib), src_wd(ib),
                                   start=(ib == 0), stop=(ib == nblk - 1))
                mmv.then_inc(sMM, 1)
            block.tensor(f)
            MMC[0] += 1

            r2 = MDC[0]
            rqe = RN[0]
            def fv(v, r2=r2, rqe=rqe):
                v.wait_ge(sMM, r2 + 1)
                v.wait_ge(rq, rqe)
                if wcol is None:
                    out = scr2.bitcast(BF16)[:, (r2 % 2) * 512:(r2 % 2 + 1) * 512]
                    v.tensor_copy(out, pw[r2 % 2][:, 0:512]).then_inc(sMD, 1)
                else:
                    out = eoutst[:, (r2 % 2) * 512:(r2 % 2 + 1) * 512]
                    v.tensor_scalar(out, pw[r2 % 2][:, 0:512], wcol, 0.0,
                                    ALU.mult, ALU.add).then_inc(sMD, 1)
                for _ in range(6):
                    v.engine_nop()
            block.vector(fv)
            MDC[0] += 1

            def fd2(sync, r2=r2):
                sync.wait_ge(sMD, r2 + 1)
                if wcol is None:
                    src = scr2.bitcast(BF16)[:, (r2 % 2) * 512:(r2 % 2 + 1) * 512]
                else:
                    src = eoutst[:, (r2 % 2) * 512:(r2 % 2 + 1) * 512]
                sync.dma_start(out=store_to, in_=src).then_inc(rq, 16)
                RN[0] += 16
            block.sync(fd2)

        p6d_dma(0, 0, first=True)
        p6d_dma(1, 1, first=True)
        SMID_RDY = MDC[0]
        for hb2 in range(4):
            par = hb2 % 2
            sem = sFF if par == 0 else sFT
            wv = (FF_CNT if par == 0 else FT_CNT)[0]
            for tb in range(8):
                down_rounds(
                    (lambda ib, tb=tb: smid[:, ib * 1024 + tb * 128:ib * 1024 + (tb + 1) * 128]),
                    (lambda ib, par=par: swdb[par][:, ib * 512:(ib + 1) * 512]),
                    NSB, sem, wv, SMID_RDY,
                    ysh_dram[tb * 128:(tb + 1) * 128, hb2 * 512:(hb2 + 1) * 512])
            if hb2 + 2 < 4:
                p6d_dma(hb2 + 2, par)
            if hb2 >= 2:
                emitB(6)
        emitB(len(BQ))
        RBF = RB[0]

        if debug:
            def fjoin(sync):
                sync.wait_ge(rB, RBF)
            block.sync(fjoin)
            tap("lgfm", lgfm[:], 16)
            tap("h_fmb", h_fmb[:], 128, BF16)
            tap("lgtm", lgtm[:])
            tap("vals", vals[:])
            tap("idx4", idx4[:])
            tap("w4", w4[:])
            tap("maskg", maskg[:])
            tap("cntrow", cntrow[:], 1)
            tap("carrow", carrow[:], 1)
            tap("rankg", rankg[:])
            tap("slot4", slot4[:])
            tap("destf", destf[:])
            tap("gidxf", gidxf[:])
            tap("gbsb", gbsb[:], 16)
            tap("myg", myg[:], 2)
            tap("idcols", idcols[:])
            tap("wcols", wcols[:])
            tap("hidxf", hidxf[:])
            tap("pairsA", pairsA[:])

        scope("P7_moe")
        _aoff[0] = FFN_MARK
        xg_fm = aab(NHB * 1024)
        xg_tm = aab(2 * HD)
        wgb = [aab(16 * 128) for _ in range(2)]
        wub = [aab(16 * 128) for _ in range(2)]
        wdbuf = [aab(NIB * 512) for _ in range(2)]
        midb = [aab(NIB * 1024) for _ in range(2)]
        eoutst = rscr2.bitcast(BF16)[:, 0:1024]

        for e01 in range(2):
            mm_e0 = MMC[0]
            for si in range(8):
                def f(v, si=si, e01=e01, mm_e0=mm_e0):
                    if e01 == 1 and si == 0:
                        v.wait_ge(sMM, mm_e0)
                    return v.memset(xg_tm[:, (si % 2) * HD:(si % 2 + 1) * HD], 0.0)
                round_("vector", f)

                def fg(gp, e01=e01, si=si):
                    gp.wait_ge(rq, RN[0])
                    gp.wait_ge(sCC, 1)
                    if e01 == 0 and si == 0:
                        gp.wait_ge(rB, RBF)
                    breg = gp.to_reg(NCORE * TOWN - 1)
                    gp.indirect_dma_start(
                        xg_tm[:, (si % 2) * HD:(si % 2 + 1) * HD], None, h_full[:],
                        IndirectOffsetOnAxis(hidx32[:, si * 2 + e01:si * 2 + e01 + 1], 0),
                        bounds_check=breg, oob_is_err=False,
                    ).then_inc(qG, 16)
                    QG[0] += 16
                    gp.engine_nop().then_inc(rq, 1)
                    RN[0] += 1
                block.gpsimd(fg)
                QGXG = QG[0]

                def f(t, si=si, e01=e01, QGXG=QGXG):
                    t.wait_ge(qG, QGXG)
                    if e01 == 0 and si == 0:
                        t.wait_ge(rB, RBF)
                    for hb in range(NHB):
                        g = hb // 4
                        last = t.transpose(p4[g].bitcast(BF16)[:, (hb % 4) * 256:(hb % 4) * 256 + 128],
                                           xg_tm[:, (si % 2) * HD + hb * 128:(si % 2) * HD + (hb + 1) * 128],
                                           cb[:, 0:128])
                    return last
                round_("tensor", f)

                def f(v, si=si):
                    for g in range(4):
                        dst = xg_fm.rearrange("p (hb t) -> p hb t", t=1024)[:, 4 * g:4 * g + 4,
                                                                           si * 128:(si + 1) * 128]
                        src = p4[g].bitcast(BF16).rearrange("p (k t) -> p k t", t=256)[:, :, 0:128]
                        last = v.tensor_copy(dst, src)
                    return last
                round_("vector", f)

            if e01 == 0:
                tap("xg_fm_e0", xg_fm[:], 128, BF16)

            def p7_dma(ib, par, e01=e01):
                sem = sFF if par == 0 else sFT
                mmw = MMC[0]
                def fd(gp, ib=ib, par=par, sem=sem, mmw=mmw):
                    gp.wait_ge(rq, RN[0])
                    if mmw:
                        gp.wait_ge(sMM, mmw)
                    gp.dma_start(out=wgb[par].rearrange("p (c n) -> p c n", n=128),
                                 in_=wg_my[e01].rearrange("(c p) n -> p c n", p=128)[:, :, ib * 128:(ib + 1) * 128]
                                 ).then_inc(sem, 16)
                    gp.dma_start(out=wub[par].rearrange("p (c n) -> p c n", n=128),
                                 in_=wu_my[e01].rearrange("(c p) n -> p c n", p=128)[:, :, ib * 128:(ib + 1) * 128]
                                 ).then_inc(sem, 16)
                    gp.engine_nop().then_inc(rq, 1)
                    RN[0] += 1
                block.gpsimd(fd)
                (FF_CNT if par == 0 else FT_CNT)[0] += 32

            def p7_mmh(ib, half, e01=e01):
                par = ib % 2
                sem = sFF if par == 0 else sFT
                wv = (FF_CNT if par == 0 else FT_CNT)[0]
                r = MMC[0]
                rqx = RN[0]
                mdw = MDC[0]
                def f(t, ib=ib, half=half, par=par, sem=sem, wv=wv, r=r, rqx=rqx,
                      mdw=mdw, e01=e01):
                    if ib == 0 and half == 0:
                        t.wait_ge(rq, rqx)
                        if e01 == 0:
                            t.wait_ge(sMD, mdw)
                    t.wait_ge(sem, wv)
                    if r >= 2:
                        t.wait_ge(sMD, r - 1)
                    for hb in range(NHB):
                        t.matmul(pw[0][:, (r % 2) * 512:(r % 2 + 1) * 512],
                                 wgb[par][:, hb * 128:(hb + 1) * 128],
                                 xg_fm[:, hb * 1024 + half * 512:hb * 1024 + (half + 1) * 512],
                                 start=(hb == 0), stop=(hb == NHB - 1))
                        mmu = t.matmul(pw[1][:, (r % 2) * 512:(r % 2 + 1) * 512],
                                       wub[par][:, hb * 128:(hb + 1) * 128],
                                       xg_fm[:, hb * 1024 + half * 512:hb * 1024 + (half + 1) * 512],
                                       start=(hb == 0), stop=(hb == NHB - 1))
                    mmu.then_inc(sMM, 1)
                block.tensor(f)
                MMC[0] += 1
                return r

            p7_dma(0, 0)
            p7_dma(1, 1)
            for ib in range(NIB):
                ra = p7_mmh(ib, 0)
                rb_ = p7_mmh(ib, 1)
                if ib + 2 < NIB:
                    p7_dma(ib + 2, ib % 2)
                gu_epih(ra, midb[e01][:, ib * 1024:ib * 1024 + 512],
                        extra_rq=(RN[0] if (e01 == 0 and ib == 0) else None))
                gu_epih(rb_, midb[e01][:, ib * 1024 + 512:ib * 1024 + 1024])

        def p7d_dma(s, first=False):
            hb2, e01, par = s >> 1, s & 1, s % 2
            sem = sFF if par == 0 else sFT
            mmw = 0 if first else MMC[0]
            def fd(gp, hb2=hb2, e01=e01, par=par, sem=sem, mmw=mmw):
                gp.wait_ge(rq, RN[0])
                if mmw:
                    gp.wait_ge(sMM, mmw)
                for ib in range(NIB):
                    gp.dma_start(out=wdbuf[par][:, ib * 512:(ib + 1) * 512],
                                 in_=wd_my[e01][ib * 128:(ib + 1) * 128, hb2 * 512:(hb2 + 1) * 512]
                                 ).then_inc(sem, 16)
                gp.engine_nop().then_inc(rq, 1)
                RN[0] += 1
            block.gpsimd(fd)
            (FF_CNT if par == 0 else FT_CNT)[0] += NIB * 16

        p7d_dma(0, first=True)
        p7d_dma(1, first=True)
        MID_RDY = MDC[0]
        for s in range(8):
            hb2, e01, par = s >> 1, s & 1, s % 2
            sem = sFF if par == 0 else sFT
            wv = (FF_CNT if par == 0 else FT_CNT)[0]
            for sb in range(8):
                down_rounds(
                    (lambda ib, e01=e01, sb=sb: midb[e01][:, ib * 1024 + sb * 128:ib * 1024 + (sb + 1) * 128]),
                    (lambda ib, par=par: wdbuf[par][:, ib * 512:(ib + 1) * 512]),
                    NIB, sem, wv, MID_RDY,
                    eout_myh[hb2][e01 * CAP + sb * 128:e01 * CAP + (sb + 1) * 128, :],
                    wcol=wcols[:, sb * 2 + e01:sb * 2 + e01 + 1])
            if s + 2 < 8:
                p7d_dma(s + 2)
            if e01 == 1:
                def fg(gp, hb2=hb2):
                    gp.wait_ge(rq, RN[0])
                    gp.collective_compute("AllGather", ALU.bypass,
                                          replica_groups=[list(range(NCORE))],
                                          ins=[eout_myh[hb2].opt()],
                                          outs=[eout_allh[hb2].opt()]).then_inc(sCC, 1)
                block.gpsimd(fg)

        def fg_drain(gp):
            gp.collective_compute("AllGather", ALU.bypass,
                                  replica_groups=[list(range(NCORE))],
                                  ins=[cnt_my.opt()],
                                  outs=[cnt_all.opt()]).then_inc(sCC, 1)
        block.gpsimd(fg_drain)

        scope("P8_final")
        _aoff[0] = FFN_MARK
        y4f = aa(8 * HD)
        y4s = aa(4 * HD)
        y4sb = y4s.bitcast(BF16)
        stats = aa(8 * 16)
        htile = y4s[:, 0:2 * HD]
        ysht = y4sb[:, 4 * HD:6 * HD]

        def f(v):
            for q in range(4):
                last = v.memset(y4f[:, q * HD:(q + 1) * HD], 0.0)
            return last
        round_("vector", f)

        def f(v):
            for q in range(4, 8):
                last = v.memset(y4f[:, q * HD:(q + 1) * HD], 0.0)
            return last
        round_("vector", f)

        for t0 in (0, 2, 4, 6):
            def fd(sync, t0=t0):
                sync.wait_ge(rq, RN[0])
                for j in range(2):
                    tt = t0 + j
                    sync.dma_start(out=htile[:, j * HD:(j + 1) * HD],
                                   in_=htm_f32[tt * 128:(tt + 1) * 128, :]).then_inc(rq, 16)
                    sync.dma_start(out=ysht[:, j * HD:(j + 1) * HD],
                                   in_=ysh_dram[tt * 128:(tt + 1) * 128, :]).then_inc(rq, 16)
                RN[0] += 64
            block.sync(fd)

            if t0 == 0:
                tap("ysht0", ysht[:, 0:HD], 128, BF16)
                tap("htile0", htile[:, 0:HD])

            def f(v, t0=t0):
                v.tensor_tensor(y4f[:, t0 * HD:(t0 + 1) * HD],
                                y4f[:, t0 * HD:(t0 + 1) * HD], htile[:, 0:HD], ALU.add)
                return v.tensor_tensor(y4f[:, (t0 + 1) * HD:(t0 + 2) * HD],
                                       y4f[:, (t0 + 1) * HD:(t0 + 2) * HD],
                                       htile[:, HD:2 * HD], ALU.add)
            round_("vector", f)

            def f(v, t0=t0):
                v.tensor_tensor(y4f[:, t0 * HD:(t0 + 1) * HD],
                                y4f[:, t0 * HD:(t0 + 1) * HD], ysht[:, 0:HD], ALU.add)
                return v.tensor_tensor(y4f[:, (t0 + 1) * HD:(t0 + 2) * HD],
                                       y4f[:, (t0 + 1) * HD:(t0 + 2) * HD],
                                       ysht[:, HD:2 * HD], ALU.add)
            round_("vector", f)

        round_("vector", lambda v: v.memset(y4s[:, 0:2 * HD], 0.0))
        round_("vector", lambda v: v.memset(y4s[:, 2 * HD:4 * HD], 0.0))

        for h in range(4):
            def fg(gp, h=h):
                gp.wait_ge(rq, RN[0])
                gp.wait_ge(sCC, 6)
                breg = gp.to_reg(EXP * CAP - 1)
                for tt in range(8):
                    for k in range(4):
                        s = tt * 4 + k
                        gp.indirect_dma_start(
                            y4sb[:, s * 512:(s + 1) * 512],
                            None, eout_allh[h][:],
                            IndirectOffsetOnAxis(gidx32[:, tt * 4 + k:tt * 4 + k + 1], 0),
                            bounds_check=breg, oob_is_err=False,
                        ).then_inc(qG, 16)
                QG[0] += 32 * 16
                gp.engine_nop().then_inc(rq, 1)
                RN[0] += 1
            block.gpsimd(fg)

            wv = QG[0]
            def fa(v, h=h, wv=wv):
                v.wait_ge(qG, wv)
                for k in range(4):
                    for tt in range(8):
                        s = tt * 4 + k
                        v.tensor_tensor(
                            y4f[:, tt * HD + h * 512:tt * HD + (h + 1) * 512],
                            y4f[:, tt * HD + h * 512:tt * HD + (h + 1) * 512],
                            y4sb[:, s * 512:(s + 1) * 512],
                            ALU.add)
                for q in range(4):
                    last = v.memset(y4s[:, q * HD:(q + 1) * HD], 0.0)
                return last
            round_("vector", fa)

        if debug:
            tap("y4a0", y4f[:, 0:HD])

        def f(v):
            for tt in range(8):
                last = v.tensor_reduce(stats[:, tt * 16 + 0:tt * 16 + 1],
                                       y4f[:, tt * HD:(tt + 1) * HD], AX.X, ALU.add)
            return last
        round_("vector", f)

        def f2(sc):
            for tt in range(4):
                last = sc.activation(y4s[:, tt * HD:(tt + 1) * HD],
                                     y4f[:, tt * HD:(tt + 1) * HD], AF.Square,
                                     accum_out=stats[:, tt * 16 + 1:tt * 16 + 2])
            return last
        round_("scalar", f2)

        def f2(sc):
            for tt in range(4, 8):
                last = sc.activation(y4s[:, (tt - 4) * HD:(tt - 3) * HD],
                                     y4f[:, tt * HD:(tt + 1) * HD], AF.Square,
                                     accum_out=stats[:, tt * 16 + 1:tt * 16 + 2])
            return last
        round_("scalar", f2)

        def f(v):
            for tt in range(8):
                v.tensor_scalar_mul(stats[:, tt * 16 + 2:tt * 16 + 3],
                                    stats[:, tt * 16 + 0:tt * 16 + 1], 1.0 / HD)
                last = v.tensor_scalar_mul(stats[:, tt * 16 + 3:tt * 16 + 4],
                                           stats[:, tt * 16 + 1:tt * 16 + 2], 1.0 / HD)
            return last
        round_("vector", f)

        def f(v):
            for tt in range(8):
                last = v.tensor_tensor(stats[:, tt * 16 + 4:tt * 16 + 5],
                                       stats[:, tt * 16 + 2:tt * 16 + 3],
                                       stats[:, tt * 16 + 2:tt * 16 + 3], ALU.mult)
            return last
        round_("vector", f)

        def f(v):
            for tt in range(8):
                last = v.tensor_tensor(stats[:, tt * 16 + 5:tt * 16 + 6],
                                       stats[:, tt * 16 + 3:tt * 16 + 4],
                                       stats[:, tt * 16 + 4:tt * 16 + 5], ALU.subtract)
            return last
        round_("vector", f)

        def f(sc):
            for tt in range(8):
                last = sc.activation(stats[:, tt * 16 + 6:tt * 16 + 7],
                                     stats[:, tt * 16 + 5:tt * 16 + 6], AF.Ln, bias=LN_EPS)
            return last
        round_("scalar", f)

        def f(sc):
            for tt in range(8):
                last = sc.activation(stats[:, tt * 16 + 7:tt * 16 + 8],
                                     stats[:, tt * 16 + 6:tt * 16 + 7], AF.Exp, scale=-0.5)
            return last
        round_("scalar", f)

        def f(v):
            for tt in range(8):
                last = v.tensor_scalar(y4f[:, tt * HD:(tt + 1) * HD],
                                       y4f[:, tt * HD:(tt + 1) * HD],
                                       stats[:, tt * 16 + 2:tt * 16 + 3],
                                       stats[:, tt * 16 + 7:tt * 16 + 8],
                                       ALU.subtract, ALU.mult)
            return last
        round_("vector", f)

        for _ in range(2):
            round_("vector", lambda v: v.engine_nop())

        def fd(sync):
            sync.wait_ge(rq, RN[0])
            for tt in range(8):
                sync.dma_start(out=out_own[tt * 128:(tt + 1) * 128, :],
                               in_=y4f[:, tt * HD:(tt + 1) * HD]).then_inc(rq, 16)
            RN[0] += 128
        block.sync(fd)

        tap("statsall", stats[:])

        def fin(sync):
            sync.wait_ge(rq, RN[0])
        block.sync(fin)
        scope(None)

    return nc


def _consts():
    c = {}
    c["ident_f"] = np.eye(128, dtype=np.float32)
    c["ident_b"] = np.eye(128, dtype=np.float32)
    c["m_triuinc"] = np.triu(np.ones((128, 128), np.float32))
    c["m_striu"] = np.triu(np.ones((128, 128), np.float32), 1)
    c["m_striun"] = -np.triu(np.ones((128, 128), np.float32), 1)
    c["m_striln"] = -np.tril(np.ones((128, 128), np.float32), -1)
    c["onescol"] = np.ones((128, 1), np.float32)
    e = np.zeros((128, 1), np.float32); e[127, 0] = 1.0
    c["e127c"] = e
    c["iota128"] = np.arange(128, dtype=np.float32).reshape(128, 1)
    c["iota16g"] = np.tile(np.arange(16, dtype=np.float32), (128, 1))
    c["iota64c"] = np.arange(64, dtype=np.float32).reshape(64, 1)
    c["fconst_d"] = np.tile(np.array([[1e-6, 1e-5, 1e-8]], np.float32), (128, 1))
    return c


def host_inmaps(inputs):
    import ml_dtypes
    x = np.asarray(inputs["L0_attn_x"])[0]
    Wk = np.asarray(inputs["L0_attn_Wk"]); Wv = np.asarray(inputs["L0_attn_Wv"])
    Wq = np.asarray(inputs["L0_attn_Wq"]); Wa = np.asarray(inputs["L0_attn_Wa_w"])
    wab = np.asarray(inputs["L0_attn_Wa_b"]).reshape(1, D)
    lam = np.asarray(inputs["L0_attn_lam"]).reshape(1, D)
    normw = np.asarray(inputs["L0_attn_norm_w"]).reshape(1, D)
    Wo = np.asarray(inputs["L0_attn_Wo"])
    gw = np.asarray(inputs["L0_ffn_gate_w"])
    w_gate = np.asarray(inputs["L0_ffn_w_gate"]); w_up = np.asarray(inputs["L0_ffn_w_up"])
    w_down = np.asarray(inputs["L0_ffn_w_down"])
    swg = np.ascontiguousarray(np.asarray(inputs["L0_ffn_sw_gate"]))
    swu = np.ascontiguousarray(np.asarray(inputs["L0_ffn_sw_up"]))
    swd = np.ascontiguousarray(np.asarray(inputs["L0_ffn_sw_down"]))

    wcatT = np.ascontiguousarray(np.concatenate([Wk, Wv, Wq, Wa], 0).T)
    woT = np.ascontiguousarray(Wo.T)
    gwT = np.ascontiguousarray(gw.T)
    rowp = np.ascontiguousarray(np.concatenate(
        [lam, wab, normw, np.zeros((1, D), np.float32)], 1))
    consts = _consts()

    in_maps = []
    for c in range(NCORE):
        pm = np.zeros((8, 1), np.float32); pm[:c] = 1.0
        sel = np.zeros((16, 2), np.float32)
        sel[2 * c, 0] = 1.0; sel[2 * c + 1, 1] = 1.0
        cv = np.array([[c * TOWN, c * NCH, c, 0]], np.float32)
        m = dict(
            x_own=np.ascontiguousarray(x[c * TOWN:(c + 1) * TOWN]),
            wcatT=wcatT, woT=woT, gwT=gwT,
            wg_my=np.ascontiguousarray(w_gate[2 * c:2 * c + 2]),
            wu_my=np.ascontiguousarray(w_up[2 * c:2 * c + 2]),
            wd_my=np.ascontiguousarray(w_down[2 * c:2 * c + 2]),
            swg=swg, swu=swu, swd=swd,
            rowp=rowp, premask=pm, esel=sel, cvec=cv,
            **consts,
        )
        m["ident_b"] = np.eye(128).astype(ml_dtypes.bfloat16)
        in_maps.append(m)
    return in_maps


def kernel(**inputs):
    in_maps = host_inmaps(inputs)
    nc = build_nc(debug=False)
    res = run_bass_kernel_spmd(nc, in_maps, core_ids=list(range(NCORE)))
    out = np.concatenate([r["out_own"] for r in res.results], 0)
    return out.reshape(1, NCORE * TOWN, HD).astype(np.float32)
